# version 04
# baseline (speedup 1.0000x reference)
import numpy as np
import ml_dtypes
from contextlib import ExitStack

import concourse.bass as bass
import concourse.mybir as mybir
from concourse.bass_utils import run_bass_kernel_spmd

F32 = mybir.dt.float32
BF16 = mybir.dt.bfloat16
AF = mybir.ActivationFunctionType
ALU = mybir.AluOpType
AX = mybir.AxisListType
BF16NP = ml_dtypes.bfloat16

NCORES = 8
D = 1024
NH = 16
DH = 64
TB = 8192
TC = 2048
HC = 1024
NCH = 8
EXPB = np.float32(-20.0)
SC8 = np.float32(0.125)
NEGC = -1.0e9
NKB = 64
NST = 4

fp = np.float32
DEBUG = False


def chunk_of(core):
    return core, 7 - core


def strip_list(core):
    c0, c1 = chunk_of(core)
    return [(0, c0 * HC), (0, c0 * HC + 512), (1, c1 * HC), (1, c1 * HC + 512)]


def owner_of(b, pos):
    if b == 0:
        return pos // HC, pos % HC
    return 7 - pos // HC, HC + pos % HC


def shard_rows(x):
    out = []
    for core in range(NCORES):
        c0, c1 = chunk_of(core)
        out.append(np.concatenate([x[0, c0 * HC:(c0 + 1) * HC], x[1, c1 * HC:(c1 + 1) * HC]], axis=0))
    return out


def unshard_rows(shards):
    x = np.empty((2, TB) + shards[0].shape[1:], dtype=shards[0].dtype)
    for core in range(NCORES):
        c0, c1 = chunk_of(core)
        x[0, c0 * HC:(c0 + 1) * HC] = shards[core][:HC]
        x[1, c1 * HC:(c1 + 1) * HC] = shards[core][HC:]
    return x


def fmT(a):
    return np.ascontiguousarray(np.asarray(a).T)


ENGS = ("sync", "scalar", "vector", "gpsimd", "tensor")
SEMS = ("dsy", "dsc", "dve", "dgp", "t", "v", "s", "g", "cc")
DMA_SEM = {"sync": "dsy", "scalar": "dsc", "vector": "dve", "gpsimd": "dgp"}


class Prog:
    def __init__(self, name):
        self.nc = bass.Bass("TRN2", target_bir_lowering=False, name=name)
        self.tasks = {e: [] for e in ENGS}
        self.cnt = {sname: 0 for sname in SEMS}
        self.wm = {e: {} for e in ENGS}

    def n(self, s):
        return self.cnt[s]

    def at(self, *sems):
        return [(s, self.cnt[s]) for s in sems]

    def _resolve(self, eng, waits):
        res = []
        for w in waits:
            name, val = w if isinstance(w, tuple) else (w, self.cnt[w])
            if val <= 0:
                continue
            if self.wm[eng].get(name, -1) < val:
                res.append((name, val))
                self.wm[eng][name] = val
        return res

    AUTOSYNC = {"vector": "v", "scalar": "s", "gpsimd": "g"}

    def op(self, eng, builder, waits=(), inc=None, inc_n=1, dma_op=False):
        auto = self.AUTOSYNC.get(eng)
        if auto is not None:
            waits = list(waits) + [(auto, self.cnt[auto])]
            if inc is None and not dma_op:
                inc = auto
        res = self._resolve(eng, waits)
        tgt = None
        if inc is not None:
            self.cnt[inc] += inc_n
            tgt = (inc, inc_n)

        def task(engine, _res=res, _b=builder, _tgt=tgt):
            for name, val in _res:
                engine.wait_ge(self._sem[name], val)
            inst = _b(engine)
            if _tgt is not None and inst is not None:
                inst.then_inc(self._sem[_tgt[0]], _tgt[1])
        self.tasks[eng].append(task)

    def dma(self, eng, out, in_, waits=(), accum=False, slow=False):
        kw = {"accum_op": ALU.add} if accum else {}
        if slow:
            kw["allow_slow_non_contiguous"] = True
        self.op(eng, lambda E, o=out, i=in_, k=kw: E.dma_start(out=o, in_=i, **k),
                waits=waits, inc=DMA_SEM[eng], inc_n=16, dma_op=True)

    def cc(self, kind, op_, ins, outs, waits=()):
        self.op("gpsimd",
                lambda E, k=kind, o=op_, i=ins, u=outs: E.collective_compute(
                    k, o, replica_groups=[list(range(NCORES))],
                    ins=[a.opt() for a in i], outs=[a.opt() for a in u]),
                waits=waits, inc="cc", inc_n=1, dma_op=True)

    def mm(self, out, lhsT, rhs, start, stop, waits=(), inc=None, tile_position=None):
        self.op("tensor",
                lambda E, o=out, l=lhsT, r=rhs, s=start, e=stop, tp=tile_position:
                E.matmul(o, l, r, start=s, stop=e, tile_position=tp),
                waits=waits, inc=inc)

    def trans(self, out, in_, ident, waits=(), inc=None):
        self.op("tensor", lambda E, o=out, i=in_, d=ident: E.transpose(o, i, d),
                waits=waits, inc=inc)

    def act(self, out, in_, func, bias=0.0, scale=1.0, waits=(), inc=None, accum_out=None):
        bias = fp(bias) if isinstance(bias, float) else bias
        scale = fp(scale) if isinstance(scale, float) else scale
        if accum_out is None:
            self.op("scalar", lambda E, o=out, i=in_, f=func, b=bias, sc=scale:
                    E.activation(o, i, f, bias=b, scale=sc), waits=waits, inc=inc)
        else:
            self.op("scalar", lambda E, o=out, i=in_, f=func, b=bias, sc=scale, a=accum_out:
                    E.activation(o, i, f, bias=b, scale=sc, accum_out=a), waits=waits, inc=inc)

    def tt(self, eng, out, a, b, aop, waits=(), inc=None):
        self.op(eng, lambda E, o=out, x=a, y=b, p=aop: E.tensor_tensor(o, x, y, p),
                waits=waits, inc=inc)

    def ts(self, eng, out, in_, s1, s2, op0, op1=ALU.bypass, waits=(), inc=None):
        s1 = float(s1) if isinstance(s1, (float, np.floating)) else s1
        s2 = float(s2) if isinstance(s2, (float, np.floating)) else s2
        self.op(eng, lambda E, o=out, i=in_, a=s1, b=s2, p0=op0, p1=op1:
                E.tensor_scalar(o, i, a, b, p0, p1), waits=waits, inc=inc)

    def stt(self, eng, out, in0, scalar, in1, op0, op1, waits=(), inc=None):
        scalar = float(scalar) if isinstance(scalar, (np.floating, float)) else scalar
        self.op(eng, lambda E, o=out, i=in0, sc=scalar, j=in1, p0=op0, p1=op1:
                E.scalar_tensor_tensor(o, i, sc, j, p0, p1), waits=waits, inc=inc)

    def copy(self, eng, out, in_, waits=(), inc=None):
        self.op(eng, lambda E, o=out, i=in_: E.tensor_copy(o, i), waits=waits, inc=inc)

    def red(self, out, in_, aop, negate=False, waits=(), inc=None):
        self.op("vector", lambda E, o=out, i=in_, p=aop, ng=negate:
                E.tensor_reduce(o, i, AX.X, p, negate=ng), waits=waits, inc=inc)

    def recip(self, out, in_, waits=(), inc=None):
        self.op("vector", lambda E, o=out, i=in_: E.reciprocal(o, i), waits=waits, inc=inc)

    def memset(self, eng, ap_, val, waits=(), inc=None):
        self.op(eng, lambda E, a=ap_, v=val: E.memset(a, v), waits=waits, inc=inc)

    def nop(self, eng, waits=(), inc=None):
        self.op(eng, lambda E: None, waits=waits, inc=inc, dma_op=True)

    def barrier(self):
        allw = self.at(*SEMS)
        for e in ENGS:
            self.nop(e, waits=allw)

    def finalize(self):
        nc = self.nc
        with ExitStack() as st:
            self._sem = {name: st.enter_context(nc.semaphore(name)) for name in SEMS}
            block = st.enter_context(nc.Block())
            for eng in ENGS:
                if not self.tasks[eng]:
                    continue
                tasks = self.tasks[eng]

                def runner(engine, _tasks=tasks):
                    for t in _tasks:
                        t(engine)
                getattr(block, eng)(runner)
        return nc


class LNBufs:
    def __init__(self, ones_c, ones_r, sq, stat, xf, psum):
        self.ones_c = ones_c
        self.ones_r = ones_r
        self.sq = sq
        self.stat = stat
        self.xf = xf
        self.ps_sum, self.ps_sq, self.ps_a, self.ps_b = psum


def emit_ln_strip(p, B, x_of_chunk, ts, xn_cb, x_ready=()):
    strip_gate = p.at("v", "s")
    for c in range(NCH):
        p.mm(B.ps_sum[:, :], B.ones_c[:, :], x_of_chunk(c), c == 0, c == NCH - 1,
             waits=(list(x_ready) + strip_gate) if c == 0 else (),
             inc="t" if c == NCH - 1 else None)
    for c in range(NCH):
        slot = c % 2
        sq = B.sq[:, slot * 512:(slot + 1) * 512]
        p.act(sq, x_of_chunk(c), AF.Square,
              waits=list(x_ready) + strip_gate + ([("t", p.n("t"))] if c >= 2 else []), inc="s")
        p.mm(B.ps_sq[:, :], B.ones_c[:, :], sq, c == 0, c == NCH - 1,
             waits=[("s", p.n("s"))], inc="t")
    mm_done = p.at("t")
    mean = B.stat[:, 0:512]; ex2 = B.stat[:, 512:1024]; var = B.stat[:, 1024:1536]
    rstd = B.stat[:, 1536:2048]; brow = B.stat[:, 2048:2560]; sqv = B.stat[:, 2560:3072]
    p.ts("vector", mean, B.ps_sum[:, :], 1.0 / D, None, ALU.mult, waits=mm_done)
    p.ts("vector", ex2, B.ps_sq[:, :], 1.0 / D, None, ALU.mult)
    p.tt("vector", var, mean, mean, ALU.mult)
    p.tt("vector", var, ex2, var, ALU.subtract, inc="v")
    p.act(sqv, var, AF.Sqrt, bias=1e-5, waits=[("v", p.n("v"))], inc="s")
    p.recip(rstd, sqv, waits=[("s", p.n("s"))])
    p.tt("vector", brow, mean, rstd, ALU.mult)
    p.ts("vector", brow, brow, -1.0, None, ALU.mult, inc="v")
    stats_v = p.at("v")
    p.mm(B.ps_a[:, :], B.ones_r[:, :], rstd, True, True, waits=stats_v)
    p.mm(B.ps_b[:, :], B.ones_r[:, :], brow, True, True, inc="t")
    bc = p.at("t")
    for c in range(NCH):
        p.tt("vector", B.xf[:, :], x_of_chunk(c), B.ps_a[:, :], ALU.mult, waits=bc)
        p.tt("vector", B.xf[:, :], B.xf[:, :], B.ps_b[:, :], ALU.add, inc="v")
        xn_cb(c, B.xf[:, :], p.at("v"))


def make_dmask():
    m = np.zeros((128, 4 * 512), dtype=np.float32)
    for kb in range(4):
        kk = np.arange(128)[:, None]
        qq = np.arange(512)[None, :]
        m[:, kb * 512:(kb + 1) * 512] = np.where(kb * 128 + kk > qq, NEGC, 0.0)
    return m


def make_bmask():
    m = np.zeros((128, 6 * 512), dtype=np.float32)
    kk = np.arange(128)[:, None]
    qq = np.arange(512)[None, :]
    qb = (qq // 64) * 64
    for cc in range(6):
        kpos = -128 + 128 * cc + kk
        bad = (kpos < qb - 64) | (kpos >= qb + 128)
        m[:, cc * 512:(cc + 1) * 512] = np.where(bad, NEGC, 0.0)
    return m.astype(BF16NP)


def make_mb(core):
    m = np.full((128, NST * NKB), NEGC, dtype=np.float32)
    for s, (b, pos0) in enumerate(strip_list(core)):
        nval = pos0 // 128
        m[:, s * NKB: s * NKB + nval] = float(EXPB)
    return m


def rope_tables(core):
    inv = (10000.0 ** (-(np.arange(32, dtype=np.float64) * 2.0 / 64.0)))
    pos = np.empty(TC, dtype=np.float64)
    for s, (b, pos0) in enumerate(strip_list(core)):
        pos[s * 512:(s + 1) * 512] = np.arange(pos0, pos0 + 512)
    fr = pos[None, :] * inv[:, None]
    fr = np.concatenate([fr, fr], axis=0)
    return (np.sin(fr.astype(np.float32).astype(np.float64)).astype(np.float32),
            np.cos(fr.astype(np.float32).astype(np.float64)).astype(np.float32))


def rope_perm():
    perm = []
    for c in range(NCH):
        A, Bh = 2 * c, 2 * c + 1
        perm += list(64 * A + np.arange(0, 64, 2)) + list(64 * Bh + np.arange(0, 64, 2))
        perm += list(64 * A + np.arange(1, 64, 2)) + list(64 * Bh + np.arange(1, 64, 2))
    return np.asarray(perm)


TWIST_MAP = [(4, 1.0), (5, 1.0), (6, -1.0), (7, -1.0), (0, -1.0), (1, -1.0), (2, 1.0), (3, 1.0)]


def build_fused(gi):
    p = Prog("mkF1")
    nc = p.nc

    def ein(name, shape, dt=F32):
        return nc.dram_tensor(name, shape, dt, kind="ExternalInput").ap()

    x_in = ein("x", [D, TC])
    id128 = ein("id128", [128, 128])
    sel8 = ein("sel8", [8, 8 * 128])
    mb_in = ein("mb", [128, NST * NKB])
    dmaskf_in = ein("dmaskf", [128, 4 * 512])
    dmaskb_in = ein("dmaskb", [128, 4 * 512], BF16)
    bmask_in = ein("bmask", [128, 6 * 512], BF16)
    wdkvT_in = ein("wdkvT", [D, 128])
    wg_in = ein("wg", [D, 8]); bg_in = ein("bg", [1, 8])
    wdq_in = ein("wdq", [128, D]); wdqT_in = ein("wdqT", [D, 128])
    wuqT_in = ein("wuqT", [128, D]); wuqr_in = ein("wuqr", [64, NH * 128])
    wuk_in = ein("wuk", [D, 128]); wuv_in = ein("wuv", [D, 128]); woT_in = ein("woT", [D, D])
    w1e_in = ein("w1e", [8 * D, 4096])
    w2e_in = ein("w2e", [8 * 4096, D])
    b1c_in = ein("b1c", [128, 8 * 32])
    b2c_in = ein("b2c", [128, 8 * 8])
    wq1_in = ein("wq1Tp", [D, D]); wk1_in = ein("wk1Tp", [D, D]); wv1_in = ein("wv1T", [D, D])
    sin_in = ein("sint", [64, TC]); cos_in = ein("cost", [64, TC])
    wo1T_in = ein("wo1T", [D, D])
    cw1_in = ein("cw1", [D, D]); cw2_in = ein("cw2", [D, 512])
    cw3_in = ein("cw3", [512, 256]); cw4_in = ein("cw4", [256, 1])
    cb1_in = ein("cb1", [128, 8]); cb2_in = ein("cb2", [128, 4])
    cb3_in = ein("cb3", [128, 2]); cb4_in = ein("cb4", [1, 1])
    gw1_in = ein("gw1", [D, 512]); gw2_in = ein("gw2", [512, 256]); gw3_in = ein("gw3", [256, 1])
    gb1_in = ein("gb1", [128, 4]); gb2_in = ein("gb2", [128, 2]); gb3_in = ein("gb3", [1, 1])
    ar_in = ein("ar", [1, 1])
    wq2T_in = ein("wq2T", [D, D]); wk2T_in = ein("wk2T", [D, D]); wv2T_in = ein("wv2T", [D, D])
    bq2_in = ein("bq2", [128, 8]); bk2_in = ein("bk2", [128, 8]); bv2_in = ein("bv2", [1, D])
    wo2T_in = ein("wo2T", [D, D]); bo2_in = ein("bo2", [128, 8])
    tw1_in = ein("tw1", [D, 4096]); tw2_in = ein("tw2", [D, 4096])
    th1_in = ein("th1", [4096, 4096]); th2_in = ein("th2", [4096, 4096])
    tw3_in = ein("tw3", [4096, D])
    s1_in = ein("s1", [128, 32]); s2_in = ein("s2", [128, 32])
    sh1_in = ein("sh1", [128, 32]); sh2_in = ein("sh2", [128, 32]); s3_in = ein("s3", [128, 8])
    wq3T_in = ein("wq3T", [D, D]); wk3T_in = ein("wk3T", [D, D]); wv3T_in = ein("wv3T", [D, D])
    bq3_in = ein("bq3", [128, 8]); bk3_in = ein("bk3", [1, D]); bv3_in = ein("bv3", [1, D])
    wo3T_in = ein("wo3T", [D, D]); bo3_in = ein("bo3", [128, 8])
    fw1_in = ein("fw1", [D, 4096]); fb1_in = ein("fb1", [128, 32])
    fw2_in = ein("fw2", [4096, D]); fb2_in = ein("fb2", [128, 8])

    out_out = nc.dram_tensor("out", [D, TC], F32, kind="ExternalOutput").ap()

    def idr(name, shape, dt=F32):
        return nc.dram_tensor(name, shape, dt).ap()

    ckv_l = idr("ckv_l", [128, TC], BF16); ckv_g = idr("ckv_g", [8 * 128, TC], BF16)
    ckvT_l = idr("ckvT_l", [TC, 128], BF16); ckvT_g = idr("ckvT_g", [8 * TC, 128], BF16)
    x1a_d = idr("x1a_d", [D, TC])
    xn1_d = idr("xn1_d", [D, TC], BF16)
    q1_d = idr("q1_d", [64, NH * TC], BF16)
    k1_l = idr("k1_l", [64, NH * TC], BF16); k1_g = idr("k1_g", [8 * 64, NH * TC], BF16)
    v1_l = idr("v1_l", [TC, NH * 65], BF16); v1_g = idr("v1_g", [8 * TC, NH * 65], BF16)
    gmp_l = idr("gmp_l", [128, 16]); gmp_g = idr("gmp_g", [128, 16])
    attn1_scr = idr("attn1_scr", [D, TC])
    x2_d = idr("x2_d", [D, TC])
    xn2_d = idr("xn2_d", [D, TC], BF16)
    q2_l = idr("q2_l", [64, NH * TC], BF16); q2_g = idr("q2_g", [8 * 64, NH * TC], BF16)
    k2_l = idr("k2_l", [64, NH * TC], BF16); k2_g = idr("k2_g", [8 * 64, NH * TC], BF16)
    v2_l = idr("v2_l", [TC, NH * 65], BF16); v2_g = idr("v2_g", [8 * TC, NH * 65], BF16)
    ctx2_l = idr("ctx2_l", [8 * D, TC], BF16)
    ctx2_r = idr("ctx2_r", [D, TC], BF16)
    x3a_d = idr("x3a_d", [D, TC])
    x3_d = idr("x3_d", [D, TC])
    xn3_d = idr("xn3_d", [D, TC], BF16)
    q3_d = idr("q3_d", [64, NH * TC], BF16)
    kvp_l = idr("kvp_l", [65, 2 * NH * 65]); kvp_g = idr("kvp_g", [65, 2 * NH * 65])
    bw2_scr = idr("bw2_scr", [4096, D], BF16)

    dbg = {}
    if DEBUG:
        for nm, shape, dt in [("d_x1a", [D, TC], F32), ("d_xn1", [D, TC], BF16),
                              ("d_x2", [D, TC], F32), ("d_ctx2", [D, TC], BF16),
                              ("d_x3", [D, TC], F32)]:
            dbg[nm] = nc.dram_tensor(nm, shape, dt, kind="ExternalOutput").ap()

    ZZCOLS = 47104
    ZZ = nc.alloc_sbuf_tensor("ZZ", [128, ZZCOLS], F32)
    ZZB = ZZ.bitcast(BF16)
    ones_r = nc.alloc_sbuf_tensor("ones_r", [1, 128], BF16)
    ones_rf = nc.alloc_sbuf_tensor("ones_rf", [1, 128], F32)
    ones_cb = nc.alloc_sbuf_tensor("ones_cb", [128, 1], BF16)
    ones_cf = nc.alloc_sbuf_tensor("ones_cf", [128, 1], F32)
    lnc = nc.alloc_sbuf_tensor("lnc", [128, 1], F32)
    lnr = nc.alloc_sbuf_tensor("lnr", [1, 128], F32)

    class Arena:
        def __init__(self, resv=0):
            self.off = 0
            if resv:
                self.off = resv
        def f32(self, n, rows=128):
            ap = ZZ[0:rows, self.off:self.off + n]
            self.off += n
            assert self.off <= ZZCOLS, f"arena overflow {self.off}"
            return ap
        def bf16(self, n, rows=128):
            n2 = (n + 1) // 2
            ap = ZZB[0:rows, 2 * self.off: 2 * self.off + n]
            self.off += n2
            assert self.off <= ZZCOLS, f"arena overflow {self.off}"
            return ap

    def carve_ln(ar):
        return (ar.f32(1024), ar.f32(3072, rows=1), ar.f32(512))

    def carve_resAB(ar):
        return (ar.bf16(16384),
                ar.bf16(2048),
                ar.bf16(2048),
                ar.f32(2048, rows=8))

    ps512 = [nc.alloc_psum_tensor(f"ps{i}", [128, 512], F32) for i in range(6)]
    ps_r1a = nc.alloc_psum_tensor("ps_r1a", [1, 512], F32)
    ps_r1b = nc.alloc_psum_tensor("ps_r1b", [1, 512], F32)

    ar = Arena()
    xn_sb, ckv_sb, ckvT_sb, mwT_s = carve_resAB(ar)
    LNPS = (ps_r1a, ps_r1b, ps512[4], ps512[5])
    ln_sq, ln_stat, ln_xf = carve_ln(ar)
    B = LNBufs(lnc, lnr, ln_sq, ln_stat, ln_xf, LNPS)
    x_sb = ar.f32(16384)
    xfs = ar.f32(4096)
    wdkvT_sb = ar.f32(1024)
    wdkvT_bf = ar.bf16(1024)
    wg_sb = ar.f32(64)
    gmisc = ar.f32(192)
    mw_sb = ar.f32(128)
    id_s = ar.f32(128)
    sel_s = None
    mb_s = ar.f32(NST * NKB)
    stage = ar.f32(4096)
    bg_sb = gmisc[0:1, 184:192]

    p.memset("vector", B.ones_c[:, :], 1.0)
    p.memset("vector", B.ones_r[:, :], 1.0)
    p.memset("vector", ones_r[:, :], 1.0)
    p.memset("vector", ones_rf[:, :], 1.0)
    p.memset("vector", ones_cb[:, :], 1.0)
    p.memset("vector", ones_cf[:, :], 1.0)
    p.dma("sync", x_sb.rearrange("p (c t) -> p c t", c=NCH), x_in.rearrange("(c p) t -> p c t", p=128))
    p.dma("sync", id_s[:, :], id128)
    p.dma("sync", mb_s[:, :], mb_in)
    p.dma("sync", wdkvT_sb.rearrange("p (c l) -> p c l", c=NCH), wdkvT_in.rearrange("(c p) l -> p c l", p=128))
    p.dma("sync", wg_sb.rearrange("p (c e) -> p c e", c=NCH), wg_in.rearrange("(c p) e -> p c e", p=128))
    p.dma("sync", bg_sb, bg_in)
    lded = p.at("dsy")
    p.copy("vector", wdkvT_bf, wdkvT_sb, waits=lded, inc="v")
    prep = p.at("v")

    gm = gmisc[:, 0:4]; gmn = gmisc[:, 4:8]; gsc = gmisc[:, 8:40]
    ge = gmisc[:, 40:72]; gs_ = gmisc[:, 72:76]; gw_ = gmisc[:, 76:80]
    ps_g = ps512[0]
    ps_ckv = ps512[1]
    ps_t = ps512[2]

    for ts in range(NST):
        x_of_chunk = lambda c, _ts=ts: x_sb[:, c * TC + _ts * 512: c * TC + (_ts + 1) * 512]

        def xn_cb(c, xf, waits, _ts=ts):
            dst_f = xfs[:, c * 512:(c + 1) * 512]
            dst_b = xn_sb[:, c * TC + _ts * 512: c * TC + (_ts + 1) * 512]
            p.copy("vector", dst_f, xf, waits=p.at("t"), inc="v")
            p.copy("vector", dst_b, xf, inc="v")

        emit_ln_strip(p, B, x_of_chunk, ts, xn_cb, x_ready=lded)
        xn_v = p.at("v")

        for tb in range(4):
            pg = ps_g[:, tb * 8:(tb + 1) * 8]
            for c in range(NCH):
                xf = xfs[:, c * 512 + tb * 128: c * 512 + (tb + 1) * 128]
                wgc = wg_sb[:, c * 8:(c + 1) * 8]
                p.mm(pg, xf, wgc, c == 0, False,
                     waits=(xn_v + p.at("v", "s")) if c == 0 else ())
            p.mm(pg, B.ones_r[:1, :], bg_sb, False, True, waits=lded, inc="t")
        gmm = p.at("t")
        for tb in range(4):
            pg = ps_g[:, tb * 8:(tb + 1) * 8]
            m_ = gm[:, tb:tb + 1]; mn_ = gmn[:, tb:tb + 1]
            sg = gsc[:, tb * 8:(tb + 1) * 8]
            ee = ge[:, tb * 8:(tb + 1) * 8]
            ss = gs_[:, tb:tb + 1]; ww = gw_[:, tb:tb + 1]
            mwt = mw_sb[:, tb * 8:(tb + 1) * 8]
            p.red(m_, pg, ALU.max, waits=gmm, inc="v")
            p.ts("vector", mn_, m_, -1.0, None, ALU.mult, inc="v")
            p.act(sg, pg, AF.Sign, bias=m_, scale=-1.0, waits=[("v", p.n("v"))], inc="s")
            p.act(ee, pg, AF.Exp, bias=mn_, scale=1.0, inc="s")
            p.red(ss, ee, ALU.add, negate=True, waits=[("s", p.n("s"))])
            p.recip(ww, ss)
            p.ts("vector", mwt, sg, 1.0, ww, ALU.subtract, ALU.mult, inc="v")
        for tb in range(4):
            p.trans(ps_t[0:8, 0:128], mw_sb[:, tb * 8:(tb + 1) * 8], id_s[:, :],
                    waits=p.at("v"), inc="t")
            p.copy("vector", mwT_s[0:8, (ts * 4 + tb) * 128:(ts * 4 + tb + 1) * 128],
                   ps_t[0:8, 0:128], waits=p.at("t"))
        for c in range(NCH):
            xb = xn_sb[:, c * TC + ts * 512: c * TC + (ts + 1) * 512]
            wc = wdkvT_bf[:, c * 128:(c + 1) * 128]
            p.mm(ps_ckv[:, :], wc, xb, c == 0, c == NCH - 1,
                 waits=(prep + xn_v + p.at("v")) if c == 0 else (),
                 inc="t" if c == NCH - 1 else None)
        p.copy("vector", ckv_sb[:, ts * 512:(ts + 1) * 512], ps_ckv[:, :],
               waits=p.at("t"), inc="v")
        p.dma("gpsimd", ckv_l[:, ts * 512:(ts + 1) * 512],
              ckv_sb[:, ts * 512:(ts + 1) * 512], waits=[("v", p.n("v"))])
        for tb in range(4):
            tcol = ts * 512 + tb * 128
            for c in range(NCH):
                p.mm(ps_t[:, 0:128], xn_sb[:, c * TC + tcol: c * TC + tcol + 128],
                     wdkvT_bf[:, c * 128:(c + 1) * 128], c == 0, c == NCH - 1,
                     waits=(xn_v + p.at("v")) if c == 0 else (),
                     inc="t" if c == NCH - 1 else None)
            p.copy("vector", ckvT_sb[:, (ts * 4 + tb) * 128:(ts * 4 + tb + 1) * 128],
                   ps_t[:, 0:128], waits=p.at("t"), inc="v")
            p.dma("gpsimd", ckvT_l[tcol:tcol + 128, :],
                  ckvT_sb[:, (ts * 4 + tb) * 128:(ts * 4 + tb + 1) * 128],
                  waits=[("v", p.n("v"))])

    p.cc("AllGather", ALU.bypass, [ckv_l], [ckv_g], waits=p.at("dgp"))
    p.cc("AllGather", ALU.bypass, [ckvT_l], [ckvT_g], waits=p.at("dgp"))
    cc_ckv = p.n("cc")

    p.barrier()
    ar = Arena()
    xn_sb, ckv_sb, ckvT_sb, mwT_s = carve_resAB(ar)
    smal = ar.bf16(15360)
    s1t = ar.bf16(8192)
    ql_s = ar.bf16(8192)
    ckv_fm = ar.bf16(8192)
    ckv_tm = ar.bf16(8192)
    x1a_s = ar.f32(4096)
    dmask_s = ar.f32(2048)
    bc_s = ar.f32(512)
    probs = ar.bf16(1024)
    stage = ar.f32(2048)
    invs = ar.f32(512, rows=1)
    mb_s = ar.f32(NST * NKB)
    qa_bf = smal[:, 6144:8192]
    wdqT_bf = smal[:, 0:1024]; wuk_bf = smal[:, 1024:2048]; wuv_bf = smal[:, 2048:3072]
    wuqT_bf = smal[:, 3072:4096]; wdq_bf = smal[:, 4096:5120]; v0_bf = smal[:, 5120:6144]
    m_bf = smal[:, 8192:10240]
    wuqr_bf = smal[0:64, 10240:12288]; p64 = smal[0:64, 12288:14336]; pstage = smal[:, 14336:14464]
    p.dma("sync", mb_s[:, :], mb_in)

    ps_sc = [ps512[0], ps512[1]]
    ps_ctx = ps512[2]
    ps_bc = ps512[3]
    ps_y = ps512[4]
    ps_sum = ps_r1a

    p.dma("sync", dmask_s, dmaskf_in)

    def ld1(dst, src2d, rows=128):
        w = src2d.shape[-1] if src2d.ndim == 2 else src2d.shape[-1] * src2d.shape[-2]
        p.dma("sync", stage[0:rows, 0:w] if src2d.ndim == 2 else
              stage[0:rows, 0:w].rearrange("p (c l) -> p c l", c=src2d.shape[1]),
              src2d, waits=p.at("v"))
        p.copy("vector", dst, stage[0:rows, 0:w], waits=p.at("dsy"))

    ld1(wdqT_bf, wdqT_in.rearrange("(c p) l -> p c l", p=128))
    ld1(wuk_bf, wuk_in.rearrange("(c p) l -> p c l", p=128))
    ld1(wuv_bf, wuv_in.rearrange("(c p) l -> p c l", p=128))
    ld1(wuqT_bf, wuqT_in)
    ld1(wdq_bf, wdq_in)
    ld1(wuqr_bf, wuqr_in, rows=64)
    casts_done = p.at("v")

    for bc in range(8):
        for n in range(2):
            p.mm(ps_sc[n][:, :], wuqT_bf[:, bc * 128:(bc + 1) * 128], wdq_bf[:, n * 512:(n + 1) * 512],
                 True, True, waits=list(casts_done) + [("v", p.n("v"))], inc="t")
            p.copy("vector", s1t[:, bc * 1024 + n * 512: bc * 1024 + (n + 1) * 512], ps_sc[n][:, :],
                   waits=p.at("t"))
    for ac in range(8):
        for bc in range(8):
            p.mm(ps_ctx[:, :128], s1t[:, bc * 1024 + ac * 128: bc * 1024 + (ac + 1) * 128],
                 wuk_bf[:, bc * 128:(bc + 1) * 128], bc == 0, bc == 7,
                 waits=p.at("v") if bc == 0 else (), inc="t" if bc == 7 else None)
        p.copy("vector", pstage[:, 0:128], ps_ctx[:, :128], waits=p.at("t", "dgp"))
        for hh in range(2):
            h = 2 * ac + hh
            p.dma("gpsimd", p64[0:64, h * 128:(h + 1) * 128],
                  pstage[hh * 64:(hh + 1) * 64, 0:128], waits=p.at("v"))
    mready = p.at("dgp", "v")
    for h in range(NH):
        p.mm(ps_bc[:, :128], wuqr_bf[0:64, h * 128:(h + 1) * 128], p64[0:64, h * 128:(h + 1) * 128],
             True, True, waits=list(mready) + [("v", p.n("v"))], inc="t")
        p.copy("vector", m_bf[:, h * 128:(h + 1) * 128], ps_bc[:, :128], waits=p.at("t"))
    for n in range(4):
        p.dma("sync", stage[:, :].rearrange("p (c d) -> p c d", c=NCH),
              woT_in.rearrange("(c p) d -> p c d", p=128)[:, :, n * 256:(n + 1) * 256],
              waits=p.at("v"))
        p.copy("vector", ql_s[:, 0:2048], stage[:, :], waits=p.at("dsy"))
        for cc2 in range(8):
            p.mm(ps_ctx[:, 0:256], wuv_bf[:, cc2 * 128:(cc2 + 1) * 128], ql_s[:, cc2 * 256:(cc2 + 1) * 256],
                 cc2 == 0, cc2 == 7, waits=p.at("v") if cc2 == 0 else (), inc="t" if cc2 == 7 else None)
        p.copy("vector", v0_bf[:, n * 256:(n + 1) * 256], ps_ctx[:, 0:256], waits=p.at("t"))
    for ts in range(NST):
        for c in range(NCH):
            p.mm(ps_ctx[:, :], wdqT_bf[:, c * 128:(c + 1) * 128],
                 xn_sb[:, c * TC + ts * 512: c * TC + (ts + 1) * 512], c == 0, c == NCH - 1,
                 waits=p.at("v", "dsy") if c == 0 else (), inc="t" if c == NCH - 1 else None)
        p.copy("vector", qa_bf[:, ts * 512:(ts + 1) * 512], ps_ctx[:, :], waits=p.at("t"))
    p0_done = p.at("v")
    p.nop("sync", waits=[("cc", cc_ckv)])

    for ts in range(NST):
        b = 0 if ts < 2 else 1
        sl = ts
        for r in range(8):
            kbslot = r * 8 if b == 0 else (7 - r) * 8
            p.dma("sync", ckv_fm[:, kbslot * 128:(kbslot + 8) * 128],
                  ckv_g.rearrange("(r p) t -> p r t", p=128)[:, r, (0 if b == 0 else 1024):(1024 if b == 0 else 2048)],
                  waits=p.at("t", "s") if r == 0 else ())
        for r in range(8):
            kbslot = r * 8 if b == 0 else (7 - r) * 8
            p.dma("sync", ckv_tm[:, kbslot * 128:(kbslot + 8) * 128].rearrange("p (kb l) -> p kb l", l=128),
                  ckvT_g.rearrange("(r t) l -> r t l", t=2048)[r, (0 if b == 0 else 1024):(1024 if b == 0 else 2048), :]
                      .rearrange("(kb p) l -> p kb l", p=128))
        for h in range(NH):
            p.mm(ps_bc[:, :], m_bf[:, h * 128:(h + 1) * 128], qa_bf[:, ts * 512:(ts + 1) * 512],
                 True, True, waits=p.at("v") + (p0_done if (ts == 0 and h == 0) else []), inc="t")
            p.copy("vector", ql_s[:, h * 512:(h + 1) * 512], ps_bc[:, :], waits=p.at("t"))
        p.dma("gpsimd", x1a_s.rearrange("p (c t) -> p c t", t=512),
              x_in[:, ts * 512:(ts + 1) * 512].rearrange("(c p) t -> p c t", p=128),
              waits=p.at("v", "dgp"))
        kv_ld = p.at("dsy")
        ql_v = p.at("v")
        xi_ld = p.at("dgp")
        NKB2 = NKB + 4

        def kb_ops(kb, _ts=ts, _sl=sl):
            if kb < NKB:
                return (ckv_fm[:, kb * 128:(kb + 1) * 128],
                        ckv_tm[:, kb * 128:(kb + 1) * 128],
                        mb_s[:, _ts * NKB + kb: _ts * NKB + kb + 1], None)
            d = kb - NKB
            return (ckv_sb[:, _sl * 512 + d * 128: _sl * 512 + (d + 1) * 128],
                    ckvT_sb[:, (_sl * 4 + d) * 128: (_sl * 4 + d + 1) * 128],
                    float(EXPB), d)

        for h in range(NH):
            qlh = ql_s[:, h * 512:(h + 1) * 512]
            score_t = {}
            for kb in range(2):
                fmop, tmop, bias, dmi = kb_ops(kb)
                p.mm(ps_sc[kb][:, :], fmop, qlh, True, True,
                     waits=list(kv_ld) + list(ql_v) + [("s", p.n("s"))], inc="t")
                score_t[kb] = p.n("t")
            hd_v = p.at("v")
            for kb in range(NKB2):
                slot = kb % 2
                fmop, tmop, bias, dmi = kb_ops(kb)
                if dmi is not None:
                    p.tt("vector", ps_sc[slot][:, :], ps_sc[slot][:, :],
                         dmask_s[:, dmi * 512:(dmi + 1) * 512], ALU.add,
                         waits=[("t", score_t[kb])])
                    ewait = [("v", p.n("v"))]
                else:
                    ewait = [("t", score_t[kb])]
                p.act(probs[:, slot * 512:(slot + 1) * 512], ps_sc[slot][:, :], AF.Exp,
                      bias=bias, scale=float(SC8), waits=ewait)
                ecnt = p.n("s")
                p.mm(ps_ctx[:, :], tmop, probs[:, slot * 512:(slot + 1) * 512],
                     kb == 0, kb == NKB2 - 1,
                     waits=[("s", ecnt)] + (hd_v if kb == 0 else []))
                p.mm(ps_sum[:, :], ones_cb[:, :], probs[:, slot * 512:(slot + 1) * 512],
                     kb == 0, kb == NKB2 - 1,
                     waits=(hd_v if kb == 0 else ()), inc="t")
                if kb + 2 < NKB2:
                    fmop2, _, _, _ = kb_ops(kb + 2)
                    p.mm(ps_sc[slot][:, :], fmop2, qlh, True, True,
                         waits=[("s", ecnt)], inc="t")
                    score_t[kb + 2] = p.n("t")
            p.recip(invs, ps_sum[:, :], waits=p.at("t"))
            p.mm(ps_bc[:, :], ones_rf[:, :], invs, True, True, waits=p.at("v"), inc="t")
            p.copy("vector", bc_s, ps_bc[:, :], waits=p.at("t"))
            p.tt("vector", probs[:, 0:512], ps_ctx[:, :], bc_s, ALU.mult)
            p.mm(ps_y[0:64, :], v0_bf[:, h * 64:(h + 1) * 64], probs[:, 0:512], True, True,
                 waits=p.at("v") + list(xi_ld), inc="t")
            dst = x1a_s[(h % 2) * 64:(h % 2) * 64 + 64, (h // 2) * 512:(h // 2) * 512 + 512]
            p.tt("vector", dst, dst, ps_y[0:64, :], ALU.add, waits=p.at("t"))
        p.dma("gpsimd", x1a_d.rearrange("(c p) t -> p c t", p=128)[:, :, ts * 512:(ts + 1) * 512],
              x1a_s.rearrange("p (c t) -> p c t", t=512), waits=p.at("v"))

    p.barrier()
    ar = Arena()
    xn_sb, ckv_sb, ckvT_sb, mwT_s = carve_resAB(ar)
    w1h_bf = ar.bf16(16384)
    w2h_bf = ar.bf16(16384)
    h_sb = ar.bf16(8192)
    yev = ar.f32(1024)
    mwbc = ar.f32(512)
    b1c_s = ar.f32(256)
    b2c_s = ar.f32(64)
    sel_s = ar.f32(1024, rows=8)
    stage = ar.f32(4096)
    p.dma("sync", sel_s[:, :], sel8)
    p.dma("sync", stage[:, 0:256], b1c_in)
    p.dma("sync", stage[:, 256:320], b2c_in)
    p.copy("vector", b1c_s, stage[:, 0:256], waits=p.at("dsy"))
    p.copy("vector", b2c_s, stage[:, 256:320])

    ps_h = [ps512[0], ps512[1]]
    ps_o = [ps512[2], ps512[3]]
    ps_b2 = ps512[4]

    for e in range(8):
        for half in range(2):
            for dk in range(8):
                p.dma("sync", stage[:, (dk % 2) * 2048:(dk % 2) * 2048 + 2048],
                      w1e_in[(e * 8 + dk) * 128:(e * 8 + dk + 1) * 128, half * 2048:(half + 1) * 2048],
                      waits=[("v", p.n("v"))])
                p.copy("vector", w1h_bf[:, dk * 2048:(dk + 1) * 2048],
                       stage[:, (dk % 2) * 2048:(dk % 2) * 2048 + 2048],
                       waits=p.at("dsy") + [("t", p.n("t"))])
            for hc in range(16):
                p.dma("sync", stage[:, (hc % 2) * 1024:(hc % 2) * 1024 + 1024],
                      w2e_in[(e * 32 + half * 16 + hc) * 128:(e * 32 + half * 16 + hc + 1) * 128, :],
                      waits=[("v", p.n("v"))])
                p.copy("vector", w2h_bf[:, hc * 1024:(hc + 1) * 1024],
                       stage[:, (hc % 2) * 1024:(hc % 2) * 1024 + 1024],
                       waits=p.at("dsy") + [("t", p.n("t"))])
            wcast = p.at("v")
            for ts2 in range(NST):
                p.mm(ps_b2[:, :], sel_s[0:8, e * 128:(e + 1) * 128],
                     mwT_s[0:8, ts2 * 512:(ts2 + 1) * 512],
                     True, True, waits=p.at("v") + [("s", p.n("s"))], inc="t")
                p.copy("vector", mwbc, ps_b2[:, :], waits=p.at("t"))
                mw_v = p.at("v")
                for ht in range(16):
                    psh = ps_h[ht % 2]
                    for dk in range(8):
                        p.mm(psh[:, :], w1h_bf[:, dk * 2048 + ht * 128: dk * 2048 + (ht + 1) * 128],
                             xn_sb[:, dk * TC + ts2 * 512:(dk * TC) + (ts2 + 1) * 512],
                             dk == 0, dk == 7,
                             waits=(list(wcast) + [("s", p.n("s"))]) if dk == 0 else (),
                             inc="t" if dk == 7 else None)
                    p.act(h_sb[:, ht * 512:(ht + 1) * 512], psh[:, :], AF.Silu,
                          bias=b1c_s[:, e * 32 + half * 16 + ht: e * 32 + half * 16 + ht + 1],
                          waits=p.at("t"))
                h_done = p.at("s")
                for dc in range(8):
                    pso = ps_o[dc % 2]
                    for ht in range(16):
                        p.mm(pso[:, :], w2h_bf[:, ht * 1024 + dc * 128: ht * 1024 + dc * 128 + 128],
                             h_sb[:, ht * 512:(ht + 1) * 512], ht == 0, ht == 15,
                             waits=(list(h_done) + [("v", p.n("v"))]) if ht == 0 else (),
                             inc="t" if ht == 15 else None)
                    ysl = yev[:, (dc % 2) * 512:(dc % 2) * 512 + 512]
                    if half == 0:
                        p.tt("vector", ysl, pso[:, :], mwbc, ALU.mult,
                             waits=p.at("t") + mw_v + [("dgp", p.n("dgp"))])
                    else:
                        p.stt("vector", ysl, pso[:, :], b2c_s[:, e * 8 + dc: e * 8 + dc + 1], mwbc,
                              ALU.add, ALU.mult, waits=p.at("t") + mw_v + [("dgp", p.n("dgp"))])
                    p.dma("gpsimd", x1a_d[dc * 128:(dc + 1) * 128, ts2 * 512:(ts2 + 1) * 512],
                          ysl, waits=[("v", p.n("v")), ("dgp", p.n("dgp"))], accum=True)

    p.barrier()
    if DEBUG:
        p.dma("sync", dbg["d_x1a"], x1a_d)
        p.nop("sync", waits=p.at("dsy"))
    ar = Arena()
    ln_sq, ln_stat, ln_xf = carve_ln(ar)
    B = LNBufs(lnc, lnr, ln_sq, ln_stat, ln_xf, (ps_r1a, ps_r1b, ps512[4], ps512[5]))
    x_sb = ar.f32(16384)
    wq_bf = ar.bf16(8192)
    wk_bf = ar.bf16(8192)
    wv_bf = ar.bf16(8192)
    sin_sb = ar.f32(2048, rows=64)
    cos_sb = ar.f32(2048, rows=64)
    scr1 = ar.f32(512, rows=64); scr2 = ar.f32(512, rows=64)
    ob1 = ar.bf16(512, rows=64); ob2 = ar.bf16(512, rows=64)
    vstage = ar.bf16(1040)
    gm_acc = ar.f32(16)
    gm_scr = ar.f32(1)
    xn_strip = ar.bf16(4096)
    stage = ar.f32(4096)

    p.dma("sync", x_sb.rearrange("p (c t) -> p c t", c=NCH), x1a_d.rearrange("(c p) t -> p c t", p=128))
    xld = p.at("dsy")
    p.dma("scalar", sin_sb, sin_in)
    p.dma("scalar", cos_sb, cos_in)
    for w_in, w_bf in ((wq1_in, wq_bf), (wk1_in, wk_bf), (wv1_in, wv_bf)):
        for half in range(2):
            p.dma("scalar", stage[:, :].rearrange("p (c e) -> p c e", c=NCH),
                  w_in.rearrange("(c p) e -> p c e", p=128)[:, :, half * 512:(half + 1) * 512],
                  waits=[("v", p.n("v"))])
            for c in range(NCH):
                p.copy("vector", w_bf[:, c * D + half * 512: c * D + (half + 1) * 512],
                       stage[:, c * 512:(c + 1) * 512], waits=p.at("dsc"))
    wv_done = p.at("v")
    p.memset("vector", gm_acc[:, :], 0.0)
    for h in range(NH):
        p.memset("vector", vstage[:, h * 65 + 64: h * 65 + 65], 1.0)

    ps_q = ps512[0]
    ps_v0 = ps512[1]
    ps_v1 = ps512[2]

    for ts in range(NST):
        x_of_chunk = lambda c, _ts=ts: x_sb[:, c * TC + _ts * 512: c * TC + (_ts + 1) * 512]
        bhalf = 0 if ts < 2 else 1

        def xn_cb(c, xf, waits, _ts=ts, _b=bhalf):
            dst_b = xn_strip[:, c * 512:(c + 1) * 512]
            p.copy("vector", dst_b, xf, waits=[("t", p.n("t")), ("dgp", p.n("dgp"))])
            p.red(gm_scr[:, :], xf, ALU.add)
            acc = gm_acc[:, _b * 8 + c: _b * 8 + c + 1]
            p.tt("vector", acc, acc, gm_scr[:, :], ALU.add)
            p.dma("gpsimd", xn1_d[c * 128:(c + 1) * 128, _ts * 512:(_ts + 1) * 512],
                  dst_b, waits=[("v", p.n("v"))])

        emit_ln_strip(p, B, x_of_chunk, ts, xn_cb, x_ready=xld)
        xn_v = p.at("v")
        sslice = sin_sb[:, ts * 512:(ts + 1) * 512]
        cslice = cos_sb[:, ts * 512:(ts + 1) * 512]
        for w_bf, out_ap in ((wq_bf, q1_d), (wk_bf, k1_l)):
            for c in range(NCH):
                for dk in range(NCH):
                    p.mm(ps_q[:, :], w_bf[:, dk * D + c * 128:(dk * D) + (c + 1) * 128],
                         xn_strip[:, dk * 512:(dk + 1) * 512],
                         dk == 0, dk == NCH - 1,
                         waits=(list(wv_done) + list(xn_v) + [("v", p.n("v")), ("dgp", p.n("dgp"))]) if dk == 0 else (),
                         inc="t" if dk == NCH - 1 else None)
                mmw = p.at("t")
                x1b = ps_q[0:64, :]
                x2b = ps_q[64:128, :]
                p.tt("vector", scr1[:, :], x1b, cslice, ALU.mult, waits=list(mmw) + p.at("dsc"))
                p.tt("vector", scr2[:, :], x2b, sslice, ALU.mult)
                p.tt("vector", ob1[:, :], scr1[:, :], scr2[:, :], ALU.subtract)
                p.tt("vector", scr1[:, :], x1b, sslice, ALU.mult)
                p.tt("vector", scr2[:, :], x2b, cslice, ALU.mult)
                p.tt("vector", ob2[:, :], scr1[:, :], scr2[:, :], ALU.add)
                obv = p.at("v")
                hA, hB = 2 * c, 2 * c + 1
                p.dma("gpsimd", out_ap[0:32, hA * TC + ts * 512: hA * TC + (ts + 1) * 512],
                      ob1[0:32, :], waits=obv)
                p.dma("gpsimd", out_ap[32:64, hA * TC + ts * 512: hA * TC + (ts + 1) * 512],
                      ob2[0:32, :])
                p.dma("gpsimd", out_ap[0:32, hB * TC + ts * 512: hB * TC + (ts + 1) * 512],
                      ob1[32:64, :])
                p.dma("gpsimd", out_ap[32:64, hB * TC + ts * 512: hB * TC + (ts + 1) * 512],
                      ob2[32:64, :])
        for tb in range(4):
            tcol = ts * 512 + tb * 128
            for half, psv in ((0, ps_v0), (1, ps_v1)):
                for dk in range(NCH):
                    p.mm(psv[:, :], xn_strip[:, dk * 512 + tb * 128: dk * 512 + tb * 128 + 128],
                         wv_bf[:, dk * D + half * 512:(dk * D) + half * 512 + 512],
                         dk == 0, dk == NCH - 1,
                         waits=(list(xn_v) + [("v", p.n("v")), ("dgp", p.n("dgp"))]) if dk == 0 else (),
                         inc="t" if dk == NCH - 1 else None)
            vmw = p.at("t")
            for h in range(NH):
                psv = ps_v0 if h < 8 else ps_v1
                p.copy("vector", vstage[:, h * 65: h * 65 + 64],
                       psv[:, (h % 8) * 64:(h % 8) * 64 + 64], waits=vmw + [("dgp", p.n("dgp"))])
            p.dma("gpsimd", v1_l[tcol:tcol + 128, :], vstage[:, :], waits=[("v", p.n("v"))])
    p.dma("gpsimd", gmp_l, gm_acc[:, :], waits=p.at("v"))
    p.cc("AllGather", ALU.bypass, [k1_l], [k1_g], waits=p.at("dgp"))
    p.cc("AllGather", ALU.bypass, [v1_l], [v1_g], waits=p.at("dgp"))
    p.cc("AllReduce", ALU.add, [gmp_l], [gmp_g], waits=p.at("dgp"))
    cc_k1 = p.n("cc")

    p.barrier()
    if DEBUG:
        p.dma("sync", dbg["d_xn1"], xn1_d)
        p.nop("sync", waits=p.at("dsy"))
    ar = Arena()
    kbuf = ar.bf16(8192)
    wo1_bf = ar.bf16(8192)
    q1s = ar.bf16(8192)
    vbuf = ar.bf16(8320)
    acc = ar.f32(8192)
    a1s = ar.f32(4096)
    ctxbf = ar.bf16(4096)
    probs = ar.bf16(2048)
    dmask_sD = ar.bf16(2048)
    mb_s = ar.f32(NST * NKB)
    stage = ar.f32(2048)
    bc_s = ar.f32(512)
    invs = ar.f32(512, rows=1)
    p.dma("sync", mb_s[:, :], mb_in)

    ps_sc = [ps512[0], ps512[1]]
    ps_pv = ps512[2]
    ps_bc = ps512[3]
    ps_w = ps512[4]
    ps_w2 = ps512[5]
    ps_r1 = ps_r1a

    p.dma("sync", dmask_sD, dmaskb_in)
    src3 = wo1T_in.rearrange("(c p) e -> p c e", p=128)
    for c0 in range(0, NCH, 2):
        p.dma("scalar", stage[:, 0:2 * D].rearrange("p (c e) -> p c e", e=D),
              src3[:, c0:c0 + 2, :], waits=[("v", p.n("v"))])
        p.copy("vector", wo1_bf[:, c0 * D:(c0 + 2) * D], stage[:, 0:2 * D], waits=p.at("dsc"))
    wo1_v = p.at("v")
    p.nop("sync", waits=[("cc", cc_k1)])

    for ts in range(NST):
        b = 0 if ts < 2 else 1
        sl = ts
        p.dma("scalar", q1s[0:64, :].rearrange("q (h t) -> q h t", t=512),
              q1_d.rearrange("q (h t) -> q h t", t=TC)[:, :, ts * 512:(ts + 1) * 512],
              waits=[("t", p.n("t"))] + (list(wo1_v) if ts == 0 else []))
        p.memset("vector", acc[:, :], 0.0, waits=p.at("t"))
        qld = p.at("dsc")
        NGRP = 9
        for grp in range(NGRP):
            if grp < 8:
                r = grp if b == 0 else 7 - grp
                colr = (0, 1024) if b == 0 else (1024, 2048)
                p.dma("sync", vbuf[:, :].rearrange("p (kb c) -> p kb c", c=1040),
                      v1_g[r * TC + colr[0]: r * TC + colr[1], :].rearrange("(kb p) c -> p kb c", p=128),
                      waits=[("t", p.n("t")), ("s", p.n("s"))])
                nkb_g = 8
                bias_of = lambda kb, _ts=ts, _g=grp: mb_s[:, _ts * NKB + _g * 8 + kb: _ts * NKB + _g * 8 + kb + 1]
                dm_of = lambda kb: None
                kwid = 1024
            else:
                p.dma("sync", vbuf[:, 0:4 * 1040].rearrange("p (kb c) -> p kb c", c=1040),
                      v1_l[sl * 512:(sl + 1) * 512, :].rearrange("(kb p) c -> p kb c", p=128),
                      waits=[("t", p.n("t")), ("s", p.n("s"))])
                nkb_g = 4
                bias_of = lambda kb: float(EXPB)
                dm_of = lambda kb: kb
                kwid = 512
            for octt in range(2):
                if grp < 8:
                    p.dma("sync", kbuf[0:64, :].rearrange("q (h t) -> q h t", t=1024),
                          k1_g.rearrange("(r q) ht -> r q ht", q=64)[r]
                              .rearrange("q (h t) -> q h t", t=TC)
                              [:, octt * 8:(octt + 1) * 8, colr[0]:colr[1]],
                          waits=[("t", p.n("t")), ("s", p.n("s"))])
                else:
                    p.dma("sync", kbuf[0:64, 0:4096].rearrange("q (h t) -> q h t", t=512),
                          k1_l.rearrange("q (h t) -> q h t", t=TC)
                              [:, octt * 8:(octt + 1) * 8, sl * 512:(sl + 1) * 512],
                          waits=[("t", p.n("t")), ("s", p.n("s"))])
                gld = p.at("dsy")
                for h in range(octt * 8, octt * 8 + 8):
                    qh = q1s[0:64, h * 512:(h + 1) * 512]
                    score_t = {}
                    hd_v = p.at("v")
                    for kb in range(2):
                        p.mm(ps_sc[kb][:, :], kbuf[0:64, (h % 8) * kwid + kb * 128:
                                                   (h % 8) * kwid + (kb + 1) * 128],
                             qh, True, True, waits=list(gld) + list(qld) + [("s", p.n("s"))], inc="t")
                        score_t[kb] = p.n("t")
                    for kb in range(nkb_g):
                        sl2 = kb % 2
                        dmi = dm_of(kb)
                        if dmi is not None:
                            p.tt("vector", ps_sc[sl2][:, :], ps_sc[sl2][:, :],
                                 dmask_sD[:, dmi * 512:(dmi + 1) * 512], ALU.add,
                                 waits=[("t", score_t[kb])])
                            ewait = [("v", p.n("v"))]
                        else:
                            ewait = [("t", score_t[kb])]
                        p.act(probs[:, sl2 * 512:(sl2 + 1) * 512], ps_sc[sl2][:, :], AF.Exp,
                              bias=bias_of(kb), scale=float(SC8), waits=ewait)
                        ecnt = p.n("s")
                        p.mm(ps_pv[0:65, :], vbuf[:, kb * 1040 + h * 65: kb * 1040 + h * 65 + 65],
                             probs[:, sl2 * 512:(sl2 + 1) * 512], kb == 0, kb == nkb_g - 1,
                             waits=[("s", ecnt)] + (hd_v if kb == 0 else []),
                             inc="t" if kb == nkb_g - 1 else None)
                        if kb + 2 < nkb_g:
                            p.mm(ps_sc[sl2][:, :], kbuf[0:64, (h % 8) * kwid + (kb + 2) * 128:
                                                        (h % 8) * kwid + (kb + 3) * 128],
                                 qh, True, True, waits=[("s", ecnt)], inc="t")
                            score_t[kb + 2] = p.n("t")
                    asl = acc[0:65, h * 512:(h + 1) * 512]
                    p.tt("vector", asl, asl, ps_pv[0:65, :], ALU.add, waits=p.at("t"))
        for h in range(NH):
            p.recip(invs, acc[64:65, h * 512:(h + 1) * 512])
            p.mm(ps_bc[:, :], ones_rf[:, :], invs, True, True, waits=p.at("v"), inc="t")
            p.copy("vector", bc_s, ps_bc[:, :], waits=p.at("t"))
            p.tt("vector", ctxbf[(h % 2) * 64:(h % 2) * 64 + 64, (h // 2) * 512:(h // 2) * 512 + 512],
                 acc[0:64, h * 512:(h + 1) * 512], bc_s[0:64, :], ALU.mult)
        ctx_v = p.at("v")
        for oc in range(NCH):
            for dk in range(NCH):
                p.mm(ps_w[:, :], wo1_bf[:, dk * D + oc * 128: dk * D + (oc + 1) * 128],
                     ctxbf[:, dk * 512:(dk + 1) * 512], dk == 0, dk == NCH - 1,
                     waits=(list(wo1_v) + list(ctx_v) + [("v", p.n("v")), ("dgp", p.n("dgp"))]) if dk == 0 else (),
                     inc="t" if dk == NCH - 1 else None)
            p.copy("vector", a1s[:, oc * 512:(oc + 1) * 512], ps_w[:, :], waits=p.at("t"))
        p.dma("gpsimd", attn1_scr.rearrange("(c p) t -> p c t", p=128)[:, :, ts * 512:(ts + 1) * 512],
              a1s[:, :].rearrange("p (c t) -> p c t", t=512), waits=p.at("v"))

    pass1_done = p.at("t", "v", "s", "dgp")

    ar = Arena()
    ln_sq, ln_stat, ln_xf = carve_ln(ar)
    B2 = LNBufs(lnc, lnr, ln_sq, ln_stat, ln_xf, (ps_r1a, ps_r1b, ps_bc, ps_pv))
    kbuf = ar.bf16(13320)
    wo1_bf = ar.bf16(8192)
    vbuf = ar.bf16(8320)
    q1s = ar.bf16(8192)
    acc = ar.f32(8192)
    ctxbf = ar.bf16(4096)
    a1s = ar.f32(4096)
    vstage = ar.bf16(1040)
    wtile = ar.bf16(1024)
    stage = ar.f32(2048)
    fsml = ar.f32(640)
    frow = ar.f32(3104, rows=1)
    bv2_bf = ar.bf16(1024, rows=1)
    bc_s = fsml[:, 0:512]
    gm = fsml[:, 512:528]
    biases = fsml[:, 528:592]
    gfv = fsml[:1, 592:594]; Ab = fsml[:1, 594:596]
    scoef = fsml[:1, 596:597]; t1r = fsml[:1, 598:600]
    invs = frow[:1, 0:512]
    bv2row = frow[:1, 512:1536]
    tf_ = frow[:1, 1536:2048]; crow = frow[:1, 2048:2560]; crn = frow[:1, 2560:3072]
    cb1_s = biases[:, 0:8]; cb2_s = biases[:, 8:12]; cb3_s = biases[:, 12:14]; cb4_s = biases[:1, 14:15]
    gb1_s = biases[:, 16:20]; gb2_s = biases[:, 20:22]; gb3_s = biases[:1, 22:23]
    bq2_s = biases[:, 24:32]; bk2_s = biases[:, 32:40]; ar_s = biases[:1, 40:41]

    gmb = ctxbf[:, 0:16]
    h1g = ctxbf[:, 16:20]; h2g = ctxbf[:, 20:22]
    p.dma("sync", stage[:, 0:16], gmp_g, waits=pass1_done + [("cc", p.n("cc"))])
    p.copy("vector", gm, stage[:, 0:16], waits=p.at("dsy"))
    p.ts("vector", gmb, gm, 1.0 / TB, None, ALU.mult)

    cw1_bf = kbuf[:, 0:8192]
    cw2_bf = kbuf[:, 8192:12288]
    cw3_bf = kbuf[:, 12288:13312]
    cw4_bf = kbuf[:, 13312:13320]
    gw1_bf = kbuf[:, 0:4096]
    gw2_bf = kbuf[:, 4096:5120]
    gw3_bf = kbuf[:, 5120:5128]
    wk2_bf = wo1_bf[:, 0:8192]
    wv2_bf = vbuf[:, 0:8192]

    def ldw(dst, src2d, extra=()):
        src3_ = src2d.rearrange("(c p) e -> p c e", p=128)
        C, E = src3_.shape[1], src3_.shape[2]
        cpg = max(1, 2048 // E)
        for c0 in range(0, C, cpg):
            n = min(cpg, C - c0)
            p.dma("scalar", stage[:, 0:n * E].rearrange("p (c e) -> p c e", e=E),
                  src3_[:, c0:c0 + n, :], waits=[("v", p.n("v"))], slow=(E == 1))
            p.copy("vector", dst[:, c0 * E:(c0 + n) * E], stage[:, 0:n * E],
                   waits=p.at("dsc") + list(extra))

    ldw(gw1_bf, gw1_in)
    ldw(gw2_bf, gw2_in)
    ldw(gw3_bf, gw3_in)
    p.dma("scalar", stage[:, 0:8], cb1_in, waits=[("v", p.n("v"))])
    p.dma("scalar", stage[:, 8:12], cb2_in)
    p.dma("scalar", stage[:, 12:14], cb3_in)
    p.dma("scalar", stage[:1, 14:15], cb4_in[:1, :])
    p.dma("scalar", stage[:, 16:20], gb1_in)
    p.dma("scalar", stage[:, 20:22], gb2_in)
    p.dma("scalar", stage[:1, 22:23], gb3_in[:1, :])
    p.dma("scalar", stage[:, 24:32], bq2_in)
    p.dma("scalar", stage[:, 32:40], bk2_in)
    p.dma("scalar", stage[:1, 40:41], ar_in[:1, :])
    p.dma("scalar", stage[:1, 48:1072], bv2_in)
    p.copy("vector", biases[:, 0:48], stage[:, 0:48], waits=p.at("dsc"))
    p.copy("vector", bv2row, stage[:1, 48:1072])
    p.copy("vector", bv2_bf, bv2row)

    for b2 in range(2):
        for oc in range(4):
            for dk in range(8):
                p.mm(ps_w[:, 0:1], gw1_bf[:, dk * 512 + oc * 128: dk * 512 + (oc + 1) * 128],
                     gmb[:, b2 * 8 + dk: b2 * 8 + dk + 1], dk == 0, dk == 7,
                     waits=[("v", p.n("v")), ("s", p.n("s"))] if dk == 0 else (),
                     inc="t" if dk == 7 else None)
            p.act(h1g[:, oc:oc + 1], ps_w[:, 0:1], AF.Gelu, bias=gb1_s[:, oc:oc + 1], waits=p.at("t"))
        for oc in range(2):
            for dk in range(4):
                p.mm(ps_w[:, 0:1], gw2_bf[:, dk * 256 + oc * 128: dk * 256 + (oc + 1) * 128],
                     h1g[:, dk:dk + 1], dk == 0, dk == 3,
                     waits=[("s", p.n("s")), ("v", p.n("v"))] if dk == 0 else (), inc="t" if dk == 3 else None)
            p.act(h2g[:, oc:oc + 1], ps_w[:, 0:1], AF.Gelu, bias=gb2_s[:, oc:oc + 1], waits=p.at("t"))
        for dk in range(2):
            p.mm(ps_r1[:1, 0:1], gw3_bf[:, dk:dk + 1], h2g[:, dk:dk + 1],
                 dk == 0, dk == 1, waits=[("s", p.n("s")), ("v", p.n("v"))] if dk == 0 else (),
                 inc="t" if dk == 1 else None)
        p.act(gfv[:, b2:b2 + 1], ps_r1[:1, 0:1], AF.Sigmoid, bias=gb3_s, waits=p.at("t"))
    gfb = p.at("t", "s")
    ldw(cw1_bf, cw1_in, extra=gfb)
    ldw(cw2_bf, cw2_in)
    ldw(cw3_bf, cw3_in)
    ldw(cw4_bf, cw4_in)
    ldw(wk2_bf, wk2T_in)
    ldw(wv2_bf, wv2T_in)
    wld = p.at("v")
    p.ts("vector", t1r, gfv[:1, 0:2], 0.7, -0.5, ALU.mult, ALU.add, waits=p.at("s"))
    p.tt("vector", t1r, t1r, ar_s.broadcast_to([1, 2]), ALU.mult)
    p.ts("vector", Ab, t1r, 2.0, 0.1, ALU.mult, ALU.add)
    p.ts("vector", scoef, ar_s, 0.6, None, ALU.mult)
    gf_v = p.at("v")

    x2s = a1s
    outbf = q1s[:, 0:4096]
    h1b = q1s[:, 4096:8192]
    xn2s = q1s[:, 4096:8192]
    h2b = ctxbf[:, 0:2048]
    h3b = ctxbf[:, 2048:3072]
    cbp = stage[:, 0:512]
    cbm = stage[:, 512:1024]
    B2 = LNBufs(lnc, lnr, ln_sq, ln_stat, ln_xf,
                (ps_r1a, ps_r1b, ps_bc, ps_pv))
    for h in range(NH):
        p.memset("vector", vstage[:, h * 65 + 64: h * 65 + 65], 1.0)

    for ts in range(NST):
        bidx = 0 if ts < 2 else 1
        p.dma("sync", x2s.rearrange("p (c t) -> p c t", t=512),
              x1a_d[:, ts * 512:(ts + 1) * 512].rearrange("(c p) t -> p c t", p=128),
              waits=[("v", p.n("v")), ("dgp", p.n("dgp")), ("t", p.n("t")), ("s", p.n("s"))])
        p.dma("gpsimd", x2s.rearrange("p (c t) -> p c t", t=512),
              attn1_scr.rearrange("(c p) t -> p c t", p=128)[:, :, ts * 512:(ts + 1) * 512],
              waits=p.at("dsy"), accum=True)
        p.dma("scalar", outbf[:, :].rearrange("p (c t) -> p c t", t=512),
              xn1_d[:, ts * 512:(ts + 1) * 512].rearrange("(c p) t -> p c t", p=128),
              waits=[("t", p.n("t")), ("s", p.n("s")), ("v", p.n("v"))])
        xld2 = p.at("dsy", "dgp", "dsc")
        p.copy("vector", acc[:, 0:4096], outbf[:, :], waits=xld2)
        pp = 0
        for cyc in range(3):
            src_o = acc[:, pp * 4096:(pp + 1) * 4096]
            dst_o = acc[:, (1 - pp) * 4096:(2 - pp) * 4096]
            p.copy("vector", outbf[:, :], src_o)
            ob_v = p.at("v")
            for oc in range(NCH):
                for dk in range(NCH):
                    p.mm(ps_w[:, :], cw1_bf[:, dk * 1024 + oc * 128: dk * 1024 + (oc + 1) * 128],
                         outbf[:, dk * 512:(dk + 1) * 512], dk == 0, dk == 7,
                         waits=(list(ob_v) + list(wld) + [("s", p.n("s"))]) if dk == 0 else (),
                         inc="t" if dk == 7 else None)
                p.act(h1b[:, oc * 512:(oc + 1) * 512], ps_w[:, :], AF.Gelu,
                      bias=cb1_s[:, oc:oc + 1], waits=p.at("t"))
            for oc in range(4):
                for dk in range(NCH):
                    p.mm(ps_w2[:, :], cw2_bf[:, dk * 512 + oc * 128: dk * 512 + (oc + 1) * 128],
                         h1b[:, dk * 512:(dk + 1) * 512], dk == 0, dk == 7,
                         waits=[("s", p.n("s"))] if dk == 0 else (), inc="t" if dk == 7 else None)
                p.act(h2b[:, oc * 512:(oc + 1) * 512], ps_w2[:, :], AF.Gelu,
                      bias=cb2_s[:, oc:oc + 1], waits=p.at("t"))
            for oc in range(2):
                for dk in range(4):
                    p.mm(ps_w[:, :], cw3_bf[:, dk * 256 + oc * 128: dk * 256 + (oc + 1) * 128],
                         h2b[:, dk * 512:(dk + 1) * 512], dk == 0, dk == 3,
                         waits=[("s", p.n("s")), ("v", p.n("v"))] if dk == 0 else (), inc="t" if dk == 3 else None)
                p.act(h3b[:, oc * 512:(oc + 1) * 512], ps_w[:, :], AF.Gelu,
                      bias=cb3_s[:, oc:oc + 1], waits=p.at("t"))
            for dk in range(2):
                p.mm(ps_r1[:1, :], cw4_bf[:, dk:dk + 1], h3b[:, dk * 512:(dk + 1) * 512],
                     dk == 0, dk == 1, waits=[("s", p.n("s")), ("v", p.n("v"))] if dk == 0 else (),
                     inc="t" if dk == 1 else None)
            p.act(tf_, ps_r1[:1, :], AF.Sigmoid, bias=cb4_s, waits=p.at("t"))
            p.ts("vector", crow, tf_, scoef[:1, 0:1], Ab[:1, bidx:bidx + 1], ALU.mult, ALU.add,
                 waits=gf_v + [("s", p.n("s"))])
            p.ts("vector", crn, crow, -1.0, None, ALU.mult)
            p.mm(ps_bc[:, :], ones_rf[:, :], crow, True, True, waits=p.at("v"), inc="t")
            p.copy("vector", cbp, ps_bc[:, :], waits=p.at("t"))
            p.mm(ps_bc[:, :], ones_rf[:, :], crn, True, True, waits=p.at("v"), inc="t")
            p.copy("vector", cbm, ps_bc[:, :], waits=p.at("t"))
            for c in range(NCH):
                srcc, sgn = TWIST_MAP[c]
                cb = cbp if sgn > 0 else cbm
                p.tt("vector", B.xf[:, :], src_o[:, srcc * 512:(srcc + 1) * 512], cb, ALU.mult)
                p.tt("vector", dst_o[:, c * 512:(c + 1) * 512], src_o[:, c * 512:(c + 1) * 512],
                     B.xf[:, :], ALU.add)
            pp = 1 - pp
        ffn_o = acc[:, pp * 4096:(pp + 1) * 4096]
        for c in range(NCH):
            sl3 = x2s[:, c * 512:(c + 1) * 512]
            p.tt("vector", sl3, sl3, ffn_o[:, c * 512:(c + 1) * 512], ALU.add)
        x2_v = p.at("v")
        p.dma("gpsimd", x2_d.rearrange("(c p) t -> p c t", p=128)[:, :, ts * 512:(ts + 1) * 512],
              x2s.rearrange("p (c t) -> p c t", t=512), waits=x2_v)
        x_of_chunk = lambda c, _ts=ts: x2s[:, c * 512:(c + 1) * 512]

        def xn_cb2(c, xf, waits, _ts=ts):
            dst_b = xn2s[:, c * 512:(c + 1) * 512]
            p.copy("vector", dst_b, xf)
            p.dma("gpsimd", xn2_d[c * 128:(c + 1) * 128, _ts * 512:(_ts + 1) * 512],
                  dst_b, waits=[("v", p.n("v"))])

        emit_ln_strip(p, B2, x_of_chunk, ts, xn_cb2, x_ready=x2_v)
        xn_v = p.at("v")
        for w_bf, b_s, out_ap in ((None, bq2_s, q2_l), (wk2_bf, bk2_s, k2_l)):
            for c in range(NCH):
                if w_bf is None:
                    p.dma("scalar", stage[:, 1024:2048].rearrange("p (k e) -> p k e", e=128),
                          wq2T_in.rearrange("(k p) e -> p k e", p=128)[:, :, c * 128:(c + 1) * 128],
                          waits=[("v", p.n("v"))])
                    p.copy("vector", wtile[:, :], stage[:, 1024:2048], waits=p.at("dsc"))
                    lhs_of = lambda dk: wtile[:, dk * 128:(dk + 1) * 128]
                else:
                    lhs_of = lambda dk, _w=w_bf, _c=c: _w[:, dk * 1024 + _c * 128: dk * 1024 + (_c + 1) * 128]
                for dk in range(NCH):
                    p.mm(ps_w[:, :], lhs_of(dk),
                         xn2s[:, dk * 512:(dk + 1) * 512], dk == 0, dk == 7,
                         waits=(list(xn_v) + [("v", p.n("v")), ("dgp", p.n("dgp"))]) if dk == 0 else (),
                         inc="t" if dk == 7 else None)
                p.ts("vector", outbf[:, c * 512:(c + 1) * 512], ps_w[:, :], b_s[:, c:c + 1], None,
                     ALU.add, waits=p.at("t"))
                obv = p.at("v")
                hA, hB = 2 * c, 2 * c + 1
                p.dma("gpsimd", out_ap[0:64, hA * TC + ts * 512: hA * TC + (ts + 1) * 512],
                      outbf[0:64, c * 512:(c + 1) * 512], waits=obv)
                p.dma("gpsimd", out_ap[0:64, hB * TC + ts * 512: hB * TC + (ts + 1) * 512],
                      outbf[64:128, c * 512:(c + 1) * 512])
        for tb in range(4):
            tcol = ts * 512 + tb * 128
            for half, psv in ((0, ps_w), (1, ps_w2)):
                for dk in range(NCH):
                    p.mm(psv[:, :], xn2s[:, dk * 512 + tb * 128: dk * 512 + tb * 128 + 128],
                         wv2_bf[:, dk * 1024 + half * 512: dk * 1024 + half * 512 + 512],
                         dk == 0, False,
                         waits=(list(xn_v) + [("v", p.n("v")), ("dgp", p.n("dgp"))]) if dk == 0 else ())
                p.mm(psv[:, :], ones_r[:1, :], bv2_bf[:1, half * 512:(half + 1) * 512],
                     False, True, inc="t")
            vmw = p.at("t")
            for h in range(NH):
                psv = ps_w if h < 8 else ps_w2
                p.copy("vector", vstage[:, h * 65: h * 65 + 64],
                       psv[:, (h % 8) * 64:(h % 8) * 64 + 64], waits=vmw + [("dgp", p.n("dgp"))])
            p.dma("gpsimd", v2_l[tcol:tcol + 128, :], vstage[:, :], waits=[("v", p.n("v"))])
    p.cc("AllGather", ALU.bypass, [q2_l], [q2_g], waits=p.at("dgp"))
    p.cc("AllGather", ALU.bypass, [k2_l], [k2_g], waits=p.at("dgp"))
    p.cc("AllGather", ALU.bypass, [v2_l], [v2_g], waits=p.at("dgp"))
    cc_k2 = p.n("cc")

    p.barrier()
    if DEBUG:
        p.dma("sync", dbg["d_x2"], x2_d)
        p.nop("sync", waits=p.at("dsy"))
    ar = Arena()
    q2s = ar.bf16(8192)
    kbs = ar.bf16(12288)
    vbs = ar.bf16(6240)
    gks = ar.bf16(4096)
    gvs = ar.bf16(2080)
    gqs = ar.bf16(4096)
    rowbf = ar.bf16(4096)
    ctxbf2 = ar.bf16(4096)
    bmask_s = ar.bf16(3072)
    probs2 = ar.bf16(1024)
    fsml2 = ar.f32(2048)
    bc1 = fsml2[:, 0:512]; bc2 = fsml2[:, 512:1024]
    t1f = fsml2[:, 1024:1536]
    invs2 = fsml2[:1, 1536:2048]

    ps_sc = [ps512[0], ps512[1]]
    ps_l = ps512[2]
    ps_c = ps512[3]
    ps_bcE = ps512[4]
    ps_rE = ps512[5]

    p.dma("sync", bmask_s, bmask_in, waits=[("cc", cc_k2)])
    for b2 in range(2):
        for ui, g in enumerate(gi):
            pos = g * 64
            r, lc = owner_of(b2, pos)
            p.dma("sync", gks[0:64, b2 * NH * 128:(b2 + 1) * NH * 128]
                      .rearrange("q (h u t) -> q h u t", u=2, t=64)[:, :, ui, :],
                  k2_g.rearrange("(r q) ht -> r q ht", q=64)[r]
                      .rearrange("q (h t) -> q h t", t=TC)[:, :, lc:lc + 64])
            p.dma("sync", gvs[ui * 64:(ui + 1) * 64, b2 * NH * 65:(b2 + 1) * NH * 65],
                  v2_g[r * TC + lc: r * TC + lc + 64, :])
            p.dma("sync", gqs[0:64, b2 * NH * 128:(b2 + 1) * NH * 128]
                      .rearrange("q (h u t) -> q h u t", u=2, t=64)[:, :, ui, :],
                  q2_g.rearrange("(r q) ht -> r q ht", q=64)[r]
                      .rearrange("q (h t) -> q h t", t=TC)[:, :, lc:lc + 64])
    gld0 = p.at("dsy")

    for b2 in range(2):
        for h in range(NH):
            score_t = {}
            first = True
            for grp in range(8):
                r = grp if b2 == 0 else 7 - grp
                colr = (0, 1024) if b2 == 0 else (1024, 2048)
                slot = grp % 2
                p.dma("sync", kbs[0:64, slot * 1024:(slot + 1) * 1024],
                      k2_g.rearrange("(r q) ht -> r q ht", q=64)[r]
                          .rearrange("q (h t) -> q h t", t=TC)[:, h, colr[0]:colr[1]],
                      waits=[("t", p.n("t")), ("s", p.n("s"))] + (list(gld0) if first else []))
                p.dma("sync", vbs[:, slot * 8 * 65:(slot + 1) * 8 * 65].rearrange("p (kb c) -> p kb c", c=65),
                      v2_g[r * TC + colr[0]: r * TC + colr[1], h * 65:(h + 1) * 65]
                          .rearrange("(kb p) c -> p kb c", p=128))
                first = False
                gld = p.at("dsy")
                qh = gqs[0:64, b2 * NH * 128 + h * 128:(b2 * NH + h + 1) * 128]
                for kb in range(8):
                    sl2 = kb % 2
                    p.mm(ps_sc[sl2][:, 0:128], kbs[0:64, slot * 1024 + kb * 128: slot * 1024 + (kb + 1) * 128],
                         qh, True, True, waits=list(gld) + [("s", p.n("s"))], inc="t")
                    p.act(probs2[:, sl2 * 512:sl2 * 512 + 128], ps_sc[sl2][:, 0:128], AF.Exp,
                          bias=float(EXPB), scale=float(SC8), waits=[("t", p.n("t"))])
                    p.mm(ps_rE[0:65, 0:128], vbs[:, slot * 8 * 65 + kb * 65: slot * 8 * 65 + kb * 65 + 65],
                         probs2[:, sl2 * 512:sl2 * 512 + 128], grp == 0 and kb == 0,
                         grp == 7 and kb == 7,
                         waits=[("s", p.n("s"))] + ([("v", p.n("v"))] if (grp == 0 and kb == 0) else []),
                         inc="t" if (grp == 7 and kb == 7) else None)
            p.recip(invs2[:1, 0:128], ps_rE[64:65, 0:128], waits=p.at("t"))
            p.mm(ps_bcE[:, 0:128], ones_rf[:, :], invs2[:1, 0:128], True, True,
                 waits=p.at("v"), inc="t")
            p.copy("vector", bc1[:, 0:128], ps_bcE[:, 0:128], waits=p.at("t"))
            hb = (h % 2) * 64
            p.tt("vector", rowbf[hb:hb + 64, b2 * 2048 + (h // 2) * 128: b2 * 2048 + (h // 2) * 128 + 128],
                 ps_rE[0:64, 0:128], bc1[hb:hb + 64, 0:128], ALU.mult)
    rows_v = p.at("v")

    for b2 in range(2):
        for qs in range(16):
            pos0 = qs * 512
            rq, lq = owner_of(b2, pos0)
            slq = (lq % HC) // 512 + (0 if b2 == 0 else 2)
            own = rq
            sl_own = lq // 512 if b2 == 0 else 2 + (lq - HC) // 512
            p.dma("scalar", q2s[0:64, :].rearrange("q (h t) -> q h t", t=512),
                  q2_g.rearrange("(r q) ht -> r q ht", q=64)[rq]
                      .rearrange("q (h t) -> q h t", t=TC)[:, :, lq:lq + 512],
                  waits=[("t", p.n("t"))])
            for cc2 in range(6):
                ks = (pos0 - 128 + cc2 * 128) % TB
                rk, lk = owner_of(b2, ks)
                p.dma("sync", kbs[0:64, :].rearrange("q (h c) -> q h c", c=768)[:, :, cc2 * 128:(cc2 + 1) * 128],
                      k2_g.rearrange("(r q) ht -> r q ht", q=64)[rk]
                          .rearrange("q (h t) -> q h t", t=TC)[:, :, lk:lk + 128],
                      waits=[("t", p.n("t")), ("s", p.n("s"))] + (list(rows_v) if (qs == 0 and b2 == 0 and cc2 == 0) else []))
                p.dma("sync", vbs[:, cc2 * NH * 65:(cc2 + 1) * NH * 65],
                      v2_g[rk * TC + lk: rk * TC + lk + 128, :])
            sld = p.at("dsy", "dsc")
            for h in range(NH):
                qh = q2s[0:64, h * 512:(h + 1) * 512]
                score_t = {}
                hd_v = p.at("v")
                for cc2 in range(2):
                    p.mm(ps_sc[cc2][:, :], kbs[0:64, h * 768 + cc2 * 128: h * 768 + (cc2 + 1) * 128],
                         qh, True, True, waits=list(sld) + [("s", p.n("s")), ("v", p.n("v"))], inc="t")
                    score_t[cc2] = p.n("t")
                for cc2 in range(6):
                    sl2 = cc2 % 2
                    p.tt("vector", ps_sc[sl2][:, :], ps_sc[sl2][:, :],
                         bmask_s[:, cc2 * 512:(cc2 + 1) * 512], ALU.add, waits=[("t", score_t[cc2])])
                    p.act(probs2[:, sl2 * 512:(sl2 + 1) * 512], ps_sc[sl2][:, :], AF.Exp,
                          bias=float(EXPB), scale=float(SC8), waits=[("v", p.n("v"))])
                    ecnt = p.n("s")
                    p.mm(ps_l[0:65, :], vbs[:, cc2 * NH * 65 + h * 65: cc2 * NH * 65 + h * 65 + 65],
                         probs2[:, sl2 * 512:(sl2 + 1) * 512], cc2 == 0, cc2 == 5,
                         waits=[("s", ecnt)] + (hd_v if cc2 == 0 else []),
                         inc="t" if cc2 == 5 else None)
                    if cc2 + 2 < 6:
                        p.mm(ps_sc[sl2][:, :], kbs[0:64, h * 768 + (cc2 + 2) * 128: h * 768 + (cc2 + 3) * 128],
                             qh, True, True, waits=[("s", ecnt)], inc="t")
                        score_t[cc2 + 2] = p.n("t")
                p.mm(ps_c[:, :], gks[0:64, b2 * NH * 128 + h * 128: b2 * NH * 128 + (h + 1) * 128],
                     qh, True, True, waits=[("s", p.n("s")), ("v", p.n("v"))], inc="t")
                p.act(probs2[:, 0:512], ps_c[:, :], AF.Exp, bias=float(EXPB), scale=float(SC8),
                      waits=p.at("t"))
                p.mm(ps_c[0:65, :], gvs[:, b2 * NH * 65 + h * 65: b2 * NH * 65 + h * 65 + 65],
                     probs2[:, 0:512], True, True, waits=[("s", p.n("s")), ("v", p.n("v"))], inc="t")
                pvq = p.at("t")
                p.recip(invs2, ps_l[64:65, :], waits=pvq)
                p.mm(ps_bcE[:, :], ones_rf[:, :], invs2, True, True, waits=p.at("v"), inc="t")
                p.copy("vector", bc1, ps_bcE[:, :], waits=p.at("t"))
                p.recip(invs2, ps_c[64:65, :])
                p.mm(ps_bcE[:, :], ones_rf[:, :], invs2, True, True, waits=p.at("v"), inc="t")
                p.copy("vector", bc2, ps_bcE[:, :], waits=p.at("t"))
                hb = (h % 2) * 64
                p.tt("vector", t1f[hb:hb + 64, :], ps_l[0:64, :], bc1[hb:hb + 64, :], ALU.mult)
                dst = ctxbf2[hb:hb + 64, (h // 2) * 512:(h // 2) * 512 + 512]
                p.tt("vector", dst, ps_c[0:64, :], bc2[hb:hb + 64, :], ALU.mult,
                     waits=[("dgp", p.n("dgp"))])
                p.tt("vector", dst, dst, t1f[hb:hb + 64, :], ALU.add)
            for ui, g in enumerate(gi):
                gpos = g * 64
                if pos0 <= gpos < pos0 + 512:
                    loc = gpos - pos0
                    for c in range(NCH):
                        p.copy("vector",
                               ctxbf2[:, c * 512 + loc: c * 512 + loc + 64],
                               rowbf[:, b2 * 2048 + c * 128 + ui * 64: b2 * 2048 + c * 128 + (ui + 1) * 64],
                               waits=rows_v)
            p.dma("gpsimd",
                  ctx2_l.rearrange("(o c p) t -> p o c t", o=8, p=128)[:, own, :, sl_own * 512:(sl_own + 1) * 512],
                  ctxbf2.rearrange("p (c t) -> p c t", t=512), waits=p.at("v"))

    p.cc("ReduceScatter", ALU.add, [ctx2_l], [ctx2_r], waits=p.at("dgp"))
    cc_ctx = p.n("cc")

    p.barrier()
    ar = Arena()
    AR1 = ar.bf16(32768)
    AR2 = ar.bf16(32768)
    ARF = ar.f32(8192)
    wt = ar.bf16(2048)
    misc = ar.f32(656)
    stage = ar.f32(2048)

    s1_s = misc[:, 0:32]; s2_s = misc[:, 32:64]; sh1_s = misc[:, 64:96]; sh2_s = misc[:, 96:128]
    s3_s = misc[:, 128:136]; bo2_s = misc[:, 136:144]
    scrF = misc[:, 144:656]

    ps1, ps2, ps3, ps4 = ps512[0], ps512[1], ps512[2], ps512[3]

    p.dma("sync", stage[:, 0:32], s1_in)
    p.dma("sync", stage[:, 32:64], s2_in)
    p.dma("sync", stage[:, 64:96], sh1_in)
    p.dma("sync", stage[:, 96:128], sh2_in)
    p.dma("sync", stage[:, 128:136], s3_in)
    p.dma("sync", stage[:, 136:144], bo2_in)
    p.copy("vector", misc[:, 0:144], stage[:, 0:144], waits=p.at("dsy"))
    sldF = p.at("v")

    wo2_bf = AR1[:, 0:8192]
    src3 = wo2T_in.rearrange("(c p) e -> p c e", p=128)
    for c0 in range(0, NCH, 2):
        p.dma("scalar", stage[:, 0:2 * D].rearrange("p (c e) -> p c e", e=D),
              src3[:, c0:c0 + 2, :], waits=[("v", p.n("v"))])
        p.copy("vector", wo2_bf[:, c0 * D:(c0 + 2) * D], stage[:, 0:2 * D], waits=p.at("dsc"))
    wo2_v = p.at("v")
    p.nop("sync", waits=[("cc", cc_ctx)])

    ctxs = AR2[:, 0:4096]
    ctxf = AR2[:, 8192:12288]
    x2sF = ARF[:, 0:4096]
    for ts in range(NST):
        p.dma("sync", ctxs.rearrange("p (c t) -> p c t", t=512),
              ctx2_r[:, ts * 512:(ts + 1) * 512].rearrange("(c p) t -> p c t", p=128),
              waits=[("t", p.n("t")), ("v", p.n("v"))])
        p.dma("gpsimd", x2sF.rearrange("p (c t) -> p c t", t=512),
              x2_d[:, ts * 512:(ts + 1) * 512].rearrange("(c p) t -> p c t", p=128),
              waits=[("v", p.n("v")), ("dgp", p.n("dgp"))])
        cld = p.at("dsy", "dgp")
        for c in range(NCH):
            p.ts("vector", ctxf[:, c * 512:(c + 1) * 512], ctxs[:, c * 512:(c + 1) * 512],
                 0.125, None, ALU.mult, waits=cld)
        ctx_v = p.at("v")
        if DEBUG:
            p.dma("gpsimd", dbg["d_ctx2"].rearrange("(c p) t -> p c t", p=128)[:, :, ts * 512:(ts + 1) * 512],
                  ctxf.rearrange("p (c t) -> p c t", t=512), waits=ctx_v)
        for oc in range(NCH):
            for dk in range(NCH):
                p.mm(ps2[:, :], wo2_bf[:, dk * D + oc * 128: dk * D + (oc + 1) * 128],
                     ctxf[:, dk * 512:(dk + 1) * 512], dk == 0, dk == NCH - 1,
                     waits=(list(wo2_v) + list(ctx_v) + [("v", p.n("v"))]) if dk == 0 else (),
                     inc="t" if dk == NCH - 1 else None)
            xsl = x2sF[:, oc * 512:(oc + 1) * 512]
            p.ts("vector", scrF, ps2[:, :], bo2_s[:, oc:oc + 1], None, ALU.add, waits=p.at("t"))
            p.tt("vector", xsl, xsl, scrF, ALU.add)
        p.dma("gpsimd", x3a_d.rearrange("(c p) t -> p c t", p=128)[:, :, ts * 512:(ts + 1) * 512],
              x2sF.rearrange("p (c t) -> p c t", t=512), waits=p.at("v"))
    f1_done = p.at("t", "v", "s", "dgp")

    bufA = AR1[:, 0:32768]
    bufB = AR2[:, 0:32768]
    xn2sF = ARF[:, 0:4096].bitcast(BF16)

    def stream_mm(ps_list, w_in, Kdim, oc, xsrc, extra_waits):
        nkc = Kdim // 128
        piece = 8
        for p0 in range(0, nkc, piece):
            n = min(piece, nkc - p0)
            slot = (p0 // piece) % 2
            p.dma("scalar", stage[:, slot * 1024: slot * 1024 + n * 128]
                      .rearrange("p (c e) -> p c e", e=128),
                  w_in.rearrange("(c p) e -> p c e", p=128)[:, p0:p0 + n, oc * 128:(oc + 1) * 128],
                  waits=[("v", p.n("v"))])
            p.copy("vector", wt[:, slot * 1024: slot * 1024 + n * 128],
                   stage[:, slot * 1024: slot * 1024 + n * 128],
                   waits=p.at("dsc") + [("t", p.n("t"))])
            wv = p.at("v")
            for dk in range(p0, p0 + n):
                for th in range(len(ps_list)):
                    p.mm(ps_list[th][:, :], wt[:, slot * 1024 + (dk - p0) * 128:
                                                 slot * 1024 + (dk - p0) * 128 + 128],
                         xsrc(dk, th), dk == 0, dk == nkc - 1,
                         waits=(list(wv) + list(extra_waits) + [("s", p.n("s")), ("v", p.n("v"))])
                         if (dk == p0 and th == 0) else (),
                         inc="t" if (dk == nkc - 1 and th == len(ps_list) - 1) else None)

    for fs in range(2):
        p.dma("sync", xn2sF.rearrange("p (c t) -> p c t", t=1024),
              xn2_d[:, fs * 1024:(fs + 1) * 1024].rearrange("(c p) t -> p c t", p=128),
              waits=[("t", p.n("t")), ("v", p.n("v"))] + (f1_done if fs == 0 else []))
        xldF = p.at("dsy")
        xsrc_x = lambda dk, th: xn2sF[:, dk * 1024 + th * 512: dk * 1024 + th * 512 + 512]
        for oc in range(32):
            stream_mm([ps1, ps2], tw1_in, 1024, oc, xsrc_x, xldF)
            for th, psx in ((0, ps1), (1, ps2)):
                p.act(bufA[:, oc * 1024 + th * 512: oc * 1024 + th * 512 + 512], psx[:, :],
                      AF.Silu, scale=s1_s[:, oc:oc + 1], waits=p.at("t") + (list(sldF) if oc == 0 else []))
        for oc in range(32):
            stream_mm([ps3, ps4], tw2_in, 1024, oc, xsrc_x, xldF)
            for th, psx in ((0, ps3), (1, ps4)):
                p.ts("vector", bufB[:, oc * 1024 + th * 512: oc * 1024 + th * 512 + 512],
                     psx[:, :], s2_s[:, oc:oc + 1], None, ALU.mult, waits=p.at("t"))
        hv = p.at("v", "s")
        for oc in range(32):
            p.tt("vector", bufB[:, oc * 1024:(oc + 1) * 1024], bufA[:, oc * 1024:(oc + 1) * 1024],
                 bufB[:, oc * 1024:(oc + 1) * 1024], ALU.mult, waits=hv)
        h_v = p.at("v")
        xsrc_B = lambda dk, th: bufB[:, dk * 1024 + th * 512: dk * 1024 + th * 512 + 512]
        for oc in range(32):
            stream_mm([ps1, ps2], th1_in, 4096, oc, xsrc_B, h_v)
            for th, psx in ((0, ps1), (1, ps2)):
                p.act(bufA[:, oc * 1024 + th * 512: oc * 1024 + th * 512 + 512], psx[:, :],
                      AF.Gelu, scale=sh1_s[:, oc:oc + 1], waits=p.at("t"))
        h1_v = p.at("s")
        xsrc_A = lambda dk, th: bufA[:, dk * 1024 + th * 512: dk * 1024 + th * 512 + 512]
        for oc in range(32):
            stream_mm([ps3, ps4], th2_in, 4096, oc, xsrc_A, h1_v)
            for th, psx in ((0, ps3), (1, ps4)):
                p.act(bufB[:, oc * 1024 + th * 512: oc * 1024 + th * 512 + 512], psx[:, :],
                      AF.Gelu, scale=sh2_s[:, oc:oc + 1], waits=p.at("t"))
        h2_v = p.at("s")
        for sub in range(2):
            ts = fs * 2 + sub
            x3sv = ARF[:, 4096:8192]
            p.dma("sync", x3sv.rearrange("p (c t) -> p c t", t=512),
                  x3a_d.rearrange("(c p) t -> p c t", p=128)[:, :, ts * 512:(ts + 1) * 512],
                  waits=[("t", p.n("t")), ("v", p.n("v")), ("s", p.n("s")), ("dgp", p.n("dgp"))])
            x3ld = p.at("dsy")
            for oc in range(NCH):
                stream_mm([ps1], tw3_in, 4096, oc,
                          lambda dk, th, _s=sub: bufB[:, dk * 1024 + _s * 512: dk * 1024 + _s * 512 + 512],
                          h2_v)
                p.ts("vector", scrF, ps1[:, :], s3_s[:, oc:oc + 1], None, ALU.mult,
                     waits=p.at("t"))
                xsl = x3sv[:, oc * 512:(oc + 1) * 512]
                p.tt("vector", xsl, xsl, scrF, ALU.add, waits=x3ld)
            p.dma("gpsimd", x3_d.rearrange("(c p) t -> p c t", p=128)[:, :, ts * 512:(ts + 1) * 512],
                  x3sv.rearrange("p (c t) -> p c t", t=512), waits=p.at("v"))

    p.barrier()
    if DEBUG:
        p.dma("sync", dbg["d_x3"], x3_d)
        p.nop("sync", waits=p.at("dsy"))
    ar = Arena()
    ln_sq, ln_stat, ln_xf = carve_ln(ar)
    B = LNBufs(lnc, lnr, ln_sq, ln_stat, ln_xf, (ps_r1a, ps_r1b, ps512[4], ps512[5]))
    x_sbG = ar.f32(16384)
    xn_sbG = ar.bf16(4096)
    wqG_bf = ar.bf16(8192)
    wkG_bf = ar.bf16(8192)
    wvG_bf = ar.bf16(8192)
    k3e = ar.bf16(1040)
    v3e = ar.bf16(1040)
    kvacc = ar.f32(2080)
    qscG = ar.bf16(512)
    e1 = ar.f32(512)
    e2 = ar.f32(512)
    biasesG = ar.f32(8)
    brow = ar.bf16(2048, rows=1)
    stage = ar.f32(4096)

    ps_q = ps512[0]
    ps_v0 = ps512[1]
    ps_v1 = ps512[2]
    ps_kv = ps512[3]

    p.memset("vector", kvacc[:, :], 0.0)
    p.dma("sync", x_sbG.rearrange("p (c t) -> p c t", c=NCH),
          x3_d.rearrange("(c p) t -> p c t", p=128))
    xldG = p.at("dsy")
    for w_in, w_bf in ((wq3T_in, wqG_bf), (wk3T_in, wkG_bf), (wv3T_in, wvG_bf)):
        src3_ = w_in.rearrange("(c p) e -> p c e", p=128)
        for c0 in range(0, NCH, 4):
            p.dma("scalar", stage[:, 0:4 * D].rearrange("p (c e) -> p c e", e=D),
                  src3_[:, c0:c0 + 4, :], waits=[("v", p.n("v"))])
            p.copy("vector", w_bf[:, c0 * D:(c0 + 4) * D], stage[:, 0:4 * D], waits=p.at("dsc"))
    p.dma("scalar", stage[:, 0:8], bq3_in, waits=[("v", p.n("v"))])
    p.dma("scalar", stage[:1, 8:1032], bk3_in)
    p.dma("scalar", stage[:1, 1032:2056], bv3_in)
    p.copy("vector", biasesG[:, :], stage[:, 0:8], waits=p.at("dsc"))
    p.copy("vector", brow[:1, 0:1024], stage[:1, 8:1032])
    p.copy("vector", brow[:1, 1024:2048], stage[:1, 1032:2056])
    wldG = p.at("v")

    for ts in range(NST):
        bsel = 0 if ts < 2 else 1
        x_of_chunk = lambda c, _ts=ts: x_sbG[:, c * TC + _ts * 512: c * TC + (_ts + 1) * 512]

        def xn_cbG(c, xf, waits, _ts=ts):
            dst_b = xn_sbG[:, c * 512:(c + 1) * 512]
            p.copy("vector", dst_b, xf, waits=[("t", p.n("t")), ("dgp", p.n("dgp"))])
            p.dma("gpsimd", xn3_d[c * 128:(c + 1) * 128, _ts * 512:(_ts + 1) * 512],
                  dst_b, waits=[("v", p.n("v"))])

        emit_ln_strip(p, B, x_of_chunk, ts, xn_cbG, x_ready=xldG)
        xn_v = p.at("v")
        for c in range(NCH):
            for dk in range(NCH):
                p.mm(ps_q[:, :], wqG_bf[:, dk * D + c * 128: dk * D + (c + 1) * 128],
                     xn_sbG[:, dk * 512:(dk + 1) * 512], dk == 0, dk == NCH - 1,
                     waits=(list(wldG) + list(xn_v) + [("v", p.n("v")), ("dgp", p.n("dgp"))]) if dk == 0 else (),
                     inc="t" if dk == NCH - 1 else None)
            p.ts("vector", e1[:, :], ps_q[:, :], biasesG[:, c:c + 1], 0.0, ALU.add, ALU.min,
                 waits=p.at("t"))
            p.act(e1[:, :], e1[:, :], AF.Exp, waits=p.at("v"))
            p.ts("vector", e2[:, :], ps_q[:, :], biasesG[:, c:c + 1], 0.0, ALU.add, ALU.max)
            p.tt("vector", qscG[:, :], e1[:, :], e2[:, :], ALU.add, waits=p.at("s"))
            obv = p.at("v")
            hA, hB = 2 * c, 2 * c + 1
            p.dma("gpsimd", q3_d[0:64, hA * TC + ts * 512: hA * TC + (ts + 1) * 512],
                  qscG[0:64, :], waits=obv)
            p.dma("gpsimd", q3_d[0:64, hB * TC + ts * 512: hB * TC + (ts + 1) * 512],
                  qscG[64:128, :])
        for tb in range(4):
            for src_w, boff, dst_ext, feat in ((wkG_bf, 0, k3e, True), (wvG_bf, 1024, v3e, False)):
                for half in range(2):
                    psx = [ps_v0, ps_v1][half]
                    for dk in range(NCH):
                        p.mm(psx[:, :], xn_sbG[:, dk * 512 + tb * 128:
                                               dk * 512 + tb * 128 + 128],
                             src_w[:, dk * D + half * 512: dk * D + half * 512 + 512],
                             dk == 0, False,
                             waits=(list(xn_v) + [("v", p.n("v")), ("t", p.n("t"))]) if dk == 0 else ())
                    p.mm(psx[:, :], ones_r[:1, :], brow[:1, boff + half * 512: boff + (half + 1) * 512],
                         False, True, inc="t")
                vmw = p.at("t")
                for h in range(NH):
                    psx = [ps_v0, ps_v1][h // 8]
                    seg = psx[:, (h % 8) * 64:(h % 8) * 64 + 64]
                    if feat:
                        p.ts("vector", e1[:, 0:64], seg, 0.0, None, ALU.min, waits=vmw)
                        p.act(e1[:, 0:64], e1[:, 0:64], AF.Exp, waits=p.at("v"))
                        p.ts("vector", e2[:, 0:64], seg, 0.0, None, ALU.max)
                        p.tt("vector", dst_ext[:, h * 65: h * 65 + 64], e1[:, 0:64], e2[:, 0:64],
                             ALU.add, waits=p.at("s"))
                    else:
                        p.copy("vector", dst_ext[:, h * 65: h * 65 + 64], seg, waits=vmw)
            for h in range(NH):
                p.memset("vector", k3e[:, h * 65 + 64: h * 65 + 65], 1.0)
                p.memset("vector", v3e[:, h * 65 + 64: h * 65 + 65], 1.0)
            kv_v = p.at("v")
            for h in range(NH):
                p.mm(ps_kv[0:65, 0:65], k3e[:, h * 65: h * 65 + 65], v3e[:, h * 65: h * 65 + 65],
                     True, True, waits=kv_v + [("v", p.n("v"))], inc="t")
                acc_sl = kvacc[0:65, (bsel * NH + h) * 65:(bsel * NH + h + 1) * 65]
                p.tt("vector", acc_sl, acc_sl, ps_kv[0:65, 0:65], ALU.add, waits=p.at("t"))
    p.dma("gpsimd", kvp_l, kvacc[0:65, :], waits=p.at("v"))
    p.cc("AllReduce", ALU.add, [kvp_l], [kvp_g], waits=p.at("dgp"))
    cc_kvp = p.n("cc")

    p.barrier()
    ar = Arena()
    kvb = ar.bf16(2080)
    wo3_bf = ar.bf16(8192)
    w2t = ar.bf16(4096)
    ctxbf3 = ar.bf16(4096)
    hstr = ar.bf16(16384)
    w1G_bf = ar.bf16(32768)
    x4s = ar.f32(4096)
    abc = ar.f32(1024)
    bcz = ar.f32(512)
    miscH = ar.f32(64)
    scrF = ar.f32(512)
    stage = ar.f32(4096)
    arow = ar.f32(2048, rows=1)
    zrow = ar.f32(512, rows=1)
    q3s = hstr[:, 0:NH * 512]
    bo3_s = miscH[:, 0:8]
    fb1_s = ar.f32(32)
    fb2_s = ar.f32(8)

    ps1, ps2, ps3, ps4 = ps512[0], ps512[1], ps512[2], ps512[3]

    p.dma("sync", stage[0:65, 0:2080], kvp_g, waits=[("cc", cc_kvp)])
    p.copy("vector", kvb[0:65, :], stage[0:65, 0:2080], waits=p.at("dsy"))
    kv_v = p.at("v")
    for w_in, w_bf, gran in ((wo3T_in, wo3_bf, 4), (fw1_in, w1G_bf, 1)):
        src3_ = w_in.rearrange("(c p) e -> p c e", p=128)
        C, E = src3_.shape[1], src3_.shape[2]
        cpg = max(1, 4096 // E)
        for c0 in range(0, C, cpg):
            n = min(cpg, C - c0)
            p.dma("scalar", stage[:, 0:n * E].rearrange("p (c e) -> p c e", e=E),
                  src3_[:, c0:c0 + n, :], waits=[("v", p.n("v"))])
            p.copy("vector", w_bf[:, c0 * E:(c0 + n) * E], stage[:, 0:n * E], waits=p.at("dsc"))
    p.dma("scalar", stage[:, 0:8], bo3_in, waits=[("v", p.n("v"))])
    p.dma("scalar", stage[:, 8:40], fb1_in)
    p.dma("scalar", stage[:, 40:48], fb2_in)
    p.copy("vector", bo3_s, stage[:, 0:8], waits=p.at("dsc"))
    p.copy("vector", fb1_s, stage[:, 8:40])
    p.copy("vector", fb2_s, stage[:, 40:48])
    wldH = p.at("v")

    for ah in range(2):
        p.memset("vector", arow[:1, :], 0.0)
        for dk in range(NCH):
            p.act(w2t[:, 0:2048], w1G_bf[:, dk * 4096 + ah * 2048: dk * 4096 + (ah + 1) * 2048], AF.Abs,
                  waits=wldH + [("v", p.n("v")), ("t", p.n("t"))])
            aw = p.at("s")
            for seg in range(4):
                p.mm(ps1[0:1, :], ones_cb[:, :], w2t[:, seg * 512:(seg + 1) * 512], True, True,
                     waits=list(aw) + [("v", p.n("v"))], inc="t")
                p.tt("vector", arow[:1, seg * 512:(seg + 1) * 512], arow[:1, seg * 512:(seg + 1) * 512],
                     ps1[0:1, :], ALU.add, waits=p.at("t"))
        p.ts("vector", arow[:1, :], arow[:1, :], 1.0 / D, None, ALU.mult)
        for seg in range(4):
            p.mm(ps2[:, :], ones_rf[:, :], arow[:1, seg * 512:(seg + 1) * 512], True, True,
                 waits=p.at("v"), inc="t")
            p.copy("vector", abc[:, 0:512], ps2[:, :], waits=p.at("t"))
            for dk in range(NCH):
                csl = w1G_bf[:, dk * 4096 + ah * 2048 + seg * 512: dk * 4096 + ah * 2048 + (seg + 1) * 512]
                p.act(w2t[:, 2048:2560], csl, AF.Sign, waits=[("v", p.n("v")), ("t", p.n("t"))])
                p.tt("vector", csl, w2t[:, 2048:2560], abc[:, 0:512], ALU.mult, waits=p.at("s"))
    w1bin = p.at("v")

    w2src0 = fw2_in.rearrange("(c p) e -> p c e", p=128)
    p.memset("vector", arow[:1, 0:1024], 0.0)
    for hc in range(32):
        p.dma("scalar", stage[:, 0:1024], w2src0[:, hc, :], waits=[("s", p.n("s")), ("v", p.n("v"))])
        p.act(stage[:, 1024:2048], stage[:, 0:1024], AF.Abs, waits=p.at("dsc"))
        aw = p.at("s")
        for seg in range(2):
            p.mm(ps1[0:1, :], ones_cf[:, :], stage[:, 1024 + seg * 512:1024 + (seg + 1) * 512],
                 True, True, waits=list(aw) + [("v", p.n("v"))], inc="t")
            p.tt("vector", arow[:1, seg * 512:(seg + 1) * 512], arow[:1, seg * 512:(seg + 1) * 512],
                 ps1[0:1, :], ALU.add, waits=p.at("t"))
    p.ts("vector", arow[:1, 0:1024], arow[:1, 0:1024], 1.0 / 4096, None, ALU.mult)
    for seg in range(2):
        p.mm(ps2[:, :], ones_rf[:, :], arow[:1, seg * 512:(seg + 1) * 512], True, True,
             waits=p.at("v"), inc="t")
        p.copy("vector", abc[:, seg * 512:(seg + 1) * 512], ps2[:, :], waits=p.at("t"))
    a2v = p.at("v")
    for hc in range(32):
        p.dma("scalar", stage[:, 0:1024], w2src0[:, hc, :], waits=[("s", p.n("s")), ("v", p.n("v"))])
        p.act(w2t[:, 0:1024], stage[:, 0:1024], AF.Sign, waits=p.at("dsc"))
        p.tt("vector", w2t[:, 1024:2048], w2t[:, 0:1024], abc[:, 0:1024], ALU.mult,
             waits=list(a2v) + [("s", p.n("s")), ("dgp", p.n("dgp"))])
        p.dma("gpsimd", bw2_scr[hc * 128:(hc + 1) * 128, :], w2t[:, 1024:2048],
              waits=[("v", p.n("v"))])
    w2bin = p.at("dgp", "v")

    for ts in range(NST):
        bsel = 0 if ts < 2 else 1
        p.dma("sync", q3s[0:64, :].rearrange("q (h t) -> q h t", t=512),
              q3_d.rearrange("q (h t) -> q h t", t=TC)[:, :, ts * 512:(ts + 1) * 512],
              waits=[("t", p.n("t"))])
        p.dma("gpsimd", x4s[:, :].rearrange("p (c t) -> p c t", t=512),
              x3_d[:, ts * 512:(ts + 1) * 512].rearrange("(c p) t -> p c t", p=128),
              waits=[("v", p.n("v"))])
        qld = p.at("dsy", "dgp")
        for h in range(NH):
            p.mm(ps1[0:65, :], kvb[0:64, (bsel * NH + h) * 65:(bsel * NH + h + 1) * 65],
                 q3s[0:64, h * 512:(h + 1) * 512], True, True,
                 waits=list(qld) + list(kv_v) + [("v", p.n("v")), ("s", p.n("s"))], inc="t")
            p.ts("vector", zrow, ps1[64:65, :], 1e-6, None, ALU.add, waits=p.at("t"))
            p.recip(zrow, zrow)
            p.mm(ps2[:, :], ones_rf[:, :], zrow, True, True, waits=p.at("v"), inc="t")
            p.copy("vector", bcz, ps2[:, :], waits=p.at("t"))
            p.tt("vector", ctxbf3[(h % 2) * 64:(h % 2) * 64 + 64, (h // 2) * 512:(h // 2) * 512 + 512],
                 ps1[0:64, :], bcz[0:64, :], ALU.mult)
        ctx_v = p.at("v")
        for oc in range(NCH):
            for dk in range(NCH):
                p.mm(ps3[:, :], wo3_bf[:, dk * D + oc * 128: dk * D + (oc + 1) * 128],
                     ctxbf3[:, dk * 512:(dk + 1) * 512], dk == 0, dk == NCH - 1,
                     waits=(list(wldH) + list(ctx_v) + [("v", p.n("v"))]) if dk == 0 else (),
                     inc="t" if dk == NCH - 1 else None)
            xsl = x4s[:, oc * 512:(oc + 1) * 512]
            p.ts("vector", scrF, ps3[:, :], bo3_s[:, oc:oc + 1], None, ALU.add,
                 waits=p.at("t"))
            p.tt("vector", xsl, xsl, scrF, ALU.add)
        p.dma("scalar", ctxbf3[:, :].rearrange("p (c t) -> p c t", t=512),
              xn3_d[:, ts * 512:(ts + 1) * 512].rearrange("(c p) t -> p c t", p=128),
              waits=[("t", p.n("t")), ("v", p.n("v"))])
        xnld = p.at("dsc")
        for oc in range(32):
            psx = [ps1, ps2][oc % 2]
            for dk in range(NCH):
                p.mm(psx[:, :], w1G_bf[:, dk * 4096 + oc * 128: dk * 4096 + (oc + 1) * 128],
                     ctxbf3[:, dk * 512:(dk + 1) * 512], dk == 0, dk == NCH - 1,
                     waits=(list(xnld) + list(w1bin) + [("s", p.n("s")), ("v", p.n("v"))]) if dk == 0 else (),
                     inc="t" if dk == NCH - 1 else None)
            p.act(hstr[:, oc * 512:(oc + 1) * 512], psx[:, :], AF.Gelu,
                  bias=fb1_s[:, oc:oc + 1], waits=p.at("t"))
        h_s = p.at("s")
        w2src = bw2_scr.rearrange("(c p) e -> p c e", p=128)
        for oc in range(NCH):
            psx = [ps3, ps4][oc % 2]
            for p0 in range(0, 32, 16):
                slot = (p0 // 16) % 2
                p.dma("scalar", w2t[:, slot * 2048: slot * 2048 + 2048]
                          .rearrange("p (c e) -> p c e", e=128),
                      w2src[:, p0:p0 + 16, oc * 128:(oc + 1) * 128],
                      waits=[("t", p.n("t"))] + (list(w2bin) if (ts == 0 and oc == 0) else []))
                wv2 = p.at("dsc")
                for dk in range(p0, p0 + 16):
                    p.mm(psx[:, :], w2t[:, slot * 2048 + (dk - p0) * 128:
                                        slot * 2048 + (dk - p0) * 128 + 128],
                         hstr[:, dk * 512:(dk + 1) * 512], dk == 0, dk == 31,
                         waits=(list(h_s) + list(wv2) + [("v", p.n("v"))]) if dk == p0 else (),
                         inc="t" if dk == 31 else None)
            xsl = x4s[:, oc * 512:(oc + 1) * 512]
            p.ts("vector", scrF, psx[:, :], fb2_s[:, oc:oc + 1], None, ALU.add,
                 waits=p.at("t"))
            p.tt("vector", xsl, xsl, scrF, ALU.add)
        p.dma("gpsimd", out_out.rearrange("(c p) t -> p c t", p=128)[:, :, ts * 512:(ts + 1) * 512],
              x4s[:, :].rearrange("p (c t) -> p c t", t=512), waits=p.at("v"))
    p.nop("gpsimd", waits=p.at("dgp"))
    p.nop("sync", waits=p.at("dgp", "dsy", "dsc"))
    return p


def host_fused(inputs):
    gi = [int(v) for v in np.asarray(inputs["L2_attn_global_indices"]).reshape(-1)]
    x = np.asarray(inputs["L0_attn_x"])
    xsh_fm = [fmT(s).astype(np.float32) for s in shard_rows(x)]

    col = lambda v, n: np.ascontiguousarray(np.asarray(v).reshape(n, 128).T).astype(np.float32)
    wdq = np.ascontiguousarray(inputs["L0_attn_W_dq"]).astype(np.float32)
    wuq = np.asarray(inputs["L0_attn_W_uq"]).astype(np.float32)
    wuqr = np.ascontiguousarray(np.concatenate(
        [wuq[64 * h:64 * (h + 1), :] for h in range(NH)], axis=1)).astype(np.float32)
    perm = rope_perm()
    wq1T = fmT(inputs["L1_attn_wq"]).astype(np.float32)[:, perm]
    wk1T = fmT(inputs["L1_attn_wk"]).astype(np.float32)[:, perm]

    W1 = np.asarray(inputs["L0_ffn_W1"]).astype(np.float32)
    W2 = np.asarray(inputs["L0_ffn_W2"]).astype(np.float32)
    b1 = np.asarray(inputs["L0_ffn_b1"]).astype(np.float32)
    b2 = np.asarray(inputs["L0_ffn_b2"]).astype(np.float32)
    b1c = np.concatenate([b1[e].reshape(32, 128).T for e in range(8)], axis=1).astype(np.float32)
    b2c = np.concatenate([b2[e].reshape(8, 128).T for e in range(8)], axis=1).astype(np.float32)

    common = {
        "id128": np.eye(128, dtype=np.float32),
        "sel8": np.ascontiguousarray(np.kron(np.eye(8, dtype=np.float32), np.ones((1, 128), np.float32))),
        "dmaskf": make_dmask(),
        "dmaskb": make_dmask().astype(BF16NP),
        "bmask": make_bmask(),
        "wdkvT": fmT(inputs["L0_attn_W_dkv"]).astype(np.float32),
        "wg": np.ascontiguousarray(inputs["L0_ffn_Wg"]).astype(np.float32),
        "bg": np.asarray(inputs["L0_ffn_bg"]).reshape(1, 8).astype(np.float32),
        "wdq": wdq, "wdqT": fmT(wdq).astype(np.float32),
        "wuqT": fmT(wuq).astype(np.float32), "wuqr": wuqr,
        "wuk": np.ascontiguousarray(inputs["L0_attn_W_uk"]).astype(np.float32),
        "wuv": np.ascontiguousarray(inputs["L0_attn_W_uv"]).astype(np.float32),
        "woT": fmT(inputs["L0_attn_W_o"]).astype(np.float32),
        "w1e": np.ascontiguousarray(W1.reshape(8 * D, 4096)),
        "w2e": np.ascontiguousarray(W2.reshape(8 * 4096, D)),
        "b1c": b1c, "b2c": b2c,
        "wq1Tp": np.ascontiguousarray(wq1T), "wk1Tp": np.ascontiguousarray(wk1T),
        "wv1T": fmT(inputs["L1_attn_wv"]).astype(np.float32),
        "wo1T": fmT(inputs["L1_attn_wo"]).astype(np.float32),
        "cw1": np.ascontiguousarray(inputs["L1_ffn_cn_w1"]).astype(np.float32),
        "cw2": np.ascontiguousarray(inputs["L1_ffn_cn_w2"]).astype(np.float32),
        "cw3": np.ascontiguousarray(inputs["L1_ffn_cn_w3"]).astype(np.float32),
        "cw4": np.ascontiguousarray(inputs["L1_ffn_cn_w4"]).astype(np.float32),
        "cb1": col(inputs["L1_ffn_cn_b1"], 8), "cb2": col(inputs["L1_ffn_cn_b2"], 4),
        "cb3": col(inputs["L1_ffn_cn_b3"], 2),
        "cb4": np.asarray(inputs["L1_ffn_cn_b4"]).reshape(1, 1).astype(np.float32),
        "gw1": np.ascontiguousarray(inputs["L1_ffn_gc_w1"]).astype(np.float32),
        "gw2": np.ascontiguousarray(inputs["L1_ffn_gc_w2"]).astype(np.float32),
        "gw3": np.ascontiguousarray(inputs["L1_ffn_gc_w3"]).astype(np.float32),
        "gb1": col(inputs["L1_ffn_gc_b1"], 4), "gb2": col(inputs["L1_ffn_gc_b2"], 2),
        "gb3": np.asarray(inputs["L1_ffn_gc_b3"]).reshape(1, 1).astype(np.float32),
        "ar": np.asarray(inputs["L1_ffn_adaptive_range"]).reshape(1, 1).astype(np.float32),
        "wq2T": fmT(inputs["L2_attn_Wq"]).astype(np.float32),
        "wk2T": fmT(inputs["L2_attn_Wk"]).astype(np.float32),
        "wv2T": fmT(inputs["L2_attn_Wv"]).astype(np.float32),
        "bq2": col(inputs["L2_attn_bq"], 8), "bk2": col(inputs["L2_attn_bk"], 8),
        "bv2": np.asarray(inputs["L2_attn_bv"]).reshape(1, D).astype(np.float32),
        "wo2T": fmT(inputs["L2_attn_Wo"]).astype(np.float32),
        "bo2": col(inputs["L2_attn_bo"], 8),
        "tw1": fmT(inputs["L2_ffn_w1_w"]).astype(np.float32),
        "tw2": fmT(inputs["L2_ffn_w2_w"]).astype(np.float32),
        "th1": fmT(inputs["L2_ffn_h1_w"]).astype(np.float32),
        "th2": fmT(inputs["L2_ffn_h2_w"]).astype(np.float32),
        "tw3": fmT(inputs["L2_ffn_w3_w"]).astype(np.float32),
        "s1": col(inputs["L2_ffn_w1_s"], 32), "s2": col(inputs["L2_ffn_w2_s"], 32),
        "sh1": col(inputs["L2_ffn_h1_s"], 32), "sh2": col(inputs["L2_ffn_h2_s"], 32),
        "s3": col(inputs["L2_ffn_w3_s"], 8),
        "wq3T": fmT(inputs["L3_attn_Wq"]).astype(np.float32),
        "wk3T": fmT(inputs["L3_attn_Wk"]).astype(np.float32),
        "wv3T": fmT(inputs["L3_attn_Wv"]).astype(np.float32),
        "bq3": col(inputs["L3_attn_bq"], 8),
        "bk3": np.asarray(inputs["L3_attn_bk"]).reshape(1, D).astype(np.float32),
        "bv3": np.asarray(inputs["L3_attn_bv"]).reshape(1, D).astype(np.float32),
        "wo3T": fmT(inputs["L3_attn_Wo"]).astype(np.float32),
        "bo3": col(inputs["L3_attn_bo"], 8),
        "fw1": fmT(inputs["L3_ffn_w1"]).astype(np.float32),
        "fb1": col(inputs["L3_ffn_b1"], 32),
        "fw2": fmT(inputs["L3_ffn_w2"]).astype(np.float32),
        "fb2": col(inputs["L3_ffn_b2"], 8),
    }
    maps = []
    for core in range(NCORES):
        m = dict(common)
        m["x"] = xsh_fm[core]
        m["mb"] = make_mb(core)
        sin_t, cos_t = rope_tables(core)
        m["sint"] = sin_t
        m["cost"] = cos_t
        maps.append(m)
    return maps, gi


_BUILT = {}


def kernel(**inputs):
    maps, gi = host_fused(inputs)
    key = ("fused", tuple(gi))
    if key not in _BUILT:
        _BUILT[key] = build_fused(gi).finalize()
    res = run_bass_kernel_spmd(_BUILT[key], maps, core_ids=list(range(NCORES)))
    out_sh = [np.ascontiguousarray(r["out"].T) for r in res.results]
    return unshard_rows(out_sh)


# version 05
# speedup vs baseline: 1.0205x; 1.0205x over previous
import numpy as np
import ml_dtypes
from contextlib import ExitStack

import concourse.bass as bass
import concourse.mybir as mybir
from concourse.bass_utils import run_bass_kernel_spmd

F32 = mybir.dt.float32
BF16 = mybir.dt.bfloat16
AF = mybir.ActivationFunctionType
ALU = mybir.AluOpType
AX = mybir.AxisListType
BF16NP = ml_dtypes.bfloat16

NCORES = 8
D = 1024
NH = 16
DH = 64
TB = 8192
TC = 2048
HC = 1024
NCH = 8
EXPB = np.float32(-20.0)
SC8 = np.float32(0.125)
NEGC = -1.0e9
NKB = 64
NST = 4

fp = np.float32
DEBUG = False


def chunk_of(core):
    return core, 7 - core


def strip_list(core):
    c0, c1 = chunk_of(core)
    return [(0, c0 * HC), (0, c0 * HC + 512), (1, c1 * HC), (1, c1 * HC + 512)]


def owner_of(b, pos):
    if b == 0:
        return pos // HC, pos % HC
    return 7 - pos // HC, HC + pos % HC


def shard_rows(x):
    out = []
    for core in range(NCORES):
        c0, c1 = chunk_of(core)
        out.append(np.concatenate([x[0, c0 * HC:(c0 + 1) * HC], x[1, c1 * HC:(c1 + 1) * HC]], axis=0))
    return out


def unshard_rows(shards):
    x = np.empty((2, TB) + shards[0].shape[1:], dtype=shards[0].dtype)
    for core in range(NCORES):
        c0, c1 = chunk_of(core)
        x[0, c0 * HC:(c0 + 1) * HC] = shards[core][:HC]
        x[1, c1 * HC:(c1 + 1) * HC] = shards[core][HC:]
    return x


def fmT(a):
    return np.ascontiguousarray(np.asarray(a).T)


ENGS = ("sync", "scalar", "vector", "gpsimd", "tensor")
SEMS = ("dsy", "dsc", "dve", "dgp", "t", "v", "s", "g", "cc")
DMA_SEM = {"sync": "dsy", "scalar": "dsc", "vector": "dve", "gpsimd": "dgp"}


class Prog:
    def __init__(self, name):
        self.nc = bass.Bass("TRN2", target_bir_lowering=False, name=name)
        self.tasks = {e: [] for e in ENGS}
        self.cnt = {sname: 0 for sname in SEMS}
        self.wm = {e: {} for e in ENGS}

    def n(self, s):
        return self.cnt[s]

    def at(self, *sems):
        return [(s, self.cnt[s]) for s in sems]

    def _resolve(self, eng, waits):
        res = []
        for w in waits:
            name, val = w if isinstance(w, tuple) else (w, self.cnt[w])
            if val <= 0:
                continue
            if self.wm[eng].get(name, -1) < val:
                res.append((name, val))
                self.wm[eng][name] = val
        return res

    AUTOSYNC = {"vector": "v", "scalar": "s", "gpsimd": "g"}

    def op(self, eng, builder, waits=(), inc=None, inc_n=1, dma_op=False):
        auto = self.AUTOSYNC.get(eng)
        if auto is not None:
            waits = list(waits) + [(auto, self.cnt[auto])]
            if inc is None and not dma_op:
                inc = auto
        res = self._resolve(eng, waits)
        tgt = None
        if inc is not None:
            self.cnt[inc] += inc_n
            tgt = (inc, inc_n)

        def task(engine, _res=res, _b=builder, _tgt=tgt):
            for name, val in _res:
                engine.wait_ge(self._sem[name], val)
            inst = _b(engine)
            if _tgt is not None and inst is not None:
                inst.then_inc(self._sem[_tgt[0]], _tgt[1])
        self.tasks[eng].append(task)

    def dma(self, eng, out, in_, waits=(), accum=False, slow=False):
        kw = {"accum_op": ALU.add} if accum else {}
        if slow:
            kw["allow_slow_non_contiguous"] = True
        self.op(eng, lambda E, o=out, i=in_, k=kw: E.dma_start(out=o, in_=i, **k),
                waits=waits, inc=DMA_SEM[eng], inc_n=16, dma_op=True)

    def cc(self, kind, op_, ins, outs, waits=()):
        self.op("gpsimd",
                lambda E, k=kind, o=op_, i=ins, u=outs: E.collective_compute(
                    k, o, replica_groups=[list(range(NCORES))],
                    ins=[a.opt() for a in i], outs=[a.opt() for a in u]),
                waits=waits, inc="cc", inc_n=1, dma_op=True)

    def mm(self, out, lhsT, rhs, start, stop, waits=(), inc=None, tile_position=None):
        self.op("tensor",
                lambda E, o=out, l=lhsT, r=rhs, s=start, e=stop, tp=tile_position:
                E.matmul(o, l, r, start=s, stop=e, tile_position=tp),
                waits=waits, inc=inc)

    def trans(self, out, in_, ident, waits=(), inc=None):
        self.op("tensor", lambda E, o=out, i=in_, d=ident: E.transpose(o, i, d),
                waits=waits, inc=inc)

    def act(self, out, in_, func, bias=0.0, scale=1.0, waits=(), inc=None, accum_out=None):
        bias = fp(bias) if isinstance(bias, float) else bias
        scale = fp(scale) if isinstance(scale, float) else scale
        if accum_out is None:
            self.op("scalar", lambda E, o=out, i=in_, f=func, b=bias, sc=scale:
                    E.activation(o, i, f, bias=b, scale=sc), waits=waits, inc=inc)
        else:
            self.op("scalar", lambda E, o=out, i=in_, f=func, b=bias, sc=scale, a=accum_out:
                    E.activation(o, i, f, bias=b, scale=sc, accum_out=a), waits=waits, inc=inc)

    def tt(self, eng, out, a, b, aop, waits=(), inc=None):
        self.op(eng, lambda E, o=out, x=a, y=b, p=aop: E.tensor_tensor(o, x, y, p),
                waits=waits, inc=inc)

    def ts(self, eng, out, in_, s1, s2, op0, op1=ALU.bypass, waits=(), inc=None):
        s1 = float(s1) if isinstance(s1, (float, np.floating)) else s1
        s2 = float(s2) if isinstance(s2, (float, np.floating)) else s2
        self.op(eng, lambda E, o=out, i=in_, a=s1, b=s2, p0=op0, p1=op1:
                E.tensor_scalar(o, i, a, b, p0, p1), waits=waits, inc=inc)

    def stt(self, eng, out, in0, scalar, in1, op0, op1, waits=(), inc=None):
        scalar = float(scalar) if isinstance(scalar, (np.floating, float)) else scalar
        self.op(eng, lambda E, o=out, i=in0, sc=scalar, j=in1, p0=op0, p1=op1:
                E.scalar_tensor_tensor(o, i, sc, j, p0, p1), waits=waits, inc=inc)

    def copy(self, eng, out, in_, waits=(), inc=None):
        self.op(eng, lambda E, o=out, i=in_: E.tensor_copy(o, i), waits=waits, inc=inc)

    def red(self, out, in_, aop, negate=False, waits=(), inc=None):
        self.op("vector", lambda E, o=out, i=in_, p=aop, ng=negate:
                E.tensor_reduce(o, i, AX.X, p, negate=ng), waits=waits, inc=inc)

    def recip(self, out, in_, waits=(), inc=None):
        self.op("vector", lambda E, o=out, i=in_: E.reciprocal(o, i), waits=waits, inc=inc)

    def memset(self, eng, ap_, val, waits=(), inc=None):
        self.op(eng, lambda E, a=ap_, v=val: E.memset(a, v), waits=waits, inc=inc)

    def nop(self, eng, waits=(), inc=None):
        self.op(eng, lambda E: None, waits=waits, inc=inc, dma_op=True)

    def barrier(self):
        allw = self.at(*SEMS)
        for e in ENGS:
            self.nop(e, waits=allw)

    def finalize(self):
        nc = self.nc
        with ExitStack() as st:
            self._sem = {name: st.enter_context(nc.semaphore(name)) for name in SEMS}
            block = st.enter_context(nc.Block())
            for eng in ENGS:
                if not self.tasks[eng]:
                    continue
                tasks = self.tasks[eng]

                def runner(engine, _tasks=tasks):
                    for t in _tasks:
                        t(engine)
                getattr(block, eng)(runner)
        return nc


class LNBufs:
    def __init__(self, ones_c, ones_r, sq, stat, xf, psum):
        self.ones_c = ones_c
        self.ones_r = ones_r
        self.sq = sq
        self.stat = stat
        self.xf = xf
        self.ps_sum, self.ps_sq, self.ps_a, self.ps_b = psum


def emit_ln_strip(p, B, x_of_chunk, ts, xn_cb, x_ready=()):
    strip_gate = p.at("v", "s")
    for c in range(NCH):
        p.mm(B.ps_sum[:, :], B.ones_c[:, :], x_of_chunk(c), c == 0, c == NCH - 1,
             waits=(list(x_ready) + strip_gate) if c == 0 else (),
             inc="t" if c == NCH - 1 else None)
    for c in range(NCH):
        slot = c % 2
        sq = B.sq[:, slot * 512:(slot + 1) * 512]
        p.act(sq, x_of_chunk(c), AF.Square,
              waits=list(x_ready) + strip_gate + ([("t", p.n("t"))] if c >= 2 else []), inc="s")
        p.mm(B.ps_sq[:, :], B.ones_c[:, :], sq, c == 0, c == NCH - 1,
             waits=[("s", p.n("s"))], inc="t")
    mm_done = p.at("t")
    mean = B.stat[:, 0:512]; ex2 = B.stat[:, 512:1024]; var = B.stat[:, 1024:1536]
    rstd = B.stat[:, 1536:2048]; brow = B.stat[:, 2048:2560]; sqv = B.stat[:, 2560:3072]
    p.ts("vector", mean, B.ps_sum[:, :], 1.0 / D, None, ALU.mult, waits=mm_done)
    p.ts("vector", ex2, B.ps_sq[:, :], 1.0 / D, None, ALU.mult)
    p.tt("vector", var, mean, mean, ALU.mult)
    p.tt("vector", var, ex2, var, ALU.subtract, inc="v")
    p.act(sqv, var, AF.Sqrt, bias=1e-5, waits=[("v", p.n("v"))], inc="s")
    p.recip(rstd, sqv, waits=[("s", p.n("s"))])
    p.tt("vector", brow, mean, rstd, ALU.mult)
    p.ts("vector", brow, brow, -1.0, None, ALU.mult, inc="v")
    stats_v = p.at("v")
    p.mm(B.ps_a[:, :], B.ones_r[:, :], rstd, True, True, waits=stats_v)
    p.mm(B.ps_b[:, :], B.ones_r[:, :], brow, True, True, inc="t")
    bc = p.at("t")
    for c in range(NCH):
        p.tt("vector", B.xf[:, :], x_of_chunk(c), B.ps_a[:, :], ALU.mult, waits=bc)
        p.tt("vector", B.xf[:, :], B.xf[:, :], B.ps_b[:, :], ALU.add, inc="v")
        xn_cb(c, B.xf[:, :], p.at("v"))


def make_dmask():
    m = np.zeros((128, 4 * 512), dtype=np.float32)
    for kb in range(4):
        kk = np.arange(128)[:, None]
        qq = np.arange(512)[None, :]
        m[:, kb * 512:(kb + 1) * 512] = np.where(kb * 128 + kk > qq, NEGC, 0.0)
    return m


def make_bmask():
    m = np.zeros((128, 6 * 512), dtype=np.float32)
    kk = np.arange(128)[:, None]
    qq = np.arange(512)[None, :]
    qb = (qq // 64) * 64
    for cc in range(6):
        kpos = -128 + 128 * cc + kk
        bad = (kpos < qb - 64) | (kpos >= qb + 128)
        m[:, cc * 512:(cc + 1) * 512] = np.where(bad, NEGC, 0.0)
    return m.astype(BF16NP)


def make_mb(core):
    m = np.full((128, NST * NKB), NEGC, dtype=np.float32)
    for s, (b, pos0) in enumerate(strip_list(core)):
        nval = pos0 // 128
        m[:, s * NKB: s * NKB + nval] = float(EXPB)
    return m


def rope_tables(core):
    inv = (10000.0 ** (-(np.arange(32, dtype=np.float64) * 2.0 / 64.0)))
    pos = np.empty(TC, dtype=np.float64)
    for s, (b, pos0) in enumerate(strip_list(core)):
        pos[s * 512:(s + 1) * 512] = np.arange(pos0, pos0 + 512)
    fr = pos[None, :] * inv[:, None]
    fr = np.concatenate([fr, fr], axis=0)
    return (np.sin(fr.astype(np.float32).astype(np.float64)).astype(np.float32),
            np.cos(fr.astype(np.float32).astype(np.float64)).astype(np.float32))


def rope_perm():
    perm = []
    for c in range(NCH):
        A, Bh = 2 * c, 2 * c + 1
        perm += list(64 * A + np.arange(0, 64, 2)) + list(64 * Bh + np.arange(0, 64, 2))
        perm += list(64 * A + np.arange(1, 64, 2)) + list(64 * Bh + np.arange(1, 64, 2))
    return np.asarray(perm)


TWIST_MAP = [(4, 1.0), (5, 1.0), (6, -1.0), (7, -1.0), (0, -1.0), (1, -1.0), (2, 1.0), (3, 1.0)]


def build_fused(gi):
    p = Prog("mkF1")
    nc = p.nc

    def ein(name, shape, dt=F32):
        return nc.dram_tensor(name, shape, dt, kind="ExternalInput").ap()

    x_in = ein("x", [D, TC])
    id128 = ein("id128", [128, 128])
    sel8 = ein("sel8", [8, 8 * 128])
    mb_in = ein("mb", [128, NST * NKB])
    dmaskf_in = ein("dmaskf", [128, 4 * 512])
    dmaskb_in = ein("dmaskb", [128, 4 * 512], BF16)
    bmask_in = ein("bmask", [128, 6 * 512], BF16)
    wdkvT_in = ein("wdkvT", [D, 128])
    wg_in = ein("wg", [D, 8]); bg_in = ein("bg", [1, 8])
    wdq_in = ein("wdq", [128, D]); wdqT_in = ein("wdqT", [D, 128])
    wuqT_in = ein("wuqT", [128, D]); wuqr_in = ein("wuqr", [64, NH * 128])
    wuk_in = ein("wuk", [D, 128]); wuv_in = ein("wuv", [D, 128]); woT_in = ein("woT", [D, D])
    w1e_in = ein("w1e", [8 * D, 4096])
    w2e_in = ein("w2e", [8 * 4096, D])
    b1c_in = ein("b1c", [128, 8 * 32])
    b2c_in = ein("b2c", [128, 8 * 8])
    wq1_in = ein("wq1Tp", [D, D]); wk1_in = ein("wk1Tp", [D, D]); wv1_in = ein("wv1T", [D, D])
    sin_in = ein("sint", [64, TC]); cos_in = ein("cost", [64, TC])
    wo1T_in = ein("wo1T", [D, D])
    cw1_in = ein("cw1", [D, D]); cw2_in = ein("cw2", [D, 512])
    cw3_in = ein("cw3", [512, 256]); cw4_in = ein("cw4", [256, 1])
    cb1_in = ein("cb1", [128, 8]); cb2_in = ein("cb2", [128, 4])
    cb3_in = ein("cb3", [128, 2]); cb4_in = ein("cb4", [1, 1])
    gw1_in = ein("gw1", [D, 512]); gw2_in = ein("gw2", [512, 256]); gw3_in = ein("gw3", [256, 1])
    gb1_in = ein("gb1", [128, 4]); gb2_in = ein("gb2", [128, 2]); gb3_in = ein("gb3", [1, 1])
    ar_in = ein("ar", [1, 1])
    wq2T_in = ein("wq2T", [D, D]); wk2T_in = ein("wk2T", [D, D]); wv2T_in = ein("wv2T", [D, D])
    bq2_in = ein("bq2", [128, 8]); bk2_in = ein("bk2", [128, 8]); bv2_in = ein("bv2", [1, D])
    wo2T_in = ein("wo2T", [D, D]); bo2_in = ein("bo2", [128, 8])
    tw1_in = ein("tw1", [D, 4096]); tw2_in = ein("tw2", [D, 4096])
    th1_in = ein("th1", [4096, 4096]); th2_in = ein("th2", [4096, 4096])
    tw3_in = ein("tw3", [4096, D])
    s1_in = ein("s1", [128, 32]); s2_in = ein("s2", [128, 32])
    sh1_in = ein("sh1", [128, 32]); sh2_in = ein("sh2", [128, 32]); s3_in = ein("s3", [128, 8])
    wq3T_in = ein("wq3T", [D, D]); wk3T_in = ein("wk3T", [D, D]); wv3T_in = ein("wv3T", [D, D])
    bq3_in = ein("bq3", [128, 8]); bk3_in = ein("bk3", [1, D]); bv3_in = ein("bv3", [1, D])
    wo3T_in = ein("wo3T", [D, D]); bo3_in = ein("bo3", [128, 8])
    fw1_in = ein("fw1", [D, 4096]); fb1_in = ein("fb1", [128, 32])
    fw2_in = ein("fw2", [4096, D]); fb2_in = ein("fb2", [128, 8])

    out_out = nc.dram_tensor("out", [D, TC], F32, kind="ExternalOutput").ap()

    def idr(name, shape, dt=F32):
        return nc.dram_tensor(name, shape, dt).ap()

    ckv_l = idr("ckv_l", [128, TC], BF16); ckv_g = idr("ckv_g", [8 * 128, TC], BF16)
    ckvT_l = idr("ckvT_l", [TC, 128], BF16); ckvT_g = idr("ckvT_g", [8 * TC, 128], BF16)
    x1a_d = idr("x1a_d", [D, TC])
    xn1_d = idr("xn1_d", [D, TC], BF16)
    q1_d = idr("q1_d", [64, NH * TC], BF16)
    k1_l = idr("k1_l", [64, NH * TC], BF16); k1_g = idr("k1_g", [8 * 64, NH * TC], BF16)
    v1_l = idr("v1_l", [TC, NH * 65], BF16); v1_g = idr("v1_g", [8 * TC, NH * 65], BF16)
    gmp_l = idr("gmp_l", [128, 16]); gmp_g = idr("gmp_g", [128, 16])
    attn1_scr = idr("attn1_scr", [D, TC])
    x2_d = idr("x2_d", [D, TC])
    xn2_d = idr("xn2_d", [D, TC], BF16)
    q2_l = idr("q2_l", [64, NH * TC], BF16); q2_g = idr("q2_g", [8 * 64, NH * TC], BF16)
    k2_l = idr("k2_l", [64, NH * TC], BF16); k2_g = idr("k2_g", [8 * 64, NH * TC], BF16)
    v2_l = idr("v2_l", [TC, NH * 65], BF16); v2_g = idr("v2_g", [8 * TC, NH * 65], BF16)
    ctx2_l = idr("ctx2_l", [8 * D, TC], BF16)
    ctx2_r = idr("ctx2_r", [D, TC], BF16)
    x3a_d = idr("x3a_d", [D, TC])
    x3_d = idr("x3_d", [D, TC])
    xn3_d = idr("xn3_d", [D, TC], BF16)
    q3_d = idr("q3_d", [64, NH * TC], BF16)
    kvp_l = idr("kvp_l", [65, 2 * NH * 65]); kvp_g = idr("kvp_g", [65, 2 * NH * 65])
    bw2_scr = idr("bw2_scr", [4096, D], BF16)

    dbg = {}
    if DEBUG:
        for nm, shape, dt in [("d_x1a", [D, TC], F32), ("d_xn1", [D, TC], BF16),
                              ("d_x2", [D, TC], F32), ("d_ctx2", [D, TC], BF16),
                              ("d_x3", [D, TC], F32)]:
            dbg[nm] = nc.dram_tensor(nm, shape, dt, kind="ExternalOutput").ap()

    ZZCOLS = 47104
    ZZ = nc.alloc_sbuf_tensor("ZZ", [128, ZZCOLS], F32)
    ZZB = ZZ.bitcast(BF16)
    ones_r = nc.alloc_sbuf_tensor("ones_r", [1, 128], BF16)
    ones_rf = nc.alloc_sbuf_tensor("ones_rf", [1, 128], F32)
    ones_cb = nc.alloc_sbuf_tensor("ones_cb", [128, 1], BF16)
    ones_cf = nc.alloc_sbuf_tensor("ones_cf", [128, 1], F32)
    lnc = nc.alloc_sbuf_tensor("lnc", [128, 1], F32)
    lnr = nc.alloc_sbuf_tensor("lnr", [1, 128], F32)

    class Arena:
        def __init__(self, resv=0):
            self.off = 0
            if resv:
                self.off = resv
        def f32(self, n, rows=128):
            ap = ZZ[0:rows, self.off:self.off + n]
            self.off += n
            assert self.off <= ZZCOLS, f"arena overflow {self.off}"
            return ap
        def bf16(self, n, rows=128):
            n2 = (n + 1) // 2
            ap = ZZB[0:rows, 2 * self.off: 2 * self.off + n]
            self.off += n2
            assert self.off <= ZZCOLS, f"arena overflow {self.off}"
            return ap

    def carve_ln(ar):
        return (ar.f32(1024), ar.f32(3072, rows=1), ar.f32(512))

    def carve_resAB(ar):
        return (ar.bf16(16384),
                ar.bf16(2048),
                ar.bf16(2048),
                ar.f32(2048, rows=8))

    ps512 = [nc.alloc_psum_tensor(f"ps{i}", [128, 512], F32) for i in range(6)]
    ps_r1a = nc.alloc_psum_tensor("ps_r1a", [1, 512], F32)
    ps_r1b = nc.alloc_psum_tensor("ps_r1b", [1, 512], F32)

    ar = Arena()
    xn_sb, ckv_sb, ckvT_sb, mwT_s = carve_resAB(ar)
    LNPS = (ps_r1a, ps_r1b, ps512[4], ps512[5])
    ln_sq, ln_stat, ln_xf = carve_ln(ar)
    B = LNBufs(lnc, lnr, ln_sq, ln_stat, ln_xf, LNPS)
    x_sb = ar.f32(16384)
    xfs = ar.f32(4096)
    wdkvT_sb = ar.f32(1024)
    wdkvT_bf = ar.bf16(1024)
    wg_sb = ar.f32(64)
    gmisc = ar.f32(192)
    mw_sb = ar.f32(128)
    id_s = ar.f32(128)
    sel_s = None
    mb_s = ar.f32(NST * NKB)
    stage = ar.f32(4096)
    bg_sb = gmisc[0:1, 184:192]

    p.memset("vector", B.ones_c[:, :], 1.0)
    p.memset("vector", B.ones_r[:, :], 1.0)
    p.memset("vector", ones_r[:, :], 1.0)
    p.memset("vector", ones_rf[:, :], 1.0)
    p.memset("vector", ones_cb[:, :], 1.0)
    p.memset("vector", ones_cf[:, :], 1.0)
    p.dma("sync", x_sb.rearrange("p (c t) -> p c t", c=NCH), x_in.rearrange("(c p) t -> p c t", p=128))
    p.dma("sync", id_s[:, :], id128)
    p.dma("sync", mb_s[:, :], mb_in)
    p.dma("sync", wdkvT_sb.rearrange("p (c l) -> p c l", c=NCH), wdkvT_in.rearrange("(c p) l -> p c l", p=128))
    p.dma("sync", wg_sb.rearrange("p (c e) -> p c e", c=NCH), wg_in.rearrange("(c p) e -> p c e", p=128))
    p.dma("sync", bg_sb, bg_in)
    lded = p.at("dsy")
    p.copy("vector", wdkvT_bf, wdkvT_sb, waits=lded, inc="v")
    prep = p.at("v")

    gm = gmisc[:, 0:4]; gmn = gmisc[:, 4:8]; gsc = gmisc[:, 8:40]
    ge = gmisc[:, 40:72]; gs_ = gmisc[:, 72:76]; gw_ = gmisc[:, 76:80]
    ps_g = ps512[0]
    ps_ckv = ps512[1]
    ps_t = ps512[2]

    for ts in range(NST):
        x_of_chunk = lambda c, _ts=ts: x_sb[:, c * TC + _ts * 512: c * TC + (_ts + 1) * 512]

        def xn_cb(c, xf, waits, _ts=ts):
            dst_f = xfs[:, c * 512:(c + 1) * 512]
            dst_b = xn_sb[:, c * TC + _ts * 512: c * TC + (_ts + 1) * 512]
            p.copy("vector", dst_f, xf, waits=p.at("t"), inc="v")
            p.copy("vector", dst_b, xf, inc="v")

        emit_ln_strip(p, B, x_of_chunk, ts, xn_cb, x_ready=lded)
        xn_v = p.at("v")

        for tb in range(4):
            pg = ps_g[:, tb * 8:(tb + 1) * 8]
            for c in range(NCH):
                xf = xfs[:, c * 512 + tb * 128: c * 512 + (tb + 1) * 128]
                wgc = wg_sb[:, c * 8:(c + 1) * 8]
                p.mm(pg, xf, wgc, c == 0, False,
                     waits=(xn_v + p.at("v", "s")) if c == 0 else ())
            p.mm(pg, B.ones_r[:1, :], bg_sb, False, True, waits=lded, inc="t")
        gmm = p.at("t")
        for tb in range(4):
            pg = ps_g[:, tb * 8:(tb + 1) * 8]
            m_ = gm[:, tb:tb + 1]; mn_ = gmn[:, tb:tb + 1]
            sg = gsc[:, tb * 8:(tb + 1) * 8]
            ee = ge[:, tb * 8:(tb + 1) * 8]
            ss = gs_[:, tb:tb + 1]; ww = gw_[:, tb:tb + 1]
            mwt = mw_sb[:, tb * 8:(tb + 1) * 8]
            p.red(m_, pg, ALU.max, waits=gmm, inc="v")
            p.ts("vector", mn_, m_, -1.0, None, ALU.mult, inc="v")
            p.act(sg, pg, AF.Sign, bias=m_, scale=-1.0, waits=[("v", p.n("v"))], inc="s")
            p.act(ee, pg, AF.Exp, bias=mn_, scale=1.0, inc="s")
            p.red(ss, ee, ALU.add, negate=True, waits=[("s", p.n("s"))])
            p.recip(ww, ss)
            p.ts("vector", mwt, sg, 1.0, ww, ALU.subtract, ALU.mult, inc="v")
        for tb in range(4):
            p.trans(ps_t[0:8, 0:128], mw_sb[:, tb * 8:(tb + 1) * 8], id_s[:, :],
                    waits=p.at("v"), inc="t")
            p.copy("vector", mwT_s[0:8, (ts * 4 + tb) * 128:(ts * 4 + tb + 1) * 128],
                   ps_t[0:8, 0:128], waits=p.at("t"))
        for c in range(NCH):
            xb = xn_sb[:, c * TC + ts * 512: c * TC + (ts + 1) * 512]
            wc = wdkvT_bf[:, c * 128:(c + 1) * 128]
            p.mm(ps_ckv[:, :], wc, xb, c == 0, c == NCH - 1,
                 waits=(prep + xn_v + p.at("v")) if c == 0 else (),
                 inc="t" if c == NCH - 1 else None)
        p.copy("vector", ckv_sb[:, ts * 512:(ts + 1) * 512], ps_ckv[:, :],
               waits=p.at("t"), inc="v")
        p.dma("gpsimd", ckv_l[:, ts * 512:(ts + 1) * 512],
              ckv_sb[:, ts * 512:(ts + 1) * 512], waits=[("v", p.n("v"))])
        for tb in range(4):
            tcol = ts * 512 + tb * 128
            for c in range(NCH):
                p.mm(ps_t[:, 0:128], xn_sb[:, c * TC + tcol: c * TC + tcol + 128],
                     wdkvT_bf[:, c * 128:(c + 1) * 128], c == 0, c == NCH - 1,
                     waits=(xn_v + p.at("v")) if c == 0 else (),
                     inc="t" if c == NCH - 1 else None)
            p.copy("vector", ckvT_sb[:, (ts * 4 + tb) * 128:(ts * 4 + tb + 1) * 128],
                   ps_t[:, 0:128], waits=p.at("t"), inc="v")
            p.dma("gpsimd", ckvT_l[tcol:tcol + 128, :],
                  ckvT_sb[:, (ts * 4 + tb) * 128:(ts * 4 + tb + 1) * 128],
                  waits=[("v", p.n("v"))])

    p.cc("AllGather", ALU.bypass, [ckv_l], [ckv_g], waits=p.at("dgp"))
    p.cc("AllGather", ALU.bypass, [ckvT_l], [ckvT_g], waits=p.at("dgp"))
    cc_ckv = p.n("cc")

    p.barrier()
    ar = Arena()
    xn_sb, ckv_sb, ckvT_sb, mwT_s = carve_resAB(ar)
    smal = ar.bf16(15360)
    s1t = ar.bf16(8192)
    ql_s = ar.bf16(8192)
    ckv_fm = ar.bf16(8192)
    ckv_tm = ar.bf16(8192)
    x1a_s = ar.f32(4096)
    dmask_s = ar.f32(2048)
    bc_s = ar.f32(512)
    probs = ar.bf16(1536)
    stage = ar.f32(2048)
    invs = ar.f32(512, rows=1)
    mb_s = ar.f32(NST * NKB)
    qa_bf = smal[:, 6144:8192]
    wdqT_bf = smal[:, 0:1024]; wuk_bf = smal[:, 1024:2048]; wuv_bf = smal[:, 2048:3072]
    wuqT_bf = smal[:, 3072:4096]; wdq_bf = smal[:, 4096:5120]; v0_bf = smal[:, 5120:6144]
    m_bf = smal[:, 8192:10240]
    wuqr_bf = smal[0:64, 10240:12288]; p64 = smal[0:64, 12288:14336]; pstage = smal[:, 14336:14464]
    p.dma("sync", mb_s[:, :], mb_in)

    ps_sc = [ps512[0], ps512[1], ps512[5]]
    ps_ctx = ps512[2]
    ps_bc = ps512[3]
    ps_y = ps512[4]
    ps_sum = ps_r1a

    p.dma("sync", dmask_s, dmaskf_in)

    def ld1(dst, src2d, rows=128):
        w = src2d.shape[-1] if src2d.ndim == 2 else src2d.shape[-1] * src2d.shape[-2]
        p.dma("sync", stage[0:rows, 0:w] if src2d.ndim == 2 else
              stage[0:rows, 0:w].rearrange("p (c l) -> p c l", c=src2d.shape[1]),
              src2d, waits=p.at("v"))
        p.copy("vector", dst, stage[0:rows, 0:w], waits=p.at("dsy"))

    ld1(wdqT_bf, wdqT_in.rearrange("(c p) l -> p c l", p=128))
    ld1(wuk_bf, wuk_in.rearrange("(c p) l -> p c l", p=128))
    ld1(wuv_bf, wuv_in.rearrange("(c p) l -> p c l", p=128))
    ld1(wuqT_bf, wuqT_in)
    ld1(wdq_bf, wdq_in)
    ld1(wuqr_bf, wuqr_in, rows=64)
    casts_done = p.at("v")

    for bc in range(8):
        for n in range(2):
            p.mm(ps_sc[n][:, :], wuqT_bf[:, bc * 128:(bc + 1) * 128], wdq_bf[:, n * 512:(n + 1) * 512],
                 True, True, waits=list(casts_done) + [("v", p.n("v"))], inc="t")
            p.copy("vector", s1t[:, bc * 1024 + n * 512: bc * 1024 + (n + 1) * 512], ps_sc[n][:, :],
                   waits=p.at("t"))
    for ac in range(8):
        for bc in range(8):
            p.mm(ps_ctx[:, :128], s1t[:, bc * 1024 + ac * 128: bc * 1024 + (ac + 1) * 128],
                 wuk_bf[:, bc * 128:(bc + 1) * 128], bc == 0, bc == 7,
                 waits=p.at("v") if bc == 0 else (), inc="t" if bc == 7 else None)
        p.copy("vector", pstage[:, 0:128], ps_ctx[:, :128], waits=p.at("t", "dgp"))
        for hh in range(2):
            h = 2 * ac + hh
            p.dma("gpsimd", p64[0:64, h * 128:(h + 1) * 128],
                  pstage[hh * 64:(hh + 1) * 64, 0:128], waits=p.at("v"))
    mready = p.at("dgp", "v")
    for h in range(NH):
        p.mm(ps_bc[:, :128], wuqr_bf[0:64, h * 128:(h + 1) * 128], p64[0:64, h * 128:(h + 1) * 128],
             True, True, waits=list(mready) + [("v", p.n("v"))], inc="t")
        p.copy("vector", m_bf[:, h * 128:(h + 1) * 128], ps_bc[:, :128], waits=p.at("t"))
    for n in range(4):
        p.dma("sync", stage[:, :].rearrange("p (c d) -> p c d", c=NCH),
              woT_in.rearrange("(c p) d -> p c d", p=128)[:, :, n * 256:(n + 1) * 256],
              waits=p.at("v"))
        p.copy("vector", ql_s[:, 0:2048], stage[:, :], waits=p.at("dsy"))
        for cc2 in range(8):
            p.mm(ps_ctx[:, 0:256], wuv_bf[:, cc2 * 128:(cc2 + 1) * 128], ql_s[:, cc2 * 256:(cc2 + 1) * 256],
                 cc2 == 0, cc2 == 7, waits=p.at("v") if cc2 == 0 else (), inc="t" if cc2 == 7 else None)
        p.copy("vector", v0_bf[:, n * 256:(n + 1) * 256], ps_ctx[:, 0:256], waits=p.at("t"))
    for ts in range(NST):
        for c in range(NCH):
            p.mm(ps_ctx[:, :], wdqT_bf[:, c * 128:(c + 1) * 128],
                 xn_sb[:, c * TC + ts * 512: c * TC + (ts + 1) * 512], c == 0, c == NCH - 1,
                 waits=p.at("v", "dsy") if c == 0 else (), inc="t" if c == NCH - 1 else None)
        p.copy("vector", qa_bf[:, ts * 512:(ts + 1) * 512], ps_ctx[:, :], waits=p.at("t"))
    p0_done = p.at("v")
    p.nop("sync", waits=[("cc", cc_ckv)])

    for ts in range(NST):
        b = 0 if ts < 2 else 1
        sl = ts
        for r in range(8):
            kbslot = r * 8 if b == 0 else (7 - r) * 8
            p.dma("sync", ckv_fm[:, kbslot * 128:(kbslot + 8) * 128],
                  ckv_g.rearrange("(r p) t -> p r t", p=128)[:, r, (0 if b == 0 else 1024):(1024 if b == 0 else 2048)],
                  waits=p.at("t", "s") if r == 0 else ())
        for r in range(8):
            kbslot = r * 8 if b == 0 else (7 - r) * 8
            p.dma("sync", ckv_tm[:, kbslot * 128:(kbslot + 8) * 128].rearrange("p (kb l) -> p kb l", l=128),
                  ckvT_g.rearrange("(r t) l -> r t l", t=2048)[r, (0 if b == 0 else 1024):(1024 if b == 0 else 2048), :]
                      .rearrange("(kb p) l -> p kb l", p=128))
        for h in range(NH):
            p.mm(ps_bc[:, :], m_bf[:, h * 128:(h + 1) * 128], qa_bf[:, ts * 512:(ts + 1) * 512],
                 True, True, waits=p.at("v") + (p0_done if (ts == 0 and h == 0) else []), inc="t")
            p.copy("vector", ql_s[:, h * 512:(h + 1) * 512], ps_bc[:, :], waits=p.at("t"))
        p.dma("gpsimd", x1a_s.rearrange("p (c t) -> p c t", t=512),
              x_in[:, ts * 512:(ts + 1) * 512].rearrange("(c p) t -> p c t", p=128),
              waits=p.at("v", "dgp"))
        kv_ld = p.at("dsy")
        ql_v = p.at("v")
        xi_ld = p.at("dgp")
        NKB2 = NKB + 4

        def kb_ops(kb, _ts=ts, _sl=sl):
            if kb < NKB:
                return (ckv_fm[:, kb * 128:(kb + 1) * 128],
                        ckv_tm[:, kb * 128:(kb + 1) * 128],
                        mb_s[:, _ts * NKB + kb: _ts * NKB + kb + 1], None)
            d = kb - NKB
            return (ckv_sb[:, _sl * 512 + d * 128: _sl * 512 + (d + 1) * 128],
                    ckvT_sb[:, (_sl * 4 + d) * 128: (_sl * 4 + d + 1) * 128],
                    float(EXPB), d)

        NSL = 3
        for h in range(NH):
            qlh = ql_s[:, h * 512:(h + 1) * 512]
            score_t = {}
            for kb in range(NSL):
                fmop, tmop, bias, dmi = kb_ops(kb)
                p.mm(ps_sc[kb][:, :], fmop, qlh, True, True,
                     waits=list(kv_ld) + list(ql_v) + [("s", p.n("s")), ("v", p.n("v"))], inc="t")
                score_t[kb] = p.n("t")
            hd_v = p.at("v")
            for kb in range(NKB2):
                slot = kb % NSL
                pslot = kb % 3
                fmop, tmop, bias, dmi = kb_ops(kb)
                if dmi is not None:
                    p.tt("vector", ps_sc[slot][:, :], ps_sc[slot][:, :],
                         dmask_s[:, dmi * 512:(dmi + 1) * 512], ALU.add,
                         waits=[("t", score_t[kb])])
                    ewait = [("v", p.n("v"))]
                else:
                    ewait = [("t", score_t[kb])]
                p.act(probs[:, pslot * 512:(pslot + 1) * 512], ps_sc[slot][:, :], AF.Exp,
                      bias=bias, scale=float(SC8), waits=ewait)
                ecnt = p.n("s")
                p.mm(ps_ctx[:, :], tmop, probs[:, pslot * 512:(pslot + 1) * 512],
                     kb == 0, kb == NKB2 - 1,
                     waits=[("s", ecnt)] + (hd_v if kb == 0 else []))
                p.mm(ps_sum[:, :], ones_cb[:, :], probs[:, pslot * 512:(pslot + 1) * 512],
                     kb == 0, kb == NKB2 - 1,
                     waits=(hd_v if kb == 0 else ()), inc="t")
                if kb + NSL < NKB2:
                    fmop2, _, _, _ = kb_ops(kb + NSL)
                    p.mm(ps_sc[slot][:, :], fmop2, qlh, True, True,
                         waits=[("s", ecnt)], inc="t")
                    score_t[kb + NSL] = p.n("t")
            p.recip(invs, ps_sum[:, :], waits=p.at("t"))
            p.mm(ps_bc[:, :], ones_rf[:, :], invs, True, True, waits=p.at("v"), inc="t")
            p.copy("vector", bc_s, ps_bc[:, :], waits=p.at("t"))
            p.tt("vector", probs[:, 0:512], ps_ctx[:, :], bc_s, ALU.mult)
            p.mm(ps_y[0:64, :], v0_bf[:, h * 64:(h + 1) * 64], probs[:, 0:512], True, True,
                 waits=p.at("v") + list(xi_ld), inc="t")
            dst = x1a_s[(h % 2) * 64:(h % 2) * 64 + 64, (h // 2) * 512:(h // 2) * 512 + 512]
            p.tt("vector", dst, dst, ps_y[0:64, :], ALU.add, waits=p.at("t"))
        p.dma("gpsimd", x1a_d.rearrange("(c p) t -> p c t", p=128)[:, :, ts * 512:(ts + 1) * 512],
              x1a_s.rearrange("p (c t) -> p c t", t=512), waits=p.at("v"))

    p.barrier()
    ar = Arena()
    xn_sb, ckv_sb, ckvT_sb, mwT_s = carve_resAB(ar)
    w1h_bf = ar.bf16(16384)
    w2h_bf = ar.bf16(16384)
    h_sb = ar.bf16(8192)
    yev = ar.f32(1024)
    mwbc = ar.f32(512)
    b1c_s = ar.f32(256)
    b2c_s = ar.f32(64)
    sel_s = ar.f32(1024, rows=8)
    stage = ar.f32(4096)
    p.dma("sync", sel_s[:, :], sel8)
    p.dma("sync", stage[:, 0:256], b1c_in)
    p.dma("sync", stage[:, 256:320], b2c_in)
    p.copy("vector", b1c_s, stage[:, 0:256], waits=p.at("dsy"))
    p.copy("vector", b2c_s, stage[:, 256:320])

    ps_h = [ps512[0], ps512[1]]
    ps_o = [ps512[2], ps512[3]]
    ps_b2 = ps512[4]
    moe_acc_cnt = {}

    for e in range(8):
        for half in range(2):
            for dk in range(8):
                p.dma("sync", stage[:, (dk % 2) * 2048:(dk % 2) * 2048 + 2048],
                      w1e_in[(e * 8 + dk) * 128:(e * 8 + dk + 1) * 128, half * 2048:(half + 1) * 2048],
                      waits=[("v", p.n("v"))])
                p.copy("vector", w1h_bf[:, dk * 2048:(dk + 1) * 2048],
                       stage[:, (dk % 2) * 2048:(dk % 2) * 2048 + 2048],
                       waits=p.at("dsy") + [("t", p.n("t"))])
            for hc in range(16):
                p.dma("sync", stage[:, (hc % 2) * 1024:(hc % 2) * 1024 + 1024],
                      w2e_in[(e * 32 + half * 16 + hc) * 128:(e * 32 + half * 16 + hc + 1) * 128, :],
                      waits=[("v", p.n("v"))])
                p.copy("vector", w2h_bf[:, hc * 1024:(hc + 1) * 1024],
                       stage[:, (hc % 2) * 1024:(hc % 2) * 1024 + 1024],
                       waits=p.at("dsy") + [("t", p.n("t"))])
            wcast = p.at("v")
            for ts2 in range(NST):
                p.mm(ps_b2[:, :], sel_s[0:8, e * 128:(e + 1) * 128],
                     mwT_s[0:8, ts2 * 512:(ts2 + 1) * 512],
                     True, True, waits=p.at("v") + [("s", p.n("s"))], inc="t")
                p.copy("vector", mwbc, ps_b2[:, :], waits=p.at("t"))
                mw_v = p.at("v")
                for ht in range(16):
                    psh = ps_h[ht % 2]
                    for dk in range(8):
                        p.mm(psh[:, :], w1h_bf[:, dk * 2048 + ht * 128: dk * 2048 + (ht + 1) * 128],
                             xn_sb[:, dk * TC + ts2 * 512:(dk * TC) + (ts2 + 1) * 512],
                             dk == 0, dk == 7,
                             waits=(list(wcast) + [("s", p.n("s"))]) if dk == 0 else (),
                             inc="t" if dk == 7 else None)
                    p.act(h_sb[:, ht * 512:(ht + 1) * 512], psh[:, :], AF.Silu,
                          bias=b1c_s[:, e * 32 + half * 16 + ht: e * 32 + half * 16 + ht + 1],
                          waits=p.at("t"))
                h_done = p.at("s")
                for dc in range(8):
                    pso = ps_o[dc % 2]
                    for ht in range(16):
                        p.mm(pso[:, :], w2h_bf[:, ht * 1024 + dc * 128: ht * 1024 + dc * 128 + 128],
                             h_sb[:, ht * 512:(ht + 1) * 512], ht == 0, ht == 15,
                             waits=(list(h_done) + [("v", p.n("v"))]) if ht == 0 else (),
                             inc="t" if ht == 15 else None)
                    ysl = yev[:, (dc % 2) * 512:(dc % 2) * 512 + 512]
                    if half == 0:
                        p.tt("vector", ysl, pso[:, :], mwbc, ALU.mult,
                             waits=p.at("t") + mw_v + [("dgp", p.n("dgp"))])
                    else:
                        p.stt("vector", ysl, pso[:, :], b2c_s[:, e * 8 + dc: e * 8 + dc + 1], mwbc,
                              ALU.add, ALU.mult, waits=p.at("t") + mw_v + [("dgp", p.n("dgp"))])
                    key = (dc, ts2)
                    prior = moe_acc_cnt.get(key, 0)
                    p.dma("gpsimd", x1a_d[dc * 128:(dc + 1) * 128, ts2 * 512:(ts2 + 1) * 512],
                          ysl, waits=[("v", p.n("v")), ("dgp", prior)], accum=True)
                    moe_acc_cnt[key] = p.n("dgp")

    p.barrier()
    if DEBUG:
        p.dma("sync", dbg["d_x1a"], x1a_d)
        p.nop("sync", waits=p.at("dsy"))
    ar = Arena()
    ln_sq, ln_stat, ln_xf = carve_ln(ar)
    B = LNBufs(lnc, lnr, ln_sq, ln_stat, ln_xf, (ps_r1a, ps_r1b, ps512[4], ps512[5]))
    x_sb = ar.f32(16384)
    wq_bf = ar.bf16(8192)
    wk_bf = ar.bf16(8192)
    wv_bf = ar.bf16(8192)
    sin_sb = ar.f32(2048, rows=64)
    cos_sb = ar.f32(2048, rows=64)
    scr1 = ar.f32(512, rows=64); scr2 = ar.f32(512, rows=64)
    ob1 = ar.bf16(512, rows=64); ob2 = ar.bf16(512, rows=64)
    vstage = ar.bf16(1040)
    gm_acc = ar.f32(16)
    gm_scr = ar.f32(1)
    xn_strip = ar.bf16(4096)
    stage = ar.f32(4096)

    p.dma("sync", x_sb.rearrange("p (c t) -> p c t", c=NCH), x1a_d.rearrange("(c p) t -> p c t", p=128))
    xld = p.at("dsy")
    p.dma("scalar", sin_sb, sin_in)
    p.dma("scalar", cos_sb, cos_in)
    for w_in, w_bf in ((wq1_in, wq_bf), (wk1_in, wk_bf), (wv1_in, wv_bf)):
        for half in range(2):
            p.dma("scalar", stage[:, :].rearrange("p (c e) -> p c e", c=NCH),
                  w_in.rearrange("(c p) e -> p c e", p=128)[:, :, half * 512:(half + 1) * 512],
                  waits=[("v", p.n("v"))])
            for c in range(NCH):
                p.copy("vector", w_bf[:, c * D + half * 512: c * D + (half + 1) * 512],
                       stage[:, c * 512:(c + 1) * 512], waits=p.at("dsc"))
    wv_done = p.at("v")
    p.memset("vector", gm_acc[:, :], 0.0)
    for h in range(NH):
        p.memset("vector", vstage[:, h * 65 + 64: h * 65 + 65], 1.0)

    ps_q = ps512[0]
    ps_v0 = ps512[1]
    ps_v1 = ps512[2]

    for ts in range(NST):
        x_of_chunk = lambda c, _ts=ts: x_sb[:, c * TC + _ts * 512: c * TC + (_ts + 1) * 512]
        bhalf = 0 if ts < 2 else 1

        def xn_cb(c, xf, waits, _ts=ts, _b=bhalf):
            dst_b = xn_strip[:, c * 512:(c + 1) * 512]
            p.copy("vector", dst_b, xf, waits=[("t", p.n("t")), ("dgp", p.n("dgp"))])
            p.red(gm_scr[:, :], xf, ALU.add)
            acc = gm_acc[:, _b * 8 + c: _b * 8 + c + 1]
            p.tt("vector", acc, acc, gm_scr[:, :], ALU.add)
            p.dma("gpsimd", xn1_d[c * 128:(c + 1) * 128, _ts * 512:(_ts + 1) * 512],
                  dst_b, waits=[("v", p.n("v"))])

        emit_ln_strip(p, B, x_of_chunk, ts, xn_cb, x_ready=xld)
        xn_v = p.at("v")
        sslice = sin_sb[:, ts * 512:(ts + 1) * 512]
        cslice = cos_sb[:, ts * 512:(ts + 1) * 512]
        for w_bf, out_ap in ((wq_bf, q1_d), (wk_bf, k1_l)):
            for c in range(NCH):
                for dk in range(NCH):
                    p.mm(ps_q[:, :], w_bf[:, dk * D + c * 128:(dk * D) + (c + 1) * 128],
                         xn_strip[:, dk * 512:(dk + 1) * 512],
                         dk == 0, dk == NCH - 1,
                         waits=(list(wv_done) + list(xn_v) + [("v", p.n("v")), ("dgp", p.n("dgp"))]) if dk == 0 else (),
                         inc="t" if dk == NCH - 1 else None)
                mmw = p.at("t")
                x1b = ps_q[0:64, :]
                x2b = ps_q[64:128, :]
                p.tt("vector", scr1[:, :], x1b, cslice, ALU.mult, waits=list(mmw) + p.at("dsc"))
                p.tt("vector", scr2[:, :], x2b, sslice, ALU.mult)
                p.tt("vector", ob1[:, :], scr1[:, :], scr2[:, :], ALU.subtract)
                p.tt("vector", scr1[:, :], x1b, sslice, ALU.mult)
                p.tt("vector", scr2[:, :], x2b, cslice, ALU.mult)
                p.tt("vector", ob2[:, :], scr1[:, :], scr2[:, :], ALU.add)
                obv = p.at("v")
                hA, hB = 2 * c, 2 * c + 1
                p.dma("gpsimd", out_ap[0:32, hA * TC + ts * 512: hA * TC + (ts + 1) * 512],
                      ob1[0:32, :], waits=obv)
                p.dma("gpsimd", out_ap[32:64, hA * TC + ts * 512: hA * TC + (ts + 1) * 512],
                      ob2[0:32, :])
                p.dma("gpsimd", out_ap[0:32, hB * TC + ts * 512: hB * TC + (ts + 1) * 512],
                      ob1[32:64, :])
                p.dma("gpsimd", out_ap[32:64, hB * TC + ts * 512: hB * TC + (ts + 1) * 512],
                      ob2[32:64, :])
        for tb in range(4):
            tcol = ts * 512 + tb * 128
            for half, psv in ((0, ps_v0), (1, ps_v1)):
                for dk in range(NCH):
                    p.mm(psv[:, :], xn_strip[:, dk * 512 + tb * 128: dk * 512 + tb * 128 + 128],
                         wv_bf[:, dk * D + half * 512:(dk * D) + half * 512 + 512],
                         dk == 0, dk == NCH - 1,
                         waits=(list(xn_v) + [("v", p.n("v")), ("dgp", p.n("dgp"))]) if dk == 0 else (),
                         inc="t" if dk == NCH - 1 else None)
            vmw = p.at("t")
            for h in range(NH):
                psv = ps_v0 if h < 8 else ps_v1
                p.copy("vector", vstage[:, h * 65: h * 65 + 64],
                       psv[:, (h % 8) * 64:(h % 8) * 64 + 64], waits=vmw + [("dgp", p.n("dgp"))])
            p.dma("gpsimd", v1_l[tcol:tcol + 128, :], vstage[:, :], waits=[("v", p.n("v"))])
    p.dma("gpsimd", gmp_l, gm_acc[:, :], waits=p.at("v"))
    p.cc("AllGather", ALU.bypass, [k1_l], [k1_g], waits=p.at("dgp"))
    p.cc("AllGather", ALU.bypass, [v1_l], [v1_g], waits=p.at("dgp"))
    p.cc("AllReduce", ALU.add, [gmp_l], [gmp_g], waits=p.at("dgp"))
    cc_k1 = p.n("cc")

    p.barrier()
    if DEBUG:
        p.dma("sync", dbg["d_xn1"], xn1_d)
        p.nop("sync", waits=p.at("dsy"))
    ar = Arena()
    kbuf = ar.bf16(8192)
    wo1_bf = ar.bf16(8192)
    q1s = ar.bf16(8192)
    vbuf = ar.bf16(8320)
    acc = ar.f32(8192)
    a1s = ar.f32(4096)
    ctxbf = ar.bf16(4096)
    probs = ar.bf16(2048)
    dmask_sD = ar.bf16(2048)
    mb_s = ar.f32(NST * NKB)
    stage = ar.f32(2048)
    bc_s = ar.f32(512)
    invs = ar.f32(512, rows=1)
    p.dma("sync", mb_s[:, :], mb_in)

    ps_sc = [ps512[0], ps512[1], ps512[4]]
    ps_pv = ps512[2]
    ps_bc = ps512[3]
    ps_w = ps512[4]
    ps_w2 = ps512[5]
    ps_r1 = ps_r1a

    p.dma("sync", dmask_sD, dmaskb_in)
    src3 = wo1T_in.rearrange("(c p) e -> p c e", p=128)
    for c0 in range(0, NCH, 2):
        p.dma("scalar", stage[:, 0:2 * D].rearrange("p (c e) -> p c e", e=D),
              src3[:, c0:c0 + 2, :], waits=[("v", p.n("v"))])
        p.copy("vector", wo1_bf[:, c0 * D:(c0 + 2) * D], stage[:, 0:2 * D], waits=p.at("dsc"))
    wo1_v = p.at("v")
    p.nop("sync", waits=[("cc", cc_k1)])

    for ts in range(NST):
        b = 0 if ts < 2 else 1
        sl = ts
        p.dma("scalar", q1s[0:64, :].rearrange("q (h t) -> q h t", t=512),
              q1_d.rearrange("q (h t) -> q h t", t=TC)[:, :, ts * 512:(ts + 1) * 512],
              waits=[("t", p.n("t"))] + (list(wo1_v) if ts == 0 else []))
        p.memset("vector", acc[:, :], 0.0, waits=p.at("t"))
        qld = p.at("dsc")
        NGRP = 9
        for grp in range(NGRP):
            if grp < 8:
                r = grp if b == 0 else 7 - grp
                colr = (0, 1024) if b == 0 else (1024, 2048)
                p.dma("sync", vbuf[:, :].rearrange("p (kb c) -> p kb c", c=1040),
                      v1_g[r * TC + colr[0]: r * TC + colr[1], :].rearrange("(kb p) c -> p kb c", p=128),
                      waits=[("t", p.n("t")), ("s", p.n("s"))])
                nkb_g = 8
                bias_of = lambda kb, _ts=ts, _g=grp: mb_s[:, _ts * NKB + _g * 8 + kb: _ts * NKB + _g * 8 + kb + 1]
                dm_of = lambda kb: None
                kwid = 1024
            else:
                p.dma("sync", vbuf[:, 0:4 * 1040].rearrange("p (kb c) -> p kb c", c=1040),
                      v1_l[sl * 512:(sl + 1) * 512, :].rearrange("(kb p) c -> p kb c", p=128),
                      waits=[("t", p.n("t")), ("s", p.n("s"))])
                nkb_g = 4
                bias_of = lambda kb: float(EXPB)
                dm_of = lambda kb: kb
                kwid = 512
            for octt in range(2):
                if grp < 8:
                    p.dma("sync", kbuf[0:64, :].rearrange("q (h t) -> q h t", t=1024),
                          k1_g.rearrange("(r q) ht -> r q ht", q=64)[r]
                              .rearrange("q (h t) -> q h t", t=TC)
                              [:, octt * 8:(octt + 1) * 8, colr[0]:colr[1]],
                          waits=[("t", p.n("t")), ("s", p.n("s"))])
                else:
                    p.dma("sync", kbuf[0:64, 0:4096].rearrange("q (h t) -> q h t", t=512),
                          k1_l.rearrange("q (h t) -> q h t", t=TC)
                              [:, octt * 8:(octt + 1) * 8, sl * 512:(sl + 1) * 512],
                          waits=[("t", p.n("t")), ("s", p.n("s"))])
                gld = p.at("dsy")
                NSL = 3
                for h in range(octt * 8, octt * 8 + 8):
                    qh = q1s[0:64, h * 512:(h + 1) * 512]
                    score_t = {}
                    hd_v = p.at("v")
                    for kb in range(min(NSL, nkb_g)):
                        p.mm(ps_sc[kb][:, :], kbuf[0:64, (h % 8) * kwid + kb * 128:
                                                   (h % 8) * kwid + (kb + 1) * 128],
                             qh, True, True,
                             waits=list(gld) + list(qld) + [("s", p.n("s")), ("v", p.n("v"))], inc="t")
                        score_t[kb] = p.n("t")
                    for kb in range(nkb_g):
                        sl2 = kb % NSL
                        pslt = kb % 3
                        dmi = dm_of(kb)
                        if dmi is not None:
                            p.tt("vector", ps_sc[sl2][:, :], ps_sc[sl2][:, :],
                                 dmask_sD[:, dmi * 512:(dmi + 1) * 512], ALU.add,
                                 waits=[("t", score_t[kb])])
                            ewait = [("v", p.n("v"))]
                        else:
                            ewait = [("t", score_t[kb])]
                        p.act(probs[:, pslt * 512:(pslt + 1) * 512], ps_sc[sl2][:, :], AF.Exp,
                              bias=bias_of(kb), scale=float(SC8), waits=ewait)
                        ecnt = p.n("s")
                        p.mm(ps_pv[0:65, :], vbuf[:, kb * 1040 + h * 65: kb * 1040 + h * 65 + 65],
                             probs[:, pslt * 512:(pslt + 1) * 512], kb == 0, kb == nkb_g - 1,
                             waits=[("s", ecnt)] + (hd_v if kb == 0 else []),
                             inc="t" if kb == nkb_g - 1 else None)
                        if kb + NSL < nkb_g:
                            p.mm(ps_sc[sl2][:, :], kbuf[0:64, (h % 8) * kwid + (kb + NSL) * 128:
                                                        (h % 8) * kwid + (kb + NSL + 1) * 128],
                                 qh, True, True, waits=[("s", ecnt)], inc="t")
                            score_t[kb + NSL] = p.n("t")
                    asl = acc[0:65, h * 512:(h + 1) * 512]
                    p.tt("vector", asl, asl, ps_pv[0:65, :], ALU.add, waits=p.at("t"))
        for h in range(NH):
            p.recip(invs, acc[64:65, h * 512:(h + 1) * 512])
            p.mm(ps_bc[:, :], ones_rf[:, :], invs, True, True, waits=p.at("v"), inc="t")
            p.copy("vector", bc_s, ps_bc[:, :], waits=p.at("t"))
            p.tt("vector", ctxbf[(h % 2) * 64:(h % 2) * 64 + 64, (h // 2) * 512:(h // 2) * 512 + 512],
                 acc[0:64, h * 512:(h + 1) * 512], bc_s[0:64, :], ALU.mult)
        ctx_v = p.at("v")
        for oc in range(NCH):
            for dk in range(NCH):
                p.mm(ps_w[:, :], wo1_bf[:, dk * D + oc * 128: dk * D + (oc + 1) * 128],
                     ctxbf[:, dk * 512:(dk + 1) * 512], dk == 0, dk == NCH - 1,
                     waits=(list(wo1_v) + list(ctx_v) + [("v", p.n("v")), ("dgp", p.n("dgp"))]) if dk == 0 else (),
                     inc="t" if dk == NCH - 1 else None)
            p.copy("vector", a1s[:, oc * 512:(oc + 1) * 512], ps_w[:, :], waits=p.at("t"))
        p.dma("gpsimd", attn1_scr.rearrange("(c p) t -> p c t", p=128)[:, :, ts * 512:(ts + 1) * 512],
              a1s[:, :].rearrange("p (c t) -> p c t", t=512), waits=p.at("v"))

    pass1_done = p.at("t", "v", "s", "dgp")

    ar = Arena()
    ln_sq, ln_stat, ln_xf = carve_ln(ar)
    B2 = LNBufs(lnc, lnr, ln_sq, ln_stat, ln_xf, (ps_r1a, ps_r1b, ps_bc, ps_pv))
    kbuf = ar.bf16(13320)
    wo1_bf = ar.bf16(8192)
    vbuf = ar.bf16(8320)
    q1s = ar.bf16(8192)
    acc = ar.f32(8192)
    ctxbf = ar.bf16(4096)
    a1s = ar.f32(4096)
    vstage = ar.bf16(1040)
    wtile = ar.bf16(1024)
    stage = ar.f32(2048)
    fsml = ar.f32(640)
    frow = ar.f32(3104, rows=1)
    bv2_bf = ar.bf16(1024, rows=1)
    bc_s = fsml[:, 0:512]
    gm = fsml[:, 512:528]
    biases = fsml[:, 528:592]
    gfv = fsml[:1, 592:594]; Ab = fsml[:1, 594:596]
    scoef = fsml[:1, 596:597]; t1r = fsml[:1, 598:600]
    invs = frow[:1, 0:512]
    bv2row = frow[:1, 512:1536]
    tf_ = frow[:1, 1536:2048]; crow = frow[:1, 2048:2560]; crn = frow[:1, 2560:3072]
    cb1_s = biases[:, 0:8]; cb2_s = biases[:, 8:12]; cb3_s = biases[:, 12:14]; cb4_s = biases[:1, 14:15]
    gb1_s = biases[:, 16:20]; gb2_s = biases[:, 20:22]; gb3_s = biases[:1, 22:23]
    bq2_s = biases[:, 24:32]; bk2_s = biases[:, 32:40]; ar_s = biases[:1, 40:41]

    gmb = ctxbf[:, 0:16]
    h1g = ctxbf[:, 16:20]; h2g = ctxbf[:, 20:22]
    p.dma("sync", stage[:, 0:16], gmp_g, waits=pass1_done + [("cc", p.n("cc"))])
    p.copy("vector", gm, stage[:, 0:16], waits=p.at("dsy"))
    p.ts("vector", gmb, gm, 1.0 / TB, None, ALU.mult)

    cw1_bf = kbuf[:, 0:8192]
    cw2_bf = kbuf[:, 8192:12288]
    cw3_bf = kbuf[:, 12288:13312]
    cw4_bf = kbuf[:, 13312:13320]
    gw1_bf = kbuf[:, 0:4096]
    gw2_bf = kbuf[:, 4096:5120]
    gw3_bf = kbuf[:, 5120:5128]
    wk2_bf = wo1_bf[:, 0:8192]
    wv2_bf = vbuf[:, 0:8192]

    def ldw(dst, src2d, extra=()):
        src3_ = src2d.rearrange("(c p) e -> p c e", p=128)
        C, E = src3_.shape[1], src3_.shape[2]
        cpg = max(1, 2048 // E)
        for c0 in range(0, C, cpg):
            n = min(cpg, C - c0)
            p.dma("scalar", stage[:, 0:n * E].rearrange("p (c e) -> p c e", e=E),
                  src3_[:, c0:c0 + n, :], waits=[("v", p.n("v"))], slow=(E == 1))
            p.copy("vector", dst[:, c0 * E:(c0 + n) * E], stage[:, 0:n * E],
                   waits=p.at("dsc") + list(extra))

    ldw(gw1_bf, gw1_in)
    ldw(gw2_bf, gw2_in)
    ldw(gw3_bf, gw3_in)
    p.dma("scalar", stage[:, 0:8], cb1_in, waits=[("v", p.n("v"))])
    p.dma("scalar", stage[:, 8:12], cb2_in)
    p.dma("scalar", stage[:, 12:14], cb3_in)
    p.dma("scalar", stage[:1, 14:15], cb4_in[:1, :])
    p.dma("scalar", stage[:, 16:20], gb1_in)
    p.dma("scalar", stage[:, 20:22], gb2_in)
    p.dma("scalar", stage[:1, 22:23], gb3_in[:1, :])
    p.dma("scalar", stage[:, 24:32], bq2_in)
    p.dma("scalar", stage[:, 32:40], bk2_in)
    p.dma("scalar", stage[:1, 40:41], ar_in[:1, :])
    p.dma("scalar", stage[:1, 48:1072], bv2_in)
    p.copy("vector", biases[:, 0:48], stage[:, 0:48], waits=p.at("dsc"))
    p.copy("vector", bv2row, stage[:1, 48:1072])
    p.copy("vector", bv2_bf, bv2row)

    for b2 in range(2):
        for oc in range(4):
            for dk in range(8):
                p.mm(ps_w[:, 0:1], gw1_bf[:, dk * 512 + oc * 128: dk * 512 + (oc + 1) * 128],
                     gmb[:, b2 * 8 + dk: b2 * 8 + dk + 1], dk == 0, dk == 7,
                     waits=[("v", p.n("v")), ("s", p.n("s"))] if dk == 0 else (),
                     inc="t" if dk == 7 else None)
            p.act(h1g[:, oc:oc + 1], ps_w[:, 0:1], AF.Gelu, bias=gb1_s[:, oc:oc + 1], waits=p.at("t"))
        for oc in range(2):
            for dk in range(4):
                p.mm(ps_w[:, 0:1], gw2_bf[:, dk * 256 + oc * 128: dk * 256 + (oc + 1) * 128],
                     h1g[:, dk:dk + 1], dk == 0, dk == 3,
                     waits=[("s", p.n("s")), ("v", p.n("v"))] if dk == 0 else (), inc="t" if dk == 3 else None)
            p.act(h2g[:, oc:oc + 1], ps_w[:, 0:1], AF.Gelu, bias=gb2_s[:, oc:oc + 1], waits=p.at("t"))
        for dk in range(2):
            p.mm(ps_r1[:1, 0:1], gw3_bf[:, dk:dk + 1], h2g[:, dk:dk + 1],
                 dk == 0, dk == 1, waits=[("s", p.n("s")), ("v", p.n("v"))] if dk == 0 else (),
                 inc="t" if dk == 1 else None)
        p.act(gfv[:, b2:b2 + 1], ps_r1[:1, 0:1], AF.Sigmoid, bias=gb3_s, waits=p.at("t"))
    gfb = p.at("t", "s")
    ldw(cw1_bf, cw1_in, extra=gfb)
    ldw(cw2_bf, cw2_in)
    ldw(cw3_bf, cw3_in)
    ldw(cw4_bf, cw4_in)
    ldw(wk2_bf, wk2T_in)
    ldw(wv2_bf, wv2T_in)
    wld = p.at("v")
    p.ts("vector", t1r, gfv[:1, 0:2], 0.7, -0.5, ALU.mult, ALU.add, waits=p.at("s"))
    p.tt("vector", t1r, t1r, ar_s.broadcast_to([1, 2]), ALU.mult)
    p.ts("vector", Ab, t1r, 2.0, 0.1, ALU.mult, ALU.add)
    p.ts("vector", scoef, ar_s, 0.6, None, ALU.mult)
    gf_v = p.at("v")

    x2s = a1s
    outbf = q1s[:, 0:4096]
    h1b = q1s[:, 4096:8192]
    xn2s = q1s[:, 4096:8192]
    h2b = ctxbf[:, 0:2048]
    h3b = ctxbf[:, 2048:3072]
    cbp = stage[:, 0:512]
    cbm = stage[:, 512:1024]
    B2 = LNBufs(lnc, lnr, ln_sq, ln_stat, ln_xf,
                (ps_r1a, ps_r1b, ps_bc, ps_pv))
    for h in range(NH):
        p.memset("vector", vstage[:, h * 65 + 64: h * 65 + 65], 1.0)

    for ts in range(NST):
        bidx = 0 if ts < 2 else 1
        p.dma("sync", x2s.rearrange("p (c t) -> p c t", t=512),
              x1a_d[:, ts * 512:(ts + 1) * 512].rearrange("(c p) t -> p c t", p=128),
              waits=[("v", p.n("v")), ("dgp", p.n("dgp")), ("t", p.n("t")), ("s", p.n("s"))])
        p.dma("gpsimd", x2s.rearrange("p (c t) -> p c t", t=512),
              attn1_scr.rearrange("(c p) t -> p c t", p=128)[:, :, ts * 512:(ts + 1) * 512],
              waits=p.at("dsy"), accum=True)
        p.dma("scalar", outbf[:, :].rearrange("p (c t) -> p c t", t=512),
              xn1_d[:, ts * 512:(ts + 1) * 512].rearrange("(c p) t -> p c t", p=128),
              waits=[("t", p.n("t")), ("s", p.n("s")), ("v", p.n("v"))])
        xld2 = p.at("dsy", "dgp", "dsc")
        p.copy("vector", acc[:, 0:4096], outbf[:, :], waits=xld2)
        pp = 0
        for cyc in range(3):
            src_o = acc[:, pp * 4096:(pp + 1) * 4096]
            dst_o = acc[:, (1 - pp) * 4096:(2 - pp) * 4096]
            p.copy("vector", outbf[:, :], src_o)
            ob_v = p.at("v")
            for oc in range(NCH):
                for dk in range(NCH):
                    p.mm(ps_w[:, :], cw1_bf[:, dk * 1024 + oc * 128: dk * 1024 + (oc + 1) * 128],
                         outbf[:, dk * 512:(dk + 1) * 512], dk == 0, dk == 7,
                         waits=(list(ob_v) + list(wld) + [("s", p.n("s"))]) if dk == 0 else (),
                         inc="t" if dk == 7 else None)
                p.act(h1b[:, oc * 512:(oc + 1) * 512], ps_w[:, :], AF.Gelu,
                      bias=cb1_s[:, oc:oc + 1], waits=p.at("t"))
            for oc in range(4):
                for dk in range(NCH):
                    p.mm(ps_w2[:, :], cw2_bf[:, dk * 512 + oc * 128: dk * 512 + (oc + 1) * 128],
                         h1b[:, dk * 512:(dk + 1) * 512], dk == 0, dk == 7,
                         waits=[("s", p.n("s"))] if dk == 0 else (), inc="t" if dk == 7 else None)
                p.act(h2b[:, oc * 512:(oc + 1) * 512], ps_w2[:, :], AF.Gelu,
                      bias=cb2_s[:, oc:oc + 1], waits=p.at("t"))
            for oc in range(2):
                for dk in range(4):
                    p.mm(ps_w[:, :], cw3_bf[:, dk * 256 + oc * 128: dk * 256 + (oc + 1) * 128],
                         h2b[:, dk * 512:(dk + 1) * 512], dk == 0, dk == 3,
                         waits=[("s", p.n("s")), ("v", p.n("v"))] if dk == 0 else (), inc="t" if dk == 3 else None)
                p.act(h3b[:, oc * 512:(oc + 1) * 512], ps_w[:, :], AF.Gelu,
                      bias=cb3_s[:, oc:oc + 1], waits=p.at("t"))
            for dk in range(2):
                p.mm(ps_r1[:1, :], cw4_bf[:, dk:dk + 1], h3b[:, dk * 512:(dk + 1) * 512],
                     dk == 0, dk == 1, waits=[("s", p.n("s")), ("v", p.n("v"))] if dk == 0 else (),
                     inc="t" if dk == 1 else None)
            p.act(tf_, ps_r1[:1, :], AF.Sigmoid, bias=cb4_s, waits=p.at("t"))
            p.ts("vector", crow, tf_, scoef[:1, 0:1], Ab[:1, bidx:bidx + 1], ALU.mult, ALU.add,
                 waits=gf_v + [("s", p.n("s"))])
            p.ts("vector", crn, crow, -1.0, None, ALU.mult)
            p.mm(ps_bc[:, :], ones_rf[:, :], crow, True, True, waits=p.at("v"), inc="t")
            p.copy("vector", cbp, ps_bc[:, :], waits=p.at("t"))
            p.mm(ps_bc[:, :], ones_rf[:, :], crn, True, True, waits=p.at("v"), inc="t")
            p.copy("vector", cbm, ps_bc[:, :], waits=p.at("t"))
            for c in range(NCH):
                srcc, sgn = TWIST_MAP[c]
                cb = cbp if sgn > 0 else cbm
                p.tt("vector", B.xf[:, :], src_o[:, srcc * 512:(srcc + 1) * 512], cb, ALU.mult)
                p.tt("vector", dst_o[:, c * 512:(c + 1) * 512], src_o[:, c * 512:(c + 1) * 512],
                     B.xf[:, :], ALU.add)
            pp = 1 - pp
        ffn_o = acc[:, pp * 4096:(pp + 1) * 4096]
        for c in range(NCH):
            sl3 = x2s[:, c * 512:(c + 1) * 512]
            p.tt("vector", sl3, sl3, ffn_o[:, c * 512:(c + 1) * 512], ALU.add)
        x2_v = p.at("v")
        p.dma("gpsimd", x2_d.rearrange("(c p) t -> p c t", p=128)[:, :, ts * 512:(ts + 1) * 512],
              x2s.rearrange("p (c t) -> p c t", t=512), waits=x2_v)
        x_of_chunk = lambda c, _ts=ts: x2s[:, c * 512:(c + 1) * 512]

        def xn_cb2(c, xf, waits, _ts=ts):
            dst_b = xn2s[:, c * 512:(c + 1) * 512]
            p.copy("vector", dst_b, xf)
            p.dma("gpsimd", xn2_d[c * 128:(c + 1) * 128, _ts * 512:(_ts + 1) * 512],
                  dst_b, waits=[("v", p.n("v"))])

        emit_ln_strip(p, B2, x_of_chunk, ts, xn_cb2, x_ready=x2_v)
        xn_v = p.at("v")
        for w_bf, b_s, out_ap in ((None, bq2_s, q2_l), (wk2_bf, bk2_s, k2_l)):
            for c in range(NCH):
                if w_bf is None:
                    p.dma("scalar", stage[:, 1024:2048].rearrange("p (k e) -> p k e", e=128),
                          wq2T_in.rearrange("(k p) e -> p k e", p=128)[:, :, c * 128:(c + 1) * 128],
                          waits=[("v", p.n("v"))])
                    p.copy("vector", wtile[:, :], stage[:, 1024:2048], waits=p.at("dsc"))
                    lhs_of = lambda dk: wtile[:, dk * 128:(dk + 1) * 128]
                else:
                    lhs_of = lambda dk, _w=w_bf, _c=c: _w[:, dk * 1024 + _c * 128: dk * 1024 + (_c + 1) * 128]
                for dk in range(NCH):
                    p.mm(ps_w[:, :], lhs_of(dk),
                         xn2s[:, dk * 512:(dk + 1) * 512], dk == 0, dk == 7,
                         waits=(list(xn_v) + [("v", p.n("v")), ("dgp", p.n("dgp"))]) if dk == 0 else (),
                         inc="t" if dk == 7 else None)
                p.ts("vector", outbf[:, c * 512:(c + 1) * 512], ps_w[:, :], b_s[:, c:c + 1], None,
                     ALU.add, waits=p.at("t"))
                obv = p.at("v")
                hA, hB = 2 * c, 2 * c + 1
                p.dma("gpsimd", out_ap[0:64, hA * TC + ts * 512: hA * TC + (ts + 1) * 512],
                      outbf[0:64, c * 512:(c + 1) * 512], waits=obv)
                p.dma("gpsimd", out_ap[0:64, hB * TC + ts * 512: hB * TC + (ts + 1) * 512],
                      outbf[64:128, c * 512:(c + 1) * 512])
        for tb in range(4):
            tcol = ts * 512 + tb * 128
            for half, psv in ((0, ps_w), (1, ps_w2)):
                for dk in range(NCH):
                    p.mm(psv[:, :], xn2s[:, dk * 512 + tb * 128: dk * 512 + tb * 128 + 128],
                         wv2_bf[:, dk * 1024 + half * 512: dk * 1024 + half * 512 + 512],
                         dk == 0, False,
                         waits=(list(xn_v) + [("v", p.n("v")), ("dgp", p.n("dgp"))]) if dk == 0 else ())
                p.mm(psv[:, :], ones_r[:1, :], bv2_bf[:1, half * 512:(half + 1) * 512],
                     False, True, inc="t")
            vmw = p.at("t")
            for h in range(NH):
                psv = ps_w if h < 8 else ps_w2
                p.copy("vector", vstage[:, h * 65: h * 65 + 64],
                       psv[:, (h % 8) * 64:(h % 8) * 64 + 64], waits=vmw + [("dgp", p.n("dgp"))])
            p.dma("gpsimd", v2_l[tcol:tcol + 128, :], vstage[:, :], waits=[("v", p.n("v"))])
    p.cc("AllGather", ALU.bypass, [q2_l], [q2_g], waits=p.at("dgp"))
    p.cc("AllGather", ALU.bypass, [k2_l], [k2_g], waits=p.at("dgp"))
    p.cc("AllGather", ALU.bypass, [v2_l], [v2_g], waits=p.at("dgp"))
    cc_k2 = p.n("cc")

    p.barrier()
    if DEBUG:
        p.dma("sync", dbg["d_x2"], x2_d)
        p.nop("sync", waits=p.at("dsy"))
    ar = Arena()
    q2s = ar.bf16(8192)
    kbs = ar.bf16(12288)
    vbs = ar.bf16(6240)
    gks = ar.bf16(4096)
    gvs = ar.bf16(2080)
    gqs = ar.bf16(4096)
    rowbf = ar.bf16(4096)
    ctxbf2 = ar.bf16(4096)
    bmask_s = ar.bf16(3072)
    probs2 = ar.bf16(1024)
    fsml2 = ar.f32(2048)
    bc1 = fsml2[:, 0:512]; bc2 = fsml2[:, 512:1024]
    t1f = fsml2[:, 1024:1536]
    invs2 = fsml2[:1, 1536:2048]

    ps_sc = [ps512[0], ps512[1]]
    ps_l = ps512[2]
    ps_c = ps512[3]
    ps_bcE = ps512[4]
    ps_rE = ps512[5]

    p.dma("sync", bmask_s, bmask_in, waits=[("cc", cc_k2)])
    for b2 in range(2):
        for ui, g in enumerate(gi):
            pos = g * 64
            r, lc = owner_of(b2, pos)
            p.dma("sync", gks[0:64, b2 * NH * 128:(b2 + 1) * NH * 128]
                      .rearrange("q (h u t) -> q h u t", u=2, t=64)[:, :, ui, :],
                  k2_g.rearrange("(r q) ht -> r q ht", q=64)[r]
                      .rearrange("q (h t) -> q h t", t=TC)[:, :, lc:lc + 64])
            p.dma("sync", gvs[ui * 64:(ui + 1) * 64, b2 * NH * 65:(b2 + 1) * NH * 65],
                  v2_g[r * TC + lc: r * TC + lc + 64, :])
            p.dma("sync", gqs[0:64, b2 * NH * 128:(b2 + 1) * NH * 128]
                      .rearrange("q (h u t) -> q h u t", u=2, t=64)[:, :, ui, :],
                  q2_g.rearrange("(r q) ht -> r q ht", q=64)[r]
                      .rearrange("q (h t) -> q h t", t=TC)[:, :, lc:lc + 64])
    gld0 = p.at("dsy")

    for b2 in range(2):
        for h in range(NH):
            score_t = {}
            first = True
            for grp in range(8):
                r = grp if b2 == 0 else 7 - grp
                colr = (0, 1024) if b2 == 0 else (1024, 2048)
                slot = grp % 2
                p.dma("sync", kbs[0:64, slot * 1024:(slot + 1) * 1024],
                      k2_g.rearrange("(r q) ht -> r q ht", q=64)[r]
                          .rearrange("q (h t) -> q h t", t=TC)[:, h, colr[0]:colr[1]],
                      waits=[("t", p.n("t")), ("s", p.n("s"))] + (list(gld0) if first else []))
                p.dma("sync", vbs[:, slot * 8 * 65:(slot + 1) * 8 * 65].rearrange("p (kb c) -> p kb c", c=65),
                      v2_g[r * TC + colr[0]: r * TC + colr[1], h * 65:(h + 1) * 65]
                          .rearrange("(kb p) c -> p kb c", p=128))
                first = False
                gld = p.at("dsy")
                qh = gqs[0:64, b2 * NH * 128 + h * 128:(b2 * NH + h + 1) * 128]
                for kb in range(8):
                    sl2 = kb % 2
                    p.mm(ps_sc[sl2][:, 0:128], kbs[0:64, slot * 1024 + kb * 128: slot * 1024 + (kb + 1) * 128],
                         qh, True, True, waits=list(gld) + [("s", p.n("s"))], inc="t")
                    p.act(probs2[:, sl2 * 512:sl2 * 512 + 128], ps_sc[sl2][:, 0:128], AF.Exp,
                          bias=float(EXPB), scale=float(SC8), waits=[("t", p.n("t"))])
                    p.mm(ps_rE[0:65, 0:128], vbs[:, slot * 8 * 65 + kb * 65: slot * 8 * 65 + kb * 65 + 65],
                         probs2[:, sl2 * 512:sl2 * 512 + 128], grp == 0 and kb == 0,
                         grp == 7 and kb == 7,
                         waits=[("s", p.n("s"))] + ([("v", p.n("v"))] if (grp == 0 and kb == 0) else []),
                         inc="t" if (grp == 7 and kb == 7) else None)
            p.recip(invs2[:1, 0:128], ps_rE[64:65, 0:128], waits=p.at("t"))
            p.mm(ps_bcE[:, 0:128], ones_rf[:, :], invs2[:1, 0:128], True, True,
                 waits=p.at("v"), inc="t")
            p.copy("vector", bc1[:, 0:128], ps_bcE[:, 0:128], waits=p.at("t"))
            hb = (h % 2) * 64
            p.tt("vector", rowbf[hb:hb + 64, b2 * 2048 + (h // 2) * 128: b2 * 2048 + (h // 2) * 128 + 128],
                 ps_rE[0:64, 0:128], bc1[hb:hb + 64, 0:128], ALU.mult)
    rows_v = p.at("v")

    for b2 in range(2):
        for qs in range(16):
            pos0 = qs * 512
            rq, lq = owner_of(b2, pos0)
            slq = (lq % HC) // 512 + (0 if b2 == 0 else 2)
            own = rq
            sl_own = lq // 512 if b2 == 0 else 2 + (lq - HC) // 512
            p.dma("scalar", q2s[0:64, :].rearrange("q (h t) -> q h t", t=512),
                  q2_g.rearrange("(r q) ht -> r q ht", q=64)[rq]
                      .rearrange("q (h t) -> q h t", t=TC)[:, :, lq:lq + 512],
                  waits=[("t", p.n("t"))])
            for cc2 in range(6):
                ks = (pos0 - 128 + cc2 * 128) % TB
                rk, lk = owner_of(b2, ks)
                p.dma("sync", kbs[0:64, :].rearrange("q (h c) -> q h c", c=768)[:, :, cc2 * 128:(cc2 + 1) * 128],
                      k2_g.rearrange("(r q) ht -> r q ht", q=64)[rk]
                          .rearrange("q (h t) -> q h t", t=TC)[:, :, lk:lk + 128],
                      waits=[("t", p.n("t")), ("s", p.n("s"))] + (list(rows_v) if (qs == 0 and b2 == 0 and cc2 == 0) else []))
                p.dma("sync", vbs[:, cc2 * NH * 65:(cc2 + 1) * NH * 65],
                      v2_g[rk * TC + lk: rk * TC + lk + 128, :])
            sld = p.at("dsy", "dsc")
            for h in range(NH):
                qh = q2s[0:64, h * 512:(h + 1) * 512]
                score_t = {}
                hd_v = p.at("v")
                for cc2 in range(2):
                    p.mm(ps_sc[cc2][:, :], kbs[0:64, h * 768 + cc2 * 128: h * 768 + (cc2 + 1) * 128],
                         qh, True, True, waits=list(sld) + [("s", p.n("s")), ("v", p.n("v"))], inc="t")
                    score_t[cc2] = p.n("t")
                for cc2 in range(6):
                    sl2 = cc2 % 2
                    p.tt("vector", ps_sc[sl2][:, :], ps_sc[sl2][:, :],
                         bmask_s[:, cc2 * 512:(cc2 + 1) * 512], ALU.add, waits=[("t", score_t[cc2])])
                    p.act(probs2[:, sl2 * 512:(sl2 + 1) * 512], ps_sc[sl2][:, :], AF.Exp,
                          bias=float(EXPB), scale=float(SC8), waits=[("v", p.n("v"))])
                    ecnt = p.n("s")
                    p.mm(ps_l[0:65, :], vbs[:, cc2 * NH * 65 + h * 65: cc2 * NH * 65 + h * 65 + 65],
                         probs2[:, sl2 * 512:(sl2 + 1) * 512], cc2 == 0, cc2 == 5,
                         waits=[("s", ecnt)] + (hd_v if cc2 == 0 else []),
                         inc="t" if cc2 == 5 else None)
                    if cc2 + 2 < 6:
                        p.mm(ps_sc[sl2][:, :], kbs[0:64, h * 768 + (cc2 + 2) * 128: h * 768 + (cc2 + 3) * 128],
                             qh, True, True, waits=[("s", ecnt)], inc="t")
                        score_t[cc2 + 2] = p.n("t")
                p.mm(ps_c[:, :], gks[0:64, b2 * NH * 128 + h * 128: b2 * NH * 128 + (h + 1) * 128],
                     qh, True, True, waits=[("s", p.n("s")), ("v", p.n("v"))], inc="t")
                p.act(probs2[:, 0:512], ps_c[:, :], AF.Exp, bias=float(EXPB), scale=float(SC8),
                      waits=p.at("t"))
                p.mm(ps_c[0:65, :], gvs[:, b2 * NH * 65 + h * 65: b2 * NH * 65 + h * 65 + 65],
                     probs2[:, 0:512], True, True, waits=[("s", p.n("s")), ("v", p.n("v"))], inc="t")
                pvq = p.at("t")
                p.recip(invs2, ps_l[64:65, :], waits=pvq)
                p.mm(ps_bcE[:, :], ones_rf[:, :], invs2, True, True, waits=p.at("v"), inc="t")
                p.copy("vector", bc1, ps_bcE[:, :], waits=p.at("t"))
                p.recip(invs2, ps_c[64:65, :])
                p.mm(ps_bcE[:, :], ones_rf[:, :], invs2, True, True, waits=p.at("v"), inc="t")
                p.copy("vector", bc2, ps_bcE[:, :], waits=p.at("t"))
                hb = (h % 2) * 64
                p.tt("vector", t1f[hb:hb + 64, :], ps_l[0:64, :], bc1[hb:hb + 64, :], ALU.mult)
                dst = ctxbf2[hb:hb + 64, (h // 2) * 512:(h // 2) * 512 + 512]
                p.tt("vector", dst, ps_c[0:64, :], bc2[hb:hb + 64, :], ALU.mult,
                     waits=[("dgp", p.n("dgp"))])
                p.tt("vector", dst, dst, t1f[hb:hb + 64, :], ALU.add)
            for ui, g in enumerate(gi):
                gpos = g * 64
                if pos0 <= gpos < pos0 + 512:
                    loc = gpos - pos0
                    for c in range(NCH):
                        p.copy("vector",
                               ctxbf2[:, c * 512 + loc: c * 512 + loc + 64],
                               rowbf[:, b2 * 2048 + c * 128 + ui * 64: b2 * 2048 + c * 128 + (ui + 1) * 64],
                               waits=rows_v)
            p.dma("gpsimd",
                  ctx2_l.rearrange("(o c p) t -> p o c t", o=8, p=128)[:, own, :, sl_own * 512:(sl_own + 1) * 512],
                  ctxbf2.rearrange("p (c t) -> p c t", t=512), waits=p.at("v"))

    p.cc("ReduceScatter", ALU.add, [ctx2_l], [ctx2_r], waits=p.at("dgp"))
    cc_ctx = p.n("cc")

    p.barrier()
    ar = Arena()
    AR1 = ar.bf16(32768)
    AR2 = ar.bf16(32768)
    ARF = ar.f32(8192)
    wt = ar.bf16(2048)
    misc = ar.f32(656)
    stage = ar.f32(2048)

    s1_s = misc[:, 0:32]; s2_s = misc[:, 32:64]; sh1_s = misc[:, 64:96]; sh2_s = misc[:, 96:128]
    s3_s = misc[:, 128:136]; bo2_s = misc[:, 136:144]
    scrF = misc[:, 144:656]

    ps1, ps2, ps3, ps4 = ps512[0], ps512[1], ps512[2], ps512[3]

    p.dma("sync", stage[:, 0:32], s1_in)
    p.dma("sync", stage[:, 32:64], s2_in)
    p.dma("sync", stage[:, 64:96], sh1_in)
    p.dma("sync", stage[:, 96:128], sh2_in)
    p.dma("sync", stage[:, 128:136], s3_in)
    p.dma("sync", stage[:, 136:144], bo2_in)
    p.copy("vector", misc[:, 0:144], stage[:, 0:144], waits=p.at("dsy"))
    sldF = p.at("v")

    wo2_bf = AR1[:, 0:8192]
    src3 = wo2T_in.rearrange("(c p) e -> p c e", p=128)
    for c0 in range(0, NCH, 2):
        p.dma("scalar", stage[:, 0:2 * D].rearrange("p (c e) -> p c e", e=D),
              src3[:, c0:c0 + 2, :], waits=[("v", p.n("v"))])
        p.copy("vector", wo2_bf[:, c0 * D:(c0 + 2) * D], stage[:, 0:2 * D], waits=p.at("dsc"))
    wo2_v = p.at("v")
    p.nop("sync", waits=[("cc", cc_ctx)])

    ctxs = AR2[:, 0:4096]
    ctxf = AR2[:, 8192:12288]
    x2sF = ARF[:, 0:4096]
    for ts in range(NST):
        p.dma("sync", ctxs.rearrange("p (c t) -> p c t", t=512),
              ctx2_r[:, ts * 512:(ts + 1) * 512].rearrange("(c p) t -> p c t", p=128),
              waits=[("t", p.n("t")), ("v", p.n("v"))])
        p.dma("gpsimd", x2sF.rearrange("p (c t) -> p c t", t=512),
              x2_d[:, ts * 512:(ts + 1) * 512].rearrange("(c p) t -> p c t", p=128),
              waits=[("v", p.n("v")), ("dgp", p.n("dgp"))])
        cld = p.at("dsy", "dgp")
        for c in range(NCH):
            p.ts("vector", ctxf[:, c * 512:(c + 1) * 512], ctxs[:, c * 512:(c + 1) * 512],
                 0.125, None, ALU.mult, waits=cld)
        ctx_v = p.at("v")
        if DEBUG:
            p.dma("gpsimd", dbg["d_ctx2"].rearrange("(c p) t -> p c t", p=128)[:, :, ts * 512:(ts + 1) * 512],
                  ctxf.rearrange("p (c t) -> p c t", t=512), waits=ctx_v)
        for oc in range(NCH):
            for dk in range(NCH):
                p.mm(ps2[:, :], wo2_bf[:, dk * D + oc * 128: dk * D + (oc + 1) * 128],
                     ctxf[:, dk * 512:(dk + 1) * 512], dk == 0, dk == NCH - 1,
                     waits=(list(wo2_v) + list(ctx_v) + [("v", p.n("v"))]) if dk == 0 else (),
                     inc="t" if dk == NCH - 1 else None)
            xsl = x2sF[:, oc * 512:(oc + 1) * 512]
            p.ts("vector", scrF, ps2[:, :], bo2_s[:, oc:oc + 1], None, ALU.add, waits=p.at("t"))
            p.tt("vector", xsl, xsl, scrF, ALU.add)
        p.dma("gpsimd", x3a_d.rearrange("(c p) t -> p c t", p=128)[:, :, ts * 512:(ts + 1) * 512],
              x2sF.rearrange("p (c t) -> p c t", t=512), waits=p.at("v"))
    f1_done = p.at("t", "v", "s", "dgp")

    bufA = AR1[:, 0:32768]
    bufB = AR2[:, 0:32768]
    xn2sF = ARF[:, 0:4096].bitcast(BF16)

    def stream_mm(ps_list, w_in, Kdim, oc, xsrc, extra_waits):
        nkc = Kdim // 128
        piece = 8
        for p0 in range(0, nkc, piece):
            n = min(piece, nkc - p0)
            slot = (p0 // piece) % 2
            p.dma("scalar", stage[:, slot * 1024: slot * 1024 + n * 128]
                      .rearrange("p (c e) -> p c e", e=128),
                  w_in.rearrange("(c p) e -> p c e", p=128)[:, p0:p0 + n, oc * 128:(oc + 1) * 128],
                  waits=[("v", p.n("v"))])
            p.copy("vector", wt[:, slot * 1024: slot * 1024 + n * 128],
                   stage[:, slot * 1024: slot * 1024 + n * 128],
                   waits=p.at("dsc") + [("t", p.n("t"))])
            wv = p.at("v")
            for dk in range(p0, p0 + n):
                for th in range(len(ps_list)):
                    p.mm(ps_list[th][:, :], wt[:, slot * 1024 + (dk - p0) * 128:
                                                 slot * 1024 + (dk - p0) * 128 + 128],
                         xsrc(dk, th), dk == 0, dk == nkc - 1,
                         waits=(list(wv) + list(extra_waits) + [("s", p.n("s")), ("v", p.n("v"))])
                         if (dk == p0 and th == 0) else (),
                         inc="t" if (dk == nkc - 1 and th == len(ps_list) - 1) else None)

    for fs in range(2):
        p.dma("sync", xn2sF.rearrange("p (c t) -> p c t", t=1024),
              xn2_d[:, fs * 1024:(fs + 1) * 1024].rearrange("(c p) t -> p c t", p=128),
              waits=[("t", p.n("t")), ("v", p.n("v"))] + (f1_done if fs == 0 else []))
        xldF = p.at("dsy")
        xsrc_x = lambda dk, th: xn2sF[:, dk * 1024 + th * 512: dk * 1024 + th * 512 + 512]
        for oc in range(32):
            stream_mm([ps1, ps2], tw1_in, 1024, oc, xsrc_x, xldF)
            for th, psx in ((0, ps1), (1, ps2)):
                p.act(bufA[:, oc * 1024 + th * 512: oc * 1024 + th * 512 + 512], psx[:, :],
                      AF.Silu, scale=s1_s[:, oc:oc + 1], waits=p.at("t") + (list(sldF) if oc == 0 else []))
        for oc in range(32):
            stream_mm([ps3, ps4], tw2_in, 1024, oc, xsrc_x, xldF)
            for th, psx in ((0, ps3), (1, ps4)):
                p.ts("vector", bufB[:, oc * 1024 + th * 512: oc * 1024 + th * 512 + 512],
                     psx[:, :], s2_s[:, oc:oc + 1], None, ALU.mult, waits=p.at("t"))
        hv = p.at("v", "s")
        for oc in range(32):
            p.tt("vector", bufB[:, oc * 1024:(oc + 1) * 1024], bufA[:, oc * 1024:(oc + 1) * 1024],
                 bufB[:, oc * 1024:(oc + 1) * 1024], ALU.mult, waits=hv)
        h_v = p.at("v")
        xsrc_B = lambda dk, th: bufB[:, dk * 1024 + th * 512: dk * 1024 + th * 512 + 512]
        for oc in range(32):
            stream_mm([ps1, ps2], th1_in, 4096, oc, xsrc_B, h_v)
            for th, psx in ((0, ps1), (1, ps2)):
                p.act(bufA[:, oc * 1024 + th * 512: oc * 1024 + th * 512 + 512], psx[:, :],
                      AF.Gelu, scale=sh1_s[:, oc:oc + 1], waits=p.at("t"))
        h1_v = p.at("s")
        xsrc_A = lambda dk, th: bufA[:, dk * 1024 + th * 512: dk * 1024 + th * 512 + 512]
        for oc in range(32):
            stream_mm([ps3, ps4], th2_in, 4096, oc, xsrc_A, h1_v)
            for th, psx in ((0, ps3), (1, ps4)):
                p.act(bufB[:, oc * 1024 + th * 512: oc * 1024 + th * 512 + 512], psx[:, :],
                      AF.Gelu, scale=sh2_s[:, oc:oc + 1], waits=p.at("t"))
        h2_v = p.at("s")
        for sub in range(2):
            ts = fs * 2 + sub
            x3sv = ARF[:, 4096:8192]
            p.dma("sync", x3sv.rearrange("p (c t) -> p c t", t=512),
                  x3a_d.rearrange("(c p) t -> p c t", p=128)[:, :, ts * 512:(ts + 1) * 512],
                  waits=[("t", p.n("t")), ("v", p.n("v")), ("s", p.n("s")), ("dgp", p.n("dgp"))])
            x3ld = p.at("dsy")
            for oc in range(NCH):
                stream_mm([ps1], tw3_in, 4096, oc,
                          lambda dk, th, _s=sub: bufB[:, dk * 1024 + _s * 512: dk * 1024 + _s * 512 + 512],
                          h2_v)
                p.ts("vector", scrF, ps1[:, :], s3_s[:, oc:oc + 1], None, ALU.mult,
                     waits=p.at("t"))
                xsl = x3sv[:, oc * 512:(oc + 1) * 512]
                p.tt("vector", xsl, xsl, scrF, ALU.add, waits=x3ld)
            p.dma("gpsimd", x3_d.rearrange("(c p) t -> p c t", p=128)[:, :, ts * 512:(ts + 1) * 512],
                  x3sv.rearrange("p (c t) -> p c t", t=512), waits=p.at("v"))

    p.barrier()
    if DEBUG:
        p.dma("sync", dbg["d_x3"], x3_d)
        p.nop("sync", waits=p.at("dsy"))
    ar = Arena()
    ln_sq, ln_stat, ln_xf = carve_ln(ar)
    B = LNBufs(lnc, lnr, ln_sq, ln_stat, ln_xf, (ps_r1a, ps_r1b, ps512[4], ps512[5]))
    x_sbG = ar.f32(16384)
    xn_sbG = ar.bf16(4096)
    wqG_bf = ar.bf16(8192)
    wkG_bf = ar.bf16(8192)
    wvG_bf = ar.bf16(8192)
    k3e = ar.bf16(1040)
    v3e = ar.bf16(1040)
    kvacc = ar.f32(2080)
    qscG = ar.bf16(512)
    e1 = ar.f32(512)
    e2 = ar.f32(512)
    biasesG = ar.f32(8)
    brow = ar.bf16(2048, rows=1)
    stage = ar.f32(4096)

    ps_q = ps512[0]
    ps_v0 = ps512[1]
    ps_v1 = ps512[2]
    ps_kv = ps512[3]

    p.memset("vector", kvacc[:, :], 0.0)
    p.dma("sync", x_sbG.rearrange("p (c t) -> p c t", c=NCH),
          x3_d.rearrange("(c p) t -> p c t", p=128))
    xldG = p.at("dsy")
    for w_in, w_bf in ((wq3T_in, wqG_bf), (wk3T_in, wkG_bf), (wv3T_in, wvG_bf)):
        src3_ = w_in.rearrange("(c p) e -> p c e", p=128)
        for c0 in range(0, NCH, 4):
            p.dma("scalar", stage[:, 0:4 * D].rearrange("p (c e) -> p c e", e=D),
                  src3_[:, c0:c0 + 4, :], waits=[("v", p.n("v"))])
            p.copy("vector", w_bf[:, c0 * D:(c0 + 4) * D], stage[:, 0:4 * D], waits=p.at("dsc"))
    p.dma("scalar", stage[:, 0:8], bq3_in, waits=[("v", p.n("v"))])
    p.dma("scalar", stage[:1, 8:1032], bk3_in)
    p.dma("scalar", stage[:1, 1032:2056], bv3_in)
    p.copy("vector", biasesG[:, :], stage[:, 0:8], waits=p.at("dsc"))
    p.copy("vector", brow[:1, 0:1024], stage[:1, 8:1032])
    p.copy("vector", brow[:1, 1024:2048], stage[:1, 1032:2056])
    wldG = p.at("v")

    for ts in range(NST):
        bsel = 0 if ts < 2 else 1
        x_of_chunk = lambda c, _ts=ts: x_sbG[:, c * TC + _ts * 512: c * TC + (_ts + 1) * 512]

        def xn_cbG(c, xf, waits, _ts=ts):
            dst_b = xn_sbG[:, c * 512:(c + 1) * 512]
            p.copy("vector", dst_b, xf, waits=[("t", p.n("t")), ("dgp", p.n("dgp"))])
            p.dma("gpsimd", xn3_d[c * 128:(c + 1) * 128, _ts * 512:(_ts + 1) * 512],
                  dst_b, waits=[("v", p.n("v"))])

        emit_ln_strip(p, B, x_of_chunk, ts, xn_cbG, x_ready=xldG)
        xn_v = p.at("v")
        for c in range(NCH):
            for dk in range(NCH):
                p.mm(ps_q[:, :], wqG_bf[:, dk * D + c * 128: dk * D + (c + 1) * 128],
                     xn_sbG[:, dk * 512:(dk + 1) * 512], dk == 0, dk == NCH - 1,
                     waits=(list(wldG) + list(xn_v) + [("v", p.n("v")), ("dgp", p.n("dgp"))]) if dk == 0 else (),
                     inc="t" if dk == NCH - 1 else None)
            p.ts("vector", e1[:, :], ps_q[:, :], biasesG[:, c:c + 1], 0.0, ALU.add, ALU.min,
                 waits=p.at("t"))
            p.act(e1[:, :], e1[:, :], AF.Exp, waits=p.at("v"))
            p.ts("vector", e2[:, :], ps_q[:, :], biasesG[:, c:c + 1], 0.0, ALU.add, ALU.max)
            p.tt("vector", qscG[:, :], e1[:, :], e2[:, :], ALU.add, waits=p.at("s"))
            obv = p.at("v")
            hA, hB = 2 * c, 2 * c + 1
            p.dma("gpsimd", q3_d[0:64, hA * TC + ts * 512: hA * TC + (ts + 1) * 512],
                  qscG[0:64, :], waits=obv)
            p.dma("gpsimd", q3_d[0:64, hB * TC + ts * 512: hB * TC + (ts + 1) * 512],
                  qscG[64:128, :])
        for tb in range(4):
            for src_w, boff, dst_ext, feat in ((wkG_bf, 0, k3e, True), (wvG_bf, 1024, v3e, False)):
                for half in range(2):
                    psx = [ps_v0, ps_v1][half]
                    for dk in range(NCH):
                        p.mm(psx[:, :], xn_sbG[:, dk * 512 + tb * 128:
                                               dk * 512 + tb * 128 + 128],
                             src_w[:, dk * D + half * 512: dk * D + half * 512 + 512],
                             dk == 0, False,
                             waits=(list(xn_v) + [("v", p.n("v")), ("t", p.n("t"))]) if dk == 0 else ())
                    p.mm(psx[:, :], ones_r[:1, :], brow[:1, boff + half * 512: boff + (half + 1) * 512],
                         False, True, inc="t")
                vmw = p.at("t")
                for h in range(NH):
                    psx = [ps_v0, ps_v1][h // 8]
                    seg = psx[:, (h % 8) * 64:(h % 8) * 64 + 64]
                    if feat:
                        p.ts("vector", e1[:, 0:64], seg, 0.0, None, ALU.min, waits=vmw)
                        p.act(e1[:, 0:64], e1[:, 0:64], AF.Exp, waits=p.at("v"))
                        p.ts("vector", e2[:, 0:64], seg, 0.0, None, ALU.max)
                        p.tt("vector", dst_ext[:, h * 65: h * 65 + 64], e1[:, 0:64], e2[:, 0:64],
                             ALU.add, waits=p.at("s"))
                    else:
                        p.copy("vector", dst_ext[:, h * 65: h * 65 + 64], seg, waits=vmw)
            for h in range(NH):
                p.memset("vector", k3e[:, h * 65 + 64: h * 65 + 65], 1.0)
                p.memset("vector", v3e[:, h * 65 + 64: h * 65 + 65], 1.0)
            kv_v = p.at("v")
            for h in range(NH):
                p.mm(ps_kv[0:65, 0:65], k3e[:, h * 65: h * 65 + 65], v3e[:, h * 65: h * 65 + 65],
                     True, True, waits=kv_v + [("v", p.n("v"))], inc="t")
                acc_sl = kvacc[0:65, (bsel * NH + h) * 65:(bsel * NH + h + 1) * 65]
                p.tt("vector", acc_sl, acc_sl, ps_kv[0:65, 0:65], ALU.add, waits=p.at("t"))
    p.dma("gpsimd", kvp_l, kvacc[0:65, :], waits=p.at("v"))
    p.cc("AllReduce", ALU.add, [kvp_l], [kvp_g], waits=p.at("dgp"))
    cc_kvp = p.n("cc")

    p.barrier()
    ar = Arena()
    kvb = ar.bf16(2080)
    wo3_bf = ar.bf16(8192)
    w2t = ar.bf16(4096)
    ctxbf3 = ar.bf16(4096)
    hstr = ar.bf16(16384)
    w1G_bf = ar.bf16(32768)
    x4s = ar.f32(4096)
    abc = ar.f32(1024)
    bcz = ar.f32(512)
    miscH = ar.f32(64)
    scrF = ar.f32(512)
    stage = ar.f32(4096)
    arow = ar.f32(2048, rows=1)
    zrow = ar.f32(512, rows=1)
    q3s = hstr[:, 0:NH * 512]
    bo3_s = miscH[:, 0:8]
    fb1_s = ar.f32(32)
    fb2_s = ar.f32(8)

    ps1, ps2, ps3, ps4 = ps512[0], ps512[1], ps512[2], ps512[3]

    p.dma("sync", stage[0:65, 0:2080], kvp_g, waits=[("cc", cc_kvp)])
    p.copy("vector", kvb[0:65, :], stage[0:65, 0:2080], waits=p.at("dsy"))
    kv_v = p.at("v")
    for w_in, w_bf, gran in ((wo3T_in, wo3_bf, 4), (fw1_in, w1G_bf, 1)):
        src3_ = w_in.rearrange("(c p) e -> p c e", p=128)
        C, E = src3_.shape[1], src3_.shape[2]
        cpg = max(1, 4096 // E)
        for c0 in range(0, C, cpg):
            n = min(cpg, C - c0)
            p.dma("scalar", stage[:, 0:n * E].rearrange("p (c e) -> p c e", e=E),
                  src3_[:, c0:c0 + n, :], waits=[("v", p.n("v"))])
            p.copy("vector", w_bf[:, c0 * E:(c0 + n) * E], stage[:, 0:n * E], waits=p.at("dsc"))
    p.dma("scalar", stage[:, 0:8], bo3_in, waits=[("v", p.n("v"))])
    p.dma("scalar", stage[:, 8:40], fb1_in)
    p.dma("scalar", stage[:, 40:48], fb2_in)
    p.copy("vector", bo3_s, stage[:, 0:8], waits=p.at("dsc"))
    p.copy("vector", fb1_s, stage[:, 8:40])
    p.copy("vector", fb2_s, stage[:, 40:48])
    wldH = p.at("v")

    for ah in range(2):
        p.memset("vector", arow[:1, :], 0.0)
        for dk in range(NCH):
            p.act(w2t[:, 0:2048], w1G_bf[:, dk * 4096 + ah * 2048: dk * 4096 + (ah + 1) * 2048], AF.Abs,
                  waits=wldH + [("v", p.n("v")), ("t", p.n("t"))])
            aw = p.at("s")
            for seg in range(4):
                p.mm(ps1[0:1, :], ones_cb[:, :], w2t[:, seg * 512:(seg + 1) * 512], True, True,
                     waits=list(aw) + [("v", p.n("v"))], inc="t")
                p.tt("vector", arow[:1, seg * 512:(seg + 1) * 512], arow[:1, seg * 512:(seg + 1) * 512],
                     ps1[0:1, :], ALU.add, waits=p.at("t"))
        p.ts("vector", arow[:1, :], arow[:1, :], 1.0 / D, None, ALU.mult)
        for seg in range(4):
            p.mm(ps2[:, :], ones_rf[:, :], arow[:1, seg * 512:(seg + 1) * 512], True, True,
                 waits=p.at("v"), inc="t")
            p.copy("vector", abc[:, 0:512], ps2[:, :], waits=p.at("t"))
            for dk in range(NCH):
                csl = w1G_bf[:, dk * 4096 + ah * 2048 + seg * 512: dk * 4096 + ah * 2048 + (seg + 1) * 512]
                p.act(w2t[:, 2048:2560], csl, AF.Sign, waits=[("v", p.n("v")), ("t", p.n("t"))])
                p.tt("vector", csl, w2t[:, 2048:2560], abc[:, 0:512], ALU.mult, waits=p.at("s"))
    w1bin = p.at("v")

    w2src0 = fw2_in.rearrange("(c p) e -> p c e", p=128)
    p.memset("vector", arow[:1, 0:1024], 0.0)
    for hc in range(32):
        p.dma("scalar", stage[:, 0:1024], w2src0[:, hc, :], waits=[("s", p.n("s")), ("v", p.n("v"))])
        p.act(stage[:, 1024:2048], stage[:, 0:1024], AF.Abs, waits=p.at("dsc"))
        aw = p.at("s")
        for seg in range(2):
            p.mm(ps1[0:1, :], ones_cf[:, :], stage[:, 1024 + seg * 512:1024 + (seg + 1) * 512],
                 True, True, waits=list(aw) + [("v", p.n("v"))], inc="t")
            p.tt("vector", arow[:1, seg * 512:(seg + 1) * 512], arow[:1, seg * 512:(seg + 1) * 512],
                 ps1[0:1, :], ALU.add, waits=p.at("t"))
    p.ts("vector", arow[:1, 0:1024], arow[:1, 0:1024], 1.0 / 4096, None, ALU.mult)
    for seg in range(2):
        p.mm(ps2[:, :], ones_rf[:, :], arow[:1, seg * 512:(seg + 1) * 512], True, True,
             waits=p.at("v"), inc="t")
        p.copy("vector", abc[:, seg * 512:(seg + 1) * 512], ps2[:, :], waits=p.at("t"))
    a2v = p.at("v")
    for hc in range(32):
        p.dma("scalar", stage[:, 0:1024], w2src0[:, hc, :], waits=[("s", p.n("s")), ("v", p.n("v"))])
        p.act(w2t[:, 0:1024], stage[:, 0:1024], AF.Sign, waits=p.at("dsc"))
        p.tt("vector", w2t[:, 1024:2048], w2t[:, 0:1024], abc[:, 0:1024], ALU.mult,
             waits=list(a2v) + [("s", p.n("s")), ("dgp", p.n("dgp"))])
        p.dma("gpsimd", bw2_scr[hc * 128:(hc + 1) * 128, :], w2t[:, 1024:2048],
              waits=[("v", p.n("v"))])
    w2bin = p.at("dgp", "v")

    for ts in range(NST):
        bsel = 0 if ts < 2 else 1
        p.dma("sync", q3s[0:64, :].rearrange("q (h t) -> q h t", t=512),
              q3_d.rearrange("q (h t) -> q h t", t=TC)[:, :, ts * 512:(ts + 1) * 512],
              waits=[("t", p.n("t"))])
        p.dma("gpsimd", x4s[:, :].rearrange("p (c t) -> p c t", t=512),
              x3_d[:, ts * 512:(ts + 1) * 512].rearrange("(c p) t -> p c t", p=128),
              waits=[("v", p.n("v"))])
        qld = p.at("dsy", "dgp")
        for h in range(NH):
            p.mm(ps1[0:65, :], kvb[0:64, (bsel * NH + h) * 65:(bsel * NH + h + 1) * 65],
                 q3s[0:64, h * 512:(h + 1) * 512], True, True,
                 waits=list(qld) + list(kv_v) + [("v", p.n("v")), ("s", p.n("s"))], inc="t")
            p.ts("vector", zrow, ps1[64:65, :], 1e-6, None, ALU.add, waits=p.at("t"))
            p.recip(zrow, zrow)
            p.mm(ps2[:, :], ones_rf[:, :], zrow, True, True, waits=p.at("v"), inc="t")
            p.copy("vector", bcz, ps2[:, :], waits=p.at("t"))
            p.tt("vector", ctxbf3[(h % 2) * 64:(h % 2) * 64 + 64, (h // 2) * 512:(h // 2) * 512 + 512],
                 ps1[0:64, :], bcz[0:64, :], ALU.mult)
        ctx_v = p.at("v")
        for oc in range(NCH):
            for dk in range(NCH):
                p.mm(ps3[:, :], wo3_bf[:, dk * D + oc * 128: dk * D + (oc + 1) * 128],
                     ctxbf3[:, dk * 512:(dk + 1) * 512], dk == 0, dk == NCH - 1,
                     waits=(list(wldH) + list(ctx_v) + [("v", p.n("v"))]) if dk == 0 else (),
                     inc="t" if dk == NCH - 1 else None)
            xsl = x4s[:, oc * 512:(oc + 1) * 512]
            p.ts("vector", scrF, ps3[:, :], bo3_s[:, oc:oc + 1], None, ALU.add,
                 waits=p.at("t"))
            p.tt("vector", xsl, xsl, scrF, ALU.add)
        p.dma("scalar", ctxbf3[:, :].rearrange("p (c t) -> p c t", t=512),
              xn3_d[:, ts * 512:(ts + 1) * 512].rearrange("(c p) t -> p c t", p=128),
              waits=[("t", p.n("t")), ("v", p.n("v"))])
        xnld = p.at("dsc")
        for oc in range(32):
            psx = [ps1, ps2][oc % 2]
            for dk in range(NCH):
                p.mm(psx[:, :], w1G_bf[:, dk * 4096 + oc * 128: dk * 4096 + (oc + 1) * 128],
                     ctxbf3[:, dk * 512:(dk + 1) * 512], dk == 0, dk == NCH - 1,
                     waits=(list(xnld) + list(w1bin) + [("s", p.n("s")), ("v", p.n("v"))]) if dk == 0 else (),
                     inc="t" if dk == NCH - 1 else None)
            p.act(hstr[:, oc * 512:(oc + 1) * 512], psx[:, :], AF.Gelu,
                  bias=fb1_s[:, oc:oc + 1], waits=p.at("t"))
        h_s = p.at("s")
        w2src = bw2_scr.rearrange("(c p) e -> p c e", p=128)
        for oc in range(NCH):
            psx = [ps3, ps4][oc % 2]
            for p0 in range(0, 32, 16):
                slot = (p0 // 16) % 2
                p.dma("scalar", w2t[:, slot * 2048: slot * 2048 + 2048]
                          .rearrange("p (c e) -> p c e", e=128),
                      w2src[:, p0:p0 + 16, oc * 128:(oc + 1) * 128],
                      waits=[("t", p.n("t"))] + (list(w2bin) if (ts == 0 and oc == 0) else []))
                wv2 = p.at("dsc")
                for dk in range(p0, p0 + 16):
                    p.mm(psx[:, :], w2t[:, slot * 2048 + (dk - p0) * 128:
                                        slot * 2048 + (dk - p0) * 128 + 128],
                         hstr[:, dk * 512:(dk + 1) * 512], dk == 0, dk == 31,
                         waits=(list(h_s) + list(wv2) + [("v", p.n("v"))]) if dk == p0 else (),
                         inc="t" if dk == 31 else None)
            xsl = x4s[:, oc * 512:(oc + 1) * 512]
            p.ts("vector", scrF, psx[:, :], fb2_s[:, oc:oc + 1], None, ALU.add,
                 waits=p.at("t"))
            p.tt("vector", xsl, xsl, scrF, ALU.add)
        p.dma("gpsimd", out_out.rearrange("(c p) t -> p c t", p=128)[:, :, ts * 512:(ts + 1) * 512],
              x4s[:, :].rearrange("p (c t) -> p c t", t=512), waits=p.at("v"))
    p.nop("gpsimd", waits=p.at("dgp"))
    p.nop("sync", waits=p.at("dgp", "dsy", "dsc"))
    return p


def host_fused(inputs):
    gi = [int(v) for v in np.asarray(inputs["L2_attn_global_indices"]).reshape(-1)]
    x = np.asarray(inputs["L0_attn_x"])
    xsh_fm = [fmT(s).astype(np.float32) for s in shard_rows(x)]

    col = lambda v, n: np.ascontiguousarray(np.asarray(v).reshape(n, 128).T).astype(np.float32)
    wdq = np.ascontiguousarray(inputs["L0_attn_W_dq"]).astype(np.float32)
    wuq = np.asarray(inputs["L0_attn_W_uq"]).astype(np.float32)
    wuqr = np.ascontiguousarray(np.concatenate(
        [wuq[64 * h:64 * (h + 1), :] for h in range(NH)], axis=1)).astype(np.float32)
    perm = rope_perm()
    wq1T = fmT(inputs["L1_attn_wq"]).astype(np.float32)[:, perm]
    wk1T = fmT(inputs["L1_attn_wk"]).astype(np.float32)[:, perm]

    W1 = np.asarray(inputs["L0_ffn_W1"]).astype(np.float32)
    W2 = np.asarray(inputs["L0_ffn_W2"]).astype(np.float32)
    b1 = np.asarray(inputs["L0_ffn_b1"]).astype(np.float32)
    b2 = np.asarray(inputs["L0_ffn_b2"]).astype(np.float32)
    b1c = np.concatenate([b1[e].reshape(32, 128).T for e in range(8)], axis=1).astype(np.float32)
    b2c = np.concatenate([b2[e].reshape(8, 128).T for e in range(8)], axis=1).astype(np.float32)

    common = {
        "id128": np.eye(128, dtype=np.float32),
        "sel8": np.ascontiguousarray(np.kron(np.eye(8, dtype=np.float32), np.ones((1, 128), np.float32))),
        "dmaskf": make_dmask(),
        "dmaskb": make_dmask().astype(BF16NP),
        "bmask": make_bmask(),
        "wdkvT": fmT(inputs["L0_attn_W_dkv"]).astype(np.float32),
        "wg": np.ascontiguousarray(inputs["L0_ffn_Wg"]).astype(np.float32),
        "bg": np.asarray(inputs["L0_ffn_bg"]).reshape(1, 8).astype(np.float32),
        "wdq": wdq, "wdqT": fmT(wdq).astype(np.float32),
        "wuqT": fmT(wuq).astype(np.float32), "wuqr": wuqr,
        "wuk": np.ascontiguousarray(inputs["L0_attn_W_uk"]).astype(np.float32),
        "wuv": np.ascontiguousarray(inputs["L0_attn_W_uv"]).astype(np.float32),
        "woT": fmT(inputs["L0_attn_W_o"]).astype(np.float32),
        "w1e": np.ascontiguousarray(W1.reshape(8 * D, 4096)),
        "w2e": np.ascontiguousarray(W2.reshape(8 * 4096, D)),
        "b1c": b1c, "b2c": b2c,
        "wq1Tp": np.ascontiguousarray(wq1T), "wk1Tp": np.ascontiguousarray(wk1T),
        "wv1T": fmT(inputs["L1_attn_wv"]).astype(np.float32),
        "wo1T": fmT(inputs["L1_attn_wo"]).astype(np.float32),
        "cw1": np.ascontiguousarray(inputs["L1_ffn_cn_w1"]).astype(np.float32),
        "cw2": np.ascontiguousarray(inputs["L1_ffn_cn_w2"]).astype(np.float32),
        "cw3": np.ascontiguousarray(inputs["L1_ffn_cn_w3"]).astype(np.float32),
        "cw4": np.ascontiguousarray(inputs["L1_ffn_cn_w4"]).astype(np.float32),
        "cb1": col(inputs["L1_ffn_cn_b1"], 8), "cb2": col(inputs["L1_ffn_cn_b2"], 4),
        "cb3": col(inputs["L1_ffn_cn_b3"], 2),
        "cb4": np.asarray(inputs["L1_ffn_cn_b4"]).reshape(1, 1).astype(np.float32),
        "gw1": np.ascontiguousarray(inputs["L1_ffn_gc_w1"]).astype(np.float32),
        "gw2": np.ascontiguousarray(inputs["L1_ffn_gc_w2"]).astype(np.float32),
        "gw3": np.ascontiguousarray(inputs["L1_ffn_gc_w3"]).astype(np.float32),
        "gb1": col(inputs["L1_ffn_gc_b1"], 4), "gb2": col(inputs["L1_ffn_gc_b2"], 2),
        "gb3": np.asarray(inputs["L1_ffn_gc_b3"]).reshape(1, 1).astype(np.float32),
        "ar": np.asarray(inputs["L1_ffn_adaptive_range"]).reshape(1, 1).astype(np.float32),
        "wq2T": fmT(inputs["L2_attn_Wq"]).astype(np.float32),
        "wk2T": fmT(inputs["L2_attn_Wk"]).astype(np.float32),
        "wv2T": fmT(inputs["L2_attn_Wv"]).astype(np.float32),
        "bq2": col(inputs["L2_attn_bq"], 8), "bk2": col(inputs["L2_attn_bk"], 8),
        "bv2": np.asarray(inputs["L2_attn_bv"]).reshape(1, D).astype(np.float32),
        "wo2T": fmT(inputs["L2_attn_Wo"]).astype(np.float32),
        "bo2": col(inputs["L2_attn_bo"], 8),
        "tw1": fmT(inputs["L2_ffn_w1_w"]).astype(np.float32),
        "tw2": fmT(inputs["L2_ffn_w2_w"]).astype(np.float32),
        "th1": fmT(inputs["L2_ffn_h1_w"]).astype(np.float32),
        "th2": fmT(inputs["L2_ffn_h2_w"]).astype(np.float32),
        "tw3": fmT(inputs["L2_ffn_w3_w"]).astype(np.float32),
        "s1": col(inputs["L2_ffn_w1_s"], 32), "s2": col(inputs["L2_ffn_w2_s"], 32),
        "sh1": col(inputs["L2_ffn_h1_s"], 32), "sh2": col(inputs["L2_ffn_h2_s"], 32),
        "s3": col(inputs["L2_ffn_w3_s"], 8),
        "wq3T": fmT(inputs["L3_attn_Wq"]).astype(np.float32),
        "wk3T": fmT(inputs["L3_attn_Wk"]).astype(np.float32),
        "wv3T": fmT(inputs["L3_attn_Wv"]).astype(np.float32),
        "bq3": col(inputs["L3_attn_bq"], 8),
        "bk3": np.asarray(inputs["L3_attn_bk"]).reshape(1, D).astype(np.float32),
        "bv3": np.asarray(inputs["L3_attn_bv"]).reshape(1, D).astype(np.float32),
        "wo3T": fmT(inputs["L3_attn_Wo"]).astype(np.float32),
        "bo3": col(inputs["L3_attn_bo"], 8),
        "fw1": fmT(inputs["L3_ffn_w1"]).astype(np.float32),
        "fb1": col(inputs["L3_ffn_b1"], 32),
        "fw2": fmT(inputs["L3_ffn_w2"]).astype(np.float32),
        "fb2": col(inputs["L3_ffn_b2"], 8),
    }
    maps = []
    for core in range(NCORES):
        m = dict(common)
        m["x"] = xsh_fm[core]
        m["mb"] = make_mb(core)
        sin_t, cos_t = rope_tables(core)
        m["sint"] = sin_t
        m["cost"] = cos_t
        maps.append(m)
    return maps, gi


_BUILT = {}


def kernel(**inputs):
    maps, gi = host_fused(inputs)
    key = ("fused", tuple(gi))
    if key not in _BUILT:
        _BUILT[key] = build_fused(gi).finalize()
    res = run_bass_kernel_spmd(_BUILT[key], maps, core_ids=list(range(NCORES)))
    out_sh = [np.ascontiguousarray(r["out"].T) for r in res.results]
    return unshard_rows(out_sh)
